# version 01
# baseline (speedup 1.0000x reference)
import contextlib
import numpy as np
import ml_dtypes
import concourse.bass as bass
import concourse.mybir as mybir
from concourse.bass_utils import run_bass_kernel_spmd

F32 = mybir.dt.float32
F16 = mybir.dt.float16
BF16 = mybir.dt.bfloat16
ALU = mybir.AluOpType
AF = mybir.ActivationFunctionType
AX = mybir.AxisListType.X

B, T, C = 16, 2048, 1024
NCORES = 8
BPC = B // NCORES
TOK = BPC * T
NT = 128
TILES = TOK // NT
TPB = T // NT
HEADS, HD = 16, 64
CI = C // 128
IN4 = 4096
EPS = 1e-5
NEG = -1.0e30
INV_C = 1.0 / C


def _mask_np():
    q = np.arange(128)[:, None]
    j = np.arange(192)[None, :]
    allowed = (j >= 64 * (q >= 64)) & (j <= q + 64)
    return np.where(allowed, np.float32(0.0), np.float32(NEG)).astype(np.float32)


class EngWrap:

    def __init__(self, eng):
        self.e = eng
        self.w = set()
        self.r = set()

    @staticmethod
    def _nm(a):
        try:
            return a.tensor.name
        except AttributeError:
            return None

    def _pre(self, outs, ins):
        on = {self._nm(a) for a in outs} - {None}
        inn = {self._nm(a) for a in ins} - {None}
        if ((on | inn) & self.w) or (on & self.r):
            self.e.drain()
            self.w.clear()
            self.r.clear()
        self.w |= on
        self.r |= inn

    def wait_ge(self, sem, val):
        return self.e.wait_ge(sem, val)

    def drain(self):
        ins = self.e.drain()
        self.w.clear()
        self.r.clear()
        return ins

    def tensor_scalar(self, out, in0, s1, s2, **kw):
        self._pre([out], [in0, s1, s2])
        return self.e.tensor_scalar(out, in0, s1, s2, **kw)

    def scalar_tensor_tensor(self, out, in0, s, in1, **kw):
        self._pre([out], [in0, s, in1])
        return self.e.scalar_tensor_tensor(out, in0, s, in1, **kw)

    def tensor_tensor(self, out, in0, in1, **kw):
        self._pre([out], [in0, in1])
        return self.e.tensor_tensor(out, in0, in1, **kw)

    def tensor_add(self, out, a, b):
        self._pre([out], [a, b])
        return self.e.tensor_add(out, a, b)

    def tensor_sub(self, out, a, b):
        self._pre([out], [a, b])
        return self.e.tensor_sub(out, a, b)

    def tensor_mul(self, out, a, b):
        self._pre([out], [a, b])
        return self.e.tensor_mul(out, a, b)

    def tensor_max(self, out, a, b):
        self._pre([out], [a, b])
        return self.e.tensor_max(out, a, b)

    def tensor_copy(self, out, in_):
        self._pre([out], [in_])
        return self.e.tensor_copy(out, in_)

    def memset(self, ap, c):
        self._pre([ap], [])
        return self.e.memset(ap, c)

    def tensor_reduce(self, out, in_, **kw):
        self._pre([out], [in_])
        return self.e.tensor_reduce(out, in_, **kw)

    def reduce_sum(self, out, in_, **kw):
        self._pre([out], [in_])
        return self.e.reduce_sum(out, in_, **kw)

    def reduce_max(self, out, in_, **kw):
        self._pre([out], [in_])
        return self.e.reduce_max(out, in_, **kw)

    def reciprocal(self, out, in_):
        self._pre([out], [in_])
        return self.e.reciprocal(out, in_)

    def activation(self, out, in_, func, bias=0.0, scale=1.0, accum_out=None):
        outs = [out] + ([accum_out] if accum_out is not None else [])
        self._pre(outs, [in_, bias, scale])
        kw = dict(func=func, bias=bias, scale=scale)
        if accum_out is not None:
            kw["accum_out"] = accum_out
        return self.e.activation(out=out, in_=in_, **kw)



class Mono:
    def __init__(self, sem, step=16):
        self.sem = sem
        self.step = step
        self.n = 0

    def inc(self, instr):
        instr.then_inc(self.sem, self.step)
        self.n += self.step
        return self.n


def build_nc(debug=0):
    nc = bass.Bass("TRN2", target_bir_lowering=False)

    def din(name, shape, dt=F32):
        return nc.dram_tensor(name, shape, dt, kind="ExternalInput")

    x_d = din("x", [TOK, C]).ap()
    wqkv_d = din("wqkv", [C, 3 * C]).ap()
    bqkv_d = din("bqkv", [1, 3 * C]).ap()
    wout_d = din("wout", [C, C]).ap()
    bout_d = din("bout", [1, C]).ap()
    ln1g_h = din("ln1g", [1, C])
    ln1b_h = din("ln1b", [1, C])
    ln2g_d = din("ln2g", [1, C]).ap()
    ln2b_d = din("ln2b", [1, C]).ap()
    w1_d = din("w1", [C, IN4]).ap()
    b1_h = din("b1", [1, IN4])
    w2_d = din("w2", [IN4, C]).ap()
    b2_d = din("b2", [1, C]).ap()
    wg_d = din("wg", [IN4, C]).ap()
    wu_d = din("wu", [IN4, C]).ap()
    wd_d = din("wd", [C, IN4]).ap()
    c_d = din("cvec", [BPC, C]).ap()
    wmod_d = din("wmod", [C, 6 * C]).ap()

    out_d = nc.dram_tensor("out", [TOK, C], F32, kind="ExternalOutput").ap()

    def dscr(name, shape, dt=F32, tap=False):
        if debug and tap:
            return nc.dram_tensor(name, shape, dt, kind="ExternalOutput").ap()
        return nc.dram_tensor(name, shape, dt).ap()

    a_s = dscr("a_s", [TOK, C], tap=True)
    p_s = dscr("p_s", [TOK, C], tap=True)
    x1_s = dscr("x1_s", [TOK, C])
    r_s = dscr("r_s", [TOK, IN4], BF16)
    modv_s = dscr("modv_s", [BPC, 6 * C])
    qb_h = nc.dram_tensor("qb_s", [1, 3 * C], F32)
    qb_s = qb_h.ap()
    osc_s = dscr("osc_s", [TOK, 1])
    sw_s = dscr("sw_s", [1, 8])

    dbg_d = {}
    if debug:
        for i in range(debug):
            dbg_d[i] = nc.dram_tensor(f"dbg{i}", [128, 4 * C], F32,
                                      kind="ExternalOutput").ap()

    ident32_c = nc.inline_tensor(np.eye(128, dtype=np.float32), name="ident32").ap()
    identbf_c = nc.inline_tensor(np.eye(128, dtype=ml_dtypes.bfloat16), name="identbf").ap()
    mask_c = nc.inline_tensor(_mask_np(), name="maskc").ap()

    es = contextlib.ExitStack()

    def sbuf(name, shape, dt=F32):
        return es.enter_context(nc.sbuf_tensor(name, shape, dt))

    psA = es.enter_context(nc.psum_tensor("psA", [128, 1536], F32))
    psT = es.enter_context(nc.psum_tensor("psT", [128, 1024], F32))
    psS = es.enter_context(nc.psum_tensor("psS", [128, 512], F32))
    psO = es.enter_context(nc.psum_tensor("psO", [128, 1024], F32))

    id32 = sbuf("id32", [128, 128])
    idbf = sbuf("idbf", [128, 128], BF16)
    maskb = sbuf("maskb", [128, 192])
    bc0 = sbuf("bc0", [128, C])
    bc1 = sbuf("bc1", [128, C])
    bc2 = sbuf("bc2", [128, C])
    bc3 = sbuf("bc3", [128, C])
    s1 = sbuf("s1", [128, 1])
    s2 = sbuf("s2", [128, 1])
    mvec = sbuf("mvec", [128, 1])
    e2v = sbuf("e2v", [128, 1])
    varv = sbuf("varv", [128, 1])
    sdv = sbuf("sdv", [128, 1])
    rstd = sbuf("rstd", [128, 1])
    mrv = sbuf("mrv", [128, 1])
    rstd1 = sbuf("rstd1", [128, 1])
    mrv1 = sbuf("mrv1", [128, 1])
    mx16 = sbuf("mx16", [128, 16])
    ng16 = sbuf("ng16", [128, 16])
    rs16 = sbuf("rs16", [128, 16])
    rr16 = sbuf("rr16", [128, 16])
    sc_0 = sbuf("sc_0", [128, 1])
    sc_1 = sbuf("sc_1", [128, 1])
    sc_2 = sbuf("sc_2", [128, 1])
    sc_3 = sbuf("sc_3", [128, 1])
    sc_4 = sbuf("sc_4", [128, 1])
    swb = sbuf("swb", [128, 8])
    epsb = sbuf("epsb", [128, 1])
    xb = sbuf("xb", [128, C])
    hb = sbuf("hb", [128, C])
    tmp1 = sbuf("tmp1", [128, C])
    fmbuf = sbuf("fmbuf", [128, C])
    qkvk = sbuf("qkvk", [128, 3 * C])
    okm = sbuf("okm", [128, C])
    ab = sbuf("ab", [128, C])

    sem_pe = es.enter_context(nc.semaphore("sem_pe"))
    sem_v = es.enter_context(nc.semaphore("sem_v"))
    sem_sc = es.enter_context(nc.semaphore("sem_sc"))
    sem_gp = es.enter_context(nc.semaphore("sem_gp"))
    ld = Mono(es.enter_context(nc.semaphore("ld_sem")))
    lda = Mono(es.enter_context(nc.semaphore("lda_sem")))
    ldb = Mono(es.enter_context(nc.semaphore("ldb_sem")))
    st = Mono(es.enter_context(nc.semaphore("st_sem")))
    hal = Mono(es.enter_context(nc.semaphore("hal_sem")))

    cnt = {"pe": 0, "v": 0, "sc": 0, "gp": 0}
    sems = {"pe": sem_pe, "v": sem_v, "sc": sem_sc, "gp": sem_gp}

    def mk(instr, which, n=1):
        instr.then_inc(sems[which], n)
        cnt[which] += n
        return cnt[which]

    dbg_used = [0]

    def dbg_dump(sy, src_ap, rows, cols):
        i = dbg_used[0]
        dbg_used[0] += 1
        if i in dbg_d:
            sy.dma_start(out=dbg_d[i][0:rows, 0:cols], in_=src_ap).then_inc(st.sem, 16)
            st.n += 16

    with contextlib.ExitStack() as p0:
        sb0 = lambda n, s, d=F32: p0.enter_context(nc.sbuf_tensor(n, s, d))
        stage = sb0("stage0", [128, 3 * C])
        cb2 = sb0("cb2", [BPC, C])
        scfm = sb0("scfm", [128, 2 * CI])
        modk = sb0("modk", [BPC, 6 * C])
        vr = sb0("vr", [BPC, 6 * C])
        vt1 = sb0("vt1", [BPC, C])
        vt2 = sb0("vt2", [BPC, C])

        with nc.Block() as blk:
            @blk.sync
            def _(sy):
                for buf, src in ((id32, ident32_c), (idbf, identbf_c),
                                 (maskb, mask_c), (cb2, c_d)):
                    ld.inc(sy.dma_start(out=buf[:], in_=src))
                sy.wait_ge(ld.sem, ld.n)

            @blk.vector
            def _(v):
                v = EngWrap(v)
                v.memset(epsb[:], EPS)
                v.memset(swb[:], 0.0)

        with nc.Block() as blk:
            @blk.tensor
            def _(te):
                ins = None
                for kc in range(CI):
                    ins = te.transpose(psT[:, 2 * kc:2 * kc + 2],
                                       cb2[:, 128 * kc:128 * (kc + 1)],
                                       id32[0:BPC, 0:BPC])
                mk(ins, "pe")

            @blk.scalar
            def _(sc):
                sc = EngWrap(sc)
                sc.wait_ge(sem_pe, cnt["pe"])
                sc.activation(out=scfm[:], in_=psT[:, 0:2 * CI], func=AF.Silu)

        for qt in range(4):
            pe0 = cnt["pe"]
            par = (lda, ldb)
            par0 = [lda.n, ldb.n]
            with nc.Block() as blk:
                @blk.sync
                def _(sy, qt=qt, pe0=pe0):
                    for kc in range(CI):
                        half = kc % 2
                        if kc >= 2:
                            sy.wait_ge(sem_pe, pe0 + kc - 1)
                        par[half].inc(sy.dma_start(
                            out=stage[:, 1536 * half:1536 * (half + 1)],
                            in_=wmod_d[128 * kc:128 * (kc + 1),
                                       1536 * qt:1536 * (qt + 1)]))

                @blk.tensor
                def _(te, qt=qt, par0=par0):
                    base = {0: par0[0], 1: par0[1]}
                    for kc in range(CI):
                        half = kc % 2
                        base[half] += 16
                        te.wait_ge(par[half].sem, base[half])
                        ins = None
                        for nn in range(3):
                            ins = te.matmul(
                                psA[0:BPC, 512 * nn:512 * (nn + 1)],
                                scfm[:, 2 * kc:2 * kc + 2],
                                stage[:, 1536 * half + 512 * nn:1536 * half + 512 * (nn + 1)],
                                start=(kc == 0), stop=(kc == CI - 1))
                        mk(ins, "pe")

                @blk.vector
                def _(v, qt=qt):
                    v = EngWrap(v)
                    v.wait_ge(sem_pe, cnt["pe"])
                    v.tensor_copy(modk[:, 1536 * qt:1536 * (qt + 1)], psA[0:BPC, :])

        with nc.Block() as blk:
            @blk.vector
            def _(v):
                v = EngWrap(v)
                P2 = slice(0, BPC)
                sa = modk[P2, 0:C]
                ca = modk[P2, C:2 * C]
                ga = modk[P2, 2 * C:3 * C]
                sbv = modk[P2, 3 * C:4 * C]
                cbv = modk[P2, 4 * C:5 * C]
                gbv = modk[P2, 5 * C:6 * C]
                v.tensor_scalar(vr[P2, 0:C], ca, 1.0, None, op0=ALU.add)
                v.tensor_copy(vr[P2, C:2 * C], sa)
                v.tensor_scalar(vr[P2, 3 * C:4 * C], cbv, 1.0, None, op0=ALU.add)
                v.tensor_copy(vr[P2, 4 * C:5 * C], sbv)
                for gsrc, gdst in ((ga, vr[P2, 2 * C:3 * C]),
                                   (gbv, vr[P2, 5 * C:6 * C])):
                    v.tensor_scalar(vt1[P2, :], gsrc, 0.0, None, op0=ALU.is_ge)
                    v.tensor_scalar(vt1[P2, :], vt1[P2, :], 2.0, 1.0,
                                    op0=ALU.mult, op1=ALU.subtract)
                    v.tensor_scalar(vt2[P2, :], gsrc, -1.0, None, op0=ALU.mult)
                    v.tensor_max(vt2[P2, :], vt2[P2, :], gsrc)
                    v.tensor_scalar(vt2[P2, :], vt2[P2, :], 0.25, None, op0=ALU.max)
                    ins = v.tensor_mul(gdst, vt1[P2, :], vt2[P2, :])
                mk(ins, "v")

            @blk.sync
            def _(sy):
                sy.wait_ge(sem_v, cnt["v"])
                st.inc(sy.dma_start(out=modv_s[:, :], in_=vr[:]))
                st.inc(sy.dma_start(out=sw_s, in_=swb[0:1, 0:8]))
                sy.wait_ge(st.sem, st.n)

    with contextlib.ExitStack() as pa:
        sba = lambda n, s, d=F32: pa.enter_context(nc.sbuf_tensor(n, s, d))
        wqkv = sba("wqkv_sb", [128, CI * 3 * C])
        wo = sba("wo_sb", [128, CI * C])
        qfm = sba("qfm", [128, C])
        kfm = sba("kfm", [128, C])
        khalo = sba("khalo", [128, 512])
        vhalo = sba("vhalo", [64, C])
        smb = sba("smb", [128, 192])
        pbuf = sba("pbuf", [128, 192])
        ptb = sba("ptb", [128, 256])
        bvb = sba("bvb", [128, C])
        qkb_f = sba("qkb_f", [128, 16])
        pa_setup = contextlib.ExitStack()
        g1f = pa_setup.enter_context(nc.sbuf_tensor("g1f", [128, CI], F32))
        b1hf = pa_setup.enter_context(nc.sbuf_tensor("b1hf", [128, CI], F32))
        rwa = qkvk[0:1, :]

        with nc.Block() as blk:
            @blk.sync
            def _(sy):
                for kc in range(CI):
                    ld.inc(sy.dma_start(out=wqkv[:, 3072 * kc:3072 * (kc + 1)],
                                        in_=wqkv_d[128 * kc:128 * (kc + 1), :]))
                for kc in range(CI):
                    ld.inc(sy.dma_start(out=wo[:, 1024 * kc:1024 * (kc + 1)],
                                        in_=wout_d[128 * kc:128 * (kc + 1), :]))
                ld.inc(sy.dma_start(out=g1f[:],
                                    in_=bass.AP(ln1g_h, 0, [[1, 128], [128, CI]]),
                                    allow_slow_non_contiguous=True))
                ld.inc(sy.dma_start(out=b1hf[:],
                                    in_=bass.AP(ln1b_h, 0, [[1, 128], [128, CI]]),
                                    allow_slow_non_contiguous=True))
                sy.wait_ge(ld.sem, ld.n)

        pe0 = cnt["pe"]
        v0 = cnt["v"]
        W0 = {}
        with nc.Block() as blk:
            @blk.tensor
            def _(te):
                for half in range(2):
                    if half == 1:
                        te.wait_ge(sem_v, v0 + 1)
                    ins = None
                    for kc in range(CI):
                        for nn in range(3):
                            co = 3072 * kc + 1536 * half + 512 * nn
                            ins = te.matmul(psA[0:1, 512 * nn:512 * (nn + 1)],
                                            b1hf[:, kc:kc + 1],
                                            wqkv[:, co:co + 512],
                                            start=(kc == 0), stop=(kc == CI - 1))
                    mk(ins, "pe")

            @blk.sync
            def _(sy):
                for pp in range(3):
                    ld.inc(sy.dma_start(out=fmbuf[0:1, 0:1024],
                                        in_=bqkv_d[0:1, 1024 * pp:1024 * (pp + 1)]))
                    W0[f"bq{pp}"] = ld.n
                    sy.wait_ge(sem_v, v0 + 2 + pp + 1)

            @blk.vector
            def _(v):
                v = EngWrap(v)
                v.wait_ge(sem_pe, pe0 + 1)
                mk(v.tensor_copy(rwa[:, 0:1536], psA[0:1, :]), "v")
                v.wait_ge(sem_pe, pe0 + 2)
                mk(v.tensor_copy(rwa[:, 1536:3072], psA[0:1, :]), "v")
                for pp in range(3):
                    v.wait_ge(ld.sem, W0[f"bq{pp}"])
                    mk(v.tensor_add(rwa[:, 1024 * pp:1024 * (pp + 1)],
                                    rwa[:, 1024 * pp:1024 * (pp + 1)],
                                    fmbuf[0:1, 0:1024]), "v")

            @blk.sync
            def _(sy):
                sy.wait_ge(sem_v, cnt["v"])
                st.inc(sy.dma_start(out=qb_s, in_=rwa[:]))
                sy.wait_ge(st.sem, st.n)

        with nc.Block() as blk:
            @blk.vector
            def _(v):
                v = EngWrap(v)
                for kc in range(CI):
                    v.tensor_scalar(wqkv[:, 3072 * kc:3072 * (kc + 1)],
                                    wqkv[:, 3072 * kc:3072 * (kc + 1)],
                                    g1f[:, kc:kc + 1], None, op0=ALU.mult)

            @blk.sync
            def _(sy):
                ld.inc(sy.dma_start(out=bvb[:],
                                    in_=qb_s[0:1, 2048:3072].broadcast_to([128, C])))
                ld.inc(sy.dma_start(out=qkb_f[:],
                                    in_=bass.AP(qb_h, 0, [[1, 128], [128, 16]]),
                                    allow_slow_non_contiguous=True))
                ld.inc(sy.dma_start(out=bc3[:], in_=bout_d.broadcast_to([128, C])))
                for i, buf in enumerate((bc0, bc1, bc2)):
                    ld.inc(sy.dma_start(
                        out=buf[:],
                        in_=modv_s[0:1, C * i:C * (i + 1)].broadcast_to([128, C])))
                sy.wait_ge(ld.sem, ld.n)

        pa_setup.close()

        for t in range(TILES):
            seq_start = (t % TPB) == 0
            if t == TPB:
                with nc.Block() as blk:
                    @blk.sync
                    def _(sy):
                        for i, buf in enumerate((bc0, bc1, bc2)):
                            ld.inc(sy.dma_start(
                                out=buf[:],
                                in_=modv_s[1:2, C * i:C * (i + 1)].broadcast_to([128, C])))
                        sy.wait_ge(ld.sem, ld.n)

            with nc.Block() as blk:
                W = {}

                v_names = (["var1", "h", "var2", "h1", "tea", "teb", "qkve1",
                            "qkve2", "qktea", "qkteb", "qktec", "qkted", "ms0"]
                           + [f"{nm}{h}" for h in range(HEADS)
                              for nm in ("mask", "pte", "oe")]
                           + ["otea", "oteb", "a_done", "p_done", "khal"])
                v_base = cnt["v"]
                v_pos = {nm: v_base + i + 1 for i, nm in enumerate(v_names)}

                pe_names = (["h1Ta", "h1Tb", "qkv1", "qkv2", "qkTa", "qkTb",
                             "qkTc", "qkTd"]
                            + [f"{nm}{h}" for h in range(HEADS)
                               for nm in ("sc", "pt", "pv")]
                            + ["oTa", "oTb", "oproj"])
                pe_base = cnt["pe"]
                pe_pos = {nm: pe_base + i + 1 for i, nm in enumerate(pe_names)}

                @blk.sync
                def _(sy, t=t):
                    ld.inc(sy.dma_start(out=xb[:], in_=x_d[NT * t:NT * (t + 1), :]))
                    W["ld_x"] = ld.n

                @blk.scalar
                def _(sc, t=t, v_pos=v_pos, pe_pos=pe_pos):
                    sc = EngWrap(sc)
                    sc.wait_ge(ld.sem, W["ld_x"])
                    W["sq1"] = mk(sc.activation(out=fmbuf[:], in_=xb[:], func=AF.Square,
                                                accum_out=s2[:]), "sc")
                    sc.wait_ge(sem_v, v_pos["var1"])
                    W["sd1"] = mk(sc.activation(out=sdv[:], in_=varv[:], func=AF.Sqrt,
                                                bias=epsb[:]), "sc")
                    sc.wait_ge(sem_v, v_pos["h"])
                    W["sq2"] = mk(sc.activation(out=fmbuf[:], in_=hb[:], func=AF.Square,
                                                accum_out=s2[:]), "sc")
                    sc.wait_ge(sem_v, v_pos["var2"])
                    W["sd2"] = mk(sc.activation(out=sdv[:], in_=varv[:], func=AF.Sqrt,
                                                bias=epsb[:]), "sc")
                    W["exp"] = []
                    for h in range(HEADS):
                        sc.wait_ge(sem_v, v_pos[f"mask{h}"])
                        if h >= 1:
                            sc.wait_ge(sem_pe, pe_pos[f"pt{h - 1}"])
                        W["exp"].append(
                            mk(sc.activation(out=pbuf[:, 0:192],
                                             in_=smb[:, 0:192],
                                             func=AF.Exp, scale=0.125,
                                             bias=ng16[:, h:h + 1],
                                             accum_out=rs16[:, h:h + 1]), "sc"))

                @blk.vector
                def _(v, t=t, seq_start=seq_start, v_pos=v_pos, pe_pos=pe_pos):
                    v = EngWrap(v)
                    v.wait_ge(ld.sem, W["ld_x"])
                    v.reduce_sum(s1[:], xb[:], axis=AX)
                    v.wait_ge(sem_sc, W["sq1"])
                    v.tensor_scalar(e2v[:], s2[:], INV_C, None, op0=ALU.mult)
                    v.tensor_scalar(mvec[:], s1[:], INV_C, None, op0=ALU.mult)
                    v.tensor_mul(varv[:], mvec[:], mvec[:])
                    mk(v.tensor_sub(varv[:], e2v[:], varv[:]), "v")
                    v.wait_ge(sem_sc, W["sd1"])
                    v.reciprocal(rstd[:], sdv[:])
                    v.tensor_mul(mrv[:], mvec[:], rstd[:])
                    v.tensor_scalar(tmp1[:], xb[:], rstd[:], mrv[:],
                                    op0=ALU.mult, op1=ALU.subtract)
                    v.tensor_mul(hb[:], tmp1[:], bc0[:])
                    mk(v.tensor_add(hb[:], hb[:], bc1[:]), "v")
                    v.reduce_sum(s1[:], hb[:], axis=AX)
                    v.wait_ge(sem_sc, W["sq2"])
                    v.tensor_scalar(e2v[:], s2[:], INV_C, None, op0=ALU.mult)
                    v.tensor_scalar(mvec[:], s1[:], INV_C, None, op0=ALU.mult)
                    v.tensor_mul(varv[:], mvec[:], mvec[:])
                    mk(v.tensor_sub(varv[:], e2v[:], varv[:]), "v")
                    v.wait_ge(sem_sc, W["sd2"])
                    v.reciprocal(rstd1[:], sdv[:])
                    v.tensor_mul(mrv1[:], mvec[:], rstd1[:])
                    mk(v.tensor_scalar(tmp1[:], hb[:], rstd1[:], mrv1[:],
                                       op0=ALU.mult, op1=ALU.subtract), "v")
                    v.wait_ge(sem_pe, pe_pos["h1Ta"])
                    mk(v.tensor_copy(fmbuf[:, 0:512], psT[:, 0:512]), "v")
                    v.wait_ge(sem_pe, pe_pos["h1Tb"])
                    mk(v.tensor_copy(fmbuf[:, 512:1024], psT[:, 512:1024]), "v")
                    v.wait_ge(sem_pe, pe_pos["qkv1"])
                    mk(v.tensor_copy(qkvk[:, 0:1536], psA[:]), "v")
                    v.wait_ge(sem_pe, pe_pos["qkv2"])
                    v.tensor_copy(qkvk[:, 1536:2048], psA[:, 0:512])
                    mk(v.scalar_tensor_tensor(qkvk[:, 2048:3072], psA[:, 512:1536],
                                              0.0, bvb[:], op0=ALU.add,
                                              op1=ALU.add), "v")
                    for g in range(4):
                        v.wait_ge(sem_pe, pe_pos["qkT" + "abcd"[g]])
                        dst = qfm if g < 2 else kfm
                        off = 512 * (g % 2)
                        ins = None
                        for i in range(4):
                            cc = 4 * (g % 2) + i
                            bcol = cc if g < 2 else 8 + cc
                            ins = v.tensor_scalar(dst[:, off + 128 * i:off + 128 * (i + 1)],
                                                  psT[:, off + 128 * i:off + 128 * (i + 1)],
                                                  qkb_f[:, bcol:bcol + 1], None,
                                                  op0=ALU.add)
                        mk(ins, "v")
                    if seq_start:
                        mk(v.memset(psS[:, 0:64], 0.0), "v")
                    else:
                        mk(v.memset(s1[0:1, 0:1], 0.0), "v")
                    for h in range(HEADS):
                        v.wait_ge(sem_pe, pe_pos[f"sc{h}"])
                        v.tensor_add(smb[:], psS[:, 0:192], maskb[:])
                        v.reduce_max(mx16[:, h:h + 1], smb[:], axis=AX)
                        mk(v.tensor_scalar(ng16[:, h:h + 1], mx16[:, h:h + 1],
                                           -0.125, None, op0=ALU.mult), "v")
                        v.wait_ge(sem_pe, pe_pos[f"pt{h}"])
                        bnk = 512 * (h % 2)
                        v.tensor_copy(ptb[0:64, 128:256],
                                      psT[0:64, bnk + 128:bnk + 256])
                        mk(v.tensor_copy(ptb[:, 0:128],
                                         psT[:, bnk:bnk + 128]), "v")
                        v.wait_ge(sem_sc, W["exp"][h])
                        v.reciprocal(rr16[:, h:h + 1], rs16[:, h:h + 1])
                        v.wait_ge(sem_pe, pe_pos[f"pv{h}"])
                        ob = 512 * (h % 2) + 64 * (h % 4)
                        mk(v.tensor_scalar(okm[:, 64 * h:64 * (h + 1)],
                                           psO[:, ob:ob + 64],
                                           rr16[:, h:h + 1], None, op0=ALU.mult), "v")
                    v.wait_ge(sem_pe, pe_pos["oTa"])
                    mk(v.tensor_copy(fmbuf[:, 0:512], psT[:, 0:512]), "v")
                    v.wait_ge(sem_pe, pe_pos["oTb"])
                    mk(v.tensor_copy(fmbuf[:, 512:1024], psT[:, 512:1024]), "v")
                    v.wait_ge(sem_pe, pe_pos["oproj"])
                    v.scalar_tensor_tensor(ab[:], psA[:, 0:1024], 0.0, hb[:],
                                           op0=ALU.add, op1=ALU.add)
                    mk(v.tensor_add(ab[:], ab[:], bc3[:]), "v")
                    v.tensor_mul(tmp1[:], ab[:], bc2[:])
                    mk(v.tensor_add(tmp1[:], tmp1[:], xb[:]), "v")
                    v.wait_ge(sem_pe, pe_pos["sc15"])
                    src = kfm[:].rearrange("p (c k) -> p c k", c=CI)[:, :, 64:128]
                    dst = khalo[:].rearrange("p (c k) -> p c k", c=CI)
                    mk(v.tensor_copy(dst, src), "v")

                @blk.tensor
                def _(te, t=t, seq_start=seq_start, v_pos=v_pos, pe_pos=pe_pos):
                    te.wait_ge(sem_v, v_pos["h1"])
                    for g in range(2):
                        ins = None
                        for i in range(4):
                            cc = 4 * g + i
                            ins = te.transpose(
                                psT[:, 512 * g + 128 * i:512 * g + 128 * (i + 1)],
                                tmp1[:, 128 * cc:128 * (cc + 1)], id32[:])
                        mk(ins, "pe")
                    te.wait_ge(sem_v, v_pos["teb"])
                    for half in range(2):
                        if half == 1:
                            te.wait_ge(sem_v, v_pos["qkve1"])
                        ins = None
                        for kc in range(CI):
                            for nn in range(3):
                                co = 3072 * kc + 1536 * half + 512 * nn
                                ins = te.matmul(psA[:, 512 * nn:512 * (nn + 1)],
                                                fmbuf[:, 128 * kc:128 * (kc + 1)],
                                                wqkv[:, co:co + 512],
                                                start=(kc == 0), stop=(kc == CI - 1))
                        mk(ins, "pe")
                    for g in range(4):
                        te.wait_ge(sem_v, v_pos["qkve1" if g < 2 else "qkve2"])
                        if g >= 2:
                            te.wait_ge(sem_v, v_pos["qktea" if g == 2 else "qkteb"])
                        qk = 0 if g < 2 else 1
                        ins = None
                        for i in range(4):
                            cc = 4 * (g % 2) + i
                            src_col = 1024 * qk + 128 * cc
                            ins = te.transpose(
                                psT[:, 512 * (g % 2) + 128 * i:512 * (g % 2) + 128 * (i + 1)],
                                qkvk[:, src_col:src_col + 128], id32[:])
                        mk(ins, "pe")
                    for h in range(HEADS):
                        r0 = 64 * (h % 2)
                        qc = h // 2
                        te.wait_ge(sem_v, v_pos["qkted"])
                        te.wait_ge(sem_v, v_pos["ms0"] if h == 0 else v_pos[f"mask{h - 1}"])
                        if not seq_start:
                            te.matmul(psS[:, 0:64],
                                      qfm[r0:r0 + 64, 128 * qc:128 * (qc + 1)],
                                      khalo[r0:r0 + 64, 64 * qc:64 * (qc + 1)],
                                      start=True, stop=True)
                        mk(te.matmul(psS[:, 64:192],
                                     qfm[r0:r0 + 64, 128 * qc:128 * (qc + 1)],
                                     kfm[r0:r0 + 64, 128 * qc:128 * (qc + 1)],
                                     start=True, stop=True), "pe")
                        te.wait_ge(sem_sc, W["exp"][h])
                        if h >= 2:
                            te.wait_ge(sem_v, v_pos[f"pte{h - 2}"])
                        bnk = 512 * (h % 2)
                        te.transpose(psT[0:64, bnk + 128:bnk + 256],
                                     pbuf[:, 0:64], id32[:])
                        mk(te.transpose(psT[:, bnk:bnk + 128],
                                        pbuf[:, 64:192], id32[:]), "pe")
                        te.wait_ge(sem_v, v_pos[f"pte{h}"])
                        if h >= 2:
                            te.wait_ge(sem_v, v_pos[f"oe{h - 2}"])
                        ob = 512 * (h % 2) + 64 * (h % 4)
                        if not seq_start:
                            te.matmul(psO[:, ob:ob + 64],
                                      ptb[0:64, 128:256],
                                      vhalo[0:64, 64 * h:64 * (h + 1)],
                                      start=True, stop=False)
                        mk(te.matmul(psO[:, ob:ob + 64],
                                     ptb[:, 0:128],
                                     qkvk[:, 2048 + 64 * h:2048 + 64 * (h + 1)],
                                     start=seq_start, stop=True), "pe")
                    te.wait_ge(sem_v, v_pos["oe15"])
                    for g in range(2):
                        ins = None
                        for i in range(4):
                            cc = 4 * g + i
                            ins = te.transpose(
                                psT[:, 512 * g + 128 * i:512 * g + 128 * (i + 1)],
                                okm[:, 128 * cc:128 * (cc + 1)], id32[:])
                        mk(ins, "pe")
                    te.wait_ge(sem_v, v_pos["oteb"])
                    ins = None
                    for kc in range(CI):
                        for nn in range(2):
                            co = 1024 * kc + 512 * nn
                            ins = te.matmul(psA[:, 512 * nn:512 * (nn + 1)],
                                            fmbuf[:, 128 * kc:128 * (kc + 1)],
                                            wo[:, co:co + 512],
                                            start=(kc == 0), stop=(kc == CI - 1))
                    mk(ins, "pe")

                @blk.sync
                def _(sy, t=t, v_pos=v_pos, pe_pos=pe_pos):
                    sy.wait_ge(sem_v, v_pos["qkve2"])
                    sy.wait_ge(sem_pe, pe_pos["pv15"])
                    hal.inc(sy.dma_start(out=vhalo[:], in_=qkvk[64:128, 2048:3072]))
                    sy.wait_ge(sem_v, v_pos["a_done"])
                    st.inc(sy.dma_start(out=a_s[NT * t:NT * (t + 1), :], in_=ab[:]))
                    sy.wait_ge(sem_v, v_pos["p_done"])
                    st.inc(sy.dma_start(out=p_s[NT * t:NT * (t + 1), :], in_=tmp1[:]))
                    if debug and t in (0, 17):
                        sy.wait_ge(sem_v, v_pos["khal"])
                        dbg_dump(sy, hb[:], 128, C)
                        dbg_dump(sy, qkvk[:], 128, 3 * C)
                        dbg_dump(sy, ab[:], 128, C)
                    sy.wait_ge(st.sem, st.n)
                    sy.wait_ge(hal.sem, hal.n)

                assert cnt["v"] == v_base + len(v_names), (cnt["v"], v_base, len(v_names))
                assert cnt["pe"] == pe_base + len(pe_names), (cnt["pe"], pe_base)

    with contextlib.ExitStack() as pb:
        sbb = lambda n, s, d=F32: pb.enter_context(nc.sbuf_tensor(n, s, d))
        w1j = sbb("w1j", [128, 2 * IN4])
        w2j = sbb("w2j", [128, 2 * IN4])
        h2q = sbb("h2q", [128, 8 * C])
        mslab = sbb("mslab", [128, 8 * C])
        gq = sbb("gq", [128, 2048])
        b1f = sbb("b1f", [128, IN4 // 128])

        par = (lda, ldb)

        def issue_wj(sy, jg):
            p_ = jg % 2
            j_ = jg % 8
            for kc in range(CI):
                par[p_].inc(sy.dma_start(
                    out=w1j[:, 4096 * p_ + 512 * kc:4096 * p_ + 512 * (kc + 1)],
                    in_=w1_d[128 * kc:128 * (kc + 1), 512 * j_:512 * (j_ + 1)]))
            for s_ in range(4):
                par[p_].inc(sy.dma_start(
                    out=w2j[:, 4096 * p_ + 1024 * s_:4096 * p_ + 1024 * (s_ + 1)],
                    in_=w2_d[512 * j_ + 128 * s_:512 * j_ + 128 * (s_ + 1), :]))
            return par[p_].n

        wj_ready = {}

        with nc.Block() as blk:
            @blk.sync
            def _(sy):
                wj_ready[0] = issue_wj(sy, 0)
                ld.inc(sy.dma_start(out=bc0[:], in_=ln2g_d.broadcast_to([128, C])))
                ld.inc(sy.dma_start(out=bc1[:], in_=ln2b_d.broadcast_to([128, C])))
                ld.inc(sy.dma_start(out=bc3[:], in_=b2_d.broadcast_to([128, C])))
                ld.inc(sy.dma_start(
                    out=bc2[:], in_=modv_s[0:1, 2 * C:3 * C].broadcast_to([128, C])))
                ld.inc(sy.dma_start(out=b1f[:],
                                    in_=bass.AP(b1_h, 0, [[1, 128], [128, IN4 // 128]]),
                                    allow_slow_non_contiguous=True))
                sy.wait_ge(ld.sem, ld.n)

        for q in range(4):
            if q == 2:
                with nc.Block() as blk:
                    @blk.sync
                    def _(sy):
                        ld.inc(sy.dma_start(
                            out=bc2[:],
                            in_=modv_s[1:2, 2 * C:3 * C].broadcast_to([128, C])))
                        sy.wait_ge(ld.sem, ld.n)

            for tt in range(8):
                tq = 8 * q + tt
                with nc.Block() as blk:
                    W = {}
                    wrapd = {}

                    @blk.sync
                    def _(sy, tq=tq):
                        ld.inc(sy.dma_start(out=ab[:], in_=a_s[NT * tq:NT * (tq + 1), :]))
                        W["ld"] = ld.n

                    @blk.scalar
                    def _(sc):
                        sc = wrapd.setdefault("sc", EngWrap(sc))
                        sc.wait_ge(ld.sem, W["ld"])
                        W["sq"] = mk(sc.activation(out=fmbuf[:], in_=ab[:],
                                                   func=AF.Square, accum_out=s2[:]), "sc")

                    @blk.vector
                    def _(v):
                        v = wrapd.setdefault("v", EngWrap(v))
                        v.wait_ge(ld.sem, W["ld"])
                        v.reduce_sum(s1[:], ab[:], axis=AX)
                        v.wait_ge(sem_sc, W["sq"])
                        v.tensor_scalar(e2v[:], s2[:], INV_C, None, op0=ALU.mult)
                        v.tensor_scalar(mvec[:], s1[:], INV_C, None, op0=ALU.mult)
                        v.tensor_mul(varv[:], mvec[:], mvec[:])
                        W["var"] = mk(v.tensor_sub(varv[:], e2v[:], varv[:]), "v")

                    @blk.scalar
                    def _(sc):
                        sc = wrapd["sc"]
                        sc.wait_ge(sem_v, W["var"])
                        W["sd"] = mk(sc.activation(out=sdv[:], in_=varv[:],
                                                   func=AF.Sqrt, bias=epsb[:]), "sc")

                    @blk.vector
                    def _(v):
                        v = wrapd["v"]
                        v.wait_ge(sem_sc, W["sd"])
                        v.reciprocal(rstd[:], sdv[:])
                        v.tensor_mul(mrv[:], mvec[:], rstd[:])
                        v.tensor_scalar(tmp1[:], ab[:], rstd[:], mrv[:],
                                        op0=ALU.mult, op1=ALU.subtract)
                        v.tensor_mul(hb[:], tmp1[:], bc0[:])
                        W["h2"] = mk(v.tensor_add(hb[:], hb[:], bc1[:]), "v")

                    @blk.tensor
                    def _(te):
                        te.wait_ge(sem_v, W["h2"])
                        W["T"] = []
                        for g_ in range(2):
                            ins = None
                            for i in range(4):
                                cc = 4 * g_ + i
                                ins = te.transpose(
                                    psT[:, 512 * g_ + 128 * i:512 * g_ + 128 * (i + 1)],
                                    hb[:, 128 * cc:128 * (cc + 1)], id32[:])
                            W["T"].append(mk(ins, "pe"))

                    @blk.vector
                    def _(v, tt=tt):
                        v = wrapd["v"]
                        h2r = h2q[:].rearrange("p (c t) -> p c t", c=CI)
                        psr = psT[:].rearrange("p (c t) -> p c t", c=CI)
                        for g_ in range(2):
                            v.wait_ge(sem_pe, W["T"][g_])
                            mk(v.tensor_copy(
                                h2r[:, 4 * g_:4 * (g_ + 1), 128 * tt:128 * (tt + 1)],
                                psr[:, 4 * g_:4 * (g_ + 1), :]), "v")

            for j in range(8):
                jg = 8 * q + j
                p_ = jg % 2
                pe_b = cnt["pe"]
                sc_b = cnt["sc"]
                v_b = cnt["v"]
                pe_g = lambda st_, m, pe_b=pe_b: pe_b + st_ * 12 + m + 1
                pe_w2 = lambda st_, t4, hf_, pe_b=pe_b: pe_b + st_ * 12 + 4 + t4 * 2 + hf_ + 1
                sc_gelu = lambda st_, m, sc_b=sc_b: sc_b + st_ * 4 + m + 1
                v_ev = lambda k, v_b=v_b: v_b + k + 1

                with nc.Block() as blk:
                    @blk.sync
                    def _(sy, jg=jg):
                        if jg + 1 < 32:
                            wj_ready[jg + 1] = issue_wj(sy, jg + 1)

                    @blk.tensor
                    def _(te, j=j, jg=jg, p_=p_, pe_g=pe_g, pe_w2=pe_w2,
                          sc_gelu=sc_gelu, v_ev=v_ev):
                        te.wait_ge(par[p_].sem, wj_ready[jg])
                        for st_ in range(2):
                            for m in range(4):
                                if st_ == 1:
                                    te.wait_ge(sem_sc, sc_gelu(0, m))
                                dst = (psA[:, 512 * m:512 * (m + 1)] if m < 3
                                       else psS[:, 0:512])
                                ins = None
                                for kc in range(CI):
                                    ins = te.matmul(
                                        dst,
                                        w1j[:, 4096 * p_ + 512 * kc + 128 * m:
                                            4096 * p_ + 512 * kc + 128 * (m + 1)],
                                        h2q[:, 1024 * kc + 512 * st_:
                                            1024 * kc + 512 * (st_ + 1)],
                                        start=(kc == 0), stop=(kc == CI - 1))
                                mk(ins, "pe")
                            te.wait_ge(sem_sc, sc_gelu(st_, 3))
                            for t4 in range(4):
                                for hf_ in range(2):
                                    k = st_ * 8 + t4 * 2 + hf_
                                    if k >= 2:
                                        te.wait_ge(sem_v, v_ev(k - 2))
                                    ins = None
                                    for m in range(4):
                                        ins = te.matmul(
                                            psT[:, 512 * hf_:512 * (hf_ + 1)],
                                            gq[:, 512 * m + 128 * t4:
                                               512 * m + 128 * (t4 + 1)],
                                            w2j[:, 4096 * p_ + 1024 * m + 512 * hf_:
                                                4096 * p_ + 1024 * m + 512 * hf_ + 512],
                                            start=(m == 0), stop=(m == 3))
                                    mk(ins, "pe")

                    @blk.scalar
                    def _(sc, j=j, pe_g=pe_g, pe_w2=pe_w2):
                        sc = EngWrap(sc)
                        for st_ in range(2):
                            if st_ == 1:
                                sc.wait_ge(sem_pe, pe_w2(0, 3, 1))
                            for m in range(4):
                                sc.wait_ge(sem_pe, pe_g(st_, m))
                                src_ = (psA[:, 512 * m:512 * (m + 1)] if m < 3
                                        else psS[:, 0:512])
                                mk(sc.activation(out=gq[:, 512 * m:512 * (m + 1)],
                                                 in_=src_, func=AF.Gelu,
                                                 bias=b1f[:, 4 * j + m:4 * j + m + 1]),
                                   "sc")

                    @blk.vector
                    def _(v, j=j, pe_w2=pe_w2):
                        v = EngWrap(v)
                        for st_ in range(2):
                            for t4 in range(4):
                                for hf_ in range(2):
                                    v.wait_ge(sem_pe, pe_w2(st_, t4, hf_))
                                    tt_ = 4 * st_ + t4
                                    dst = mslab[:, 1024 * tt_ + 512 * hf_:
                                                1024 * tt_ + 512 * hf_ + 512]
                                    if j == 0:
                                        ins = v.scalar_tensor_tensor(
                                            dst, psT[:, 512 * hf_:512 * (hf_ + 1)], 0.0,
                                            bc3[:, 512 * hf_:512 * (hf_ + 1)],
                                            op0=ALU.add, op1=ALU.add)
                                    else:
                                        ins = v.tensor_add(
                                            dst, dst, psT[:, 512 * hf_:512 * (hf_ + 1)])
                                    mk(ins, "v")

            for tt in range(8):
                tq = 8 * q + tt
                with nc.Block() as blk:
                    W = {}

                    @blk.sync
                    def _(sy, tq=tq):
                        ld.inc(sy.dma_start(out=xb[:], in_=p_s[NT * tq:NT * (tq + 1), :]))
                        W["ld"] = ld.n

                    @blk.vector
                    def _(v, tt=tt):
                        v = EngWrap(v)
                        v.wait_ge(ld.sem, W["ld"])
                        v.tensor_mul(tmp1[:], mslab[:, 1024 * tt:1024 * (tt + 1)], bc2[:])
                        W["x1"] = mk(v.tensor_add(tmp1[:], tmp1[:], xb[:]), "v")

                    @blk.sync
                    def _(sy, tq=tq):
                        sy.wait_ge(sem_v, W["x1"])
                        st.inc(sy.dma_start(out=x1_s[NT * tq:NT * (tq + 1), :], in_=tmp1[:]))
                        sy.wait_ge(st.sem, st.n)

    par = (lda, ldb)

    def w_mean_pass(w_dram, nch, width, col, stg):
        v_b = cnt["v"]
        p0_ = [par[0].n, par[1].n]
        with nc.Block() as blk:
            @blk.sync
            def _(sy, nch=nch, width=width):
                for ci_ in range(nch):
                    pp = ci_ % 2
                    if ci_ >= 2:
                        sy.wait_ge(sem_v, v_b + ci_ - 1)
                    par[pp].inc(sy.dma_start(
                        out=stg[:, width * pp:width * (pp + 1)],
                        in_=w_dram[128 * ci_:128 * (ci_ + 1), :]))

            @blk.vector
            def _(v, nch=nch, width=width):
                v = EngWrap(v)
                base = list(p0_)
                for ci_ in range(nch):
                    pp = ci_ % 2
                    base[pp] += 16
                    v.wait_ge(par[pp].sem, base[pp])
                    v.tensor_reduce(s1[:], stg[:, width * pp:width * (pp + 1)],
                                    axis=AX, op=ALU.add, apply_absolute_value=True)
                    if ci_ == 0:
                        ins = v.tensor_copy(sc_4[:], s1[:])
                    else:
                        ins = v.tensor_add(sc_4[:], sc_4[:], s1[:])
                    mk(ins, "v")

        with nc.Block() as blk:
            @blk.tensor
            def _(te):
                mk(te.transpose(psS[0:1, 0:128], sc_4[:], id32[:]), "pe")

            @blk.vector
            def _(v, nch=nch, width=width):
                v = EngWrap(v)
                v.wait_ge(sem_pe, cnt["pe"])
                v.reduce_sum(s1[0:1, 0:1], psS[0:1, 0:128], axis=AX)
                v.tensor_scalar(s1[0:1, 0:1], s1[0:1, 0:1],
                                1.0 / (nch * 128 * width), 1e-5,
                                op0=ALU.mult, op1=ALU.max)
                v.reciprocal(s2[0:1, 0:1], s1[0:1, 0:1])
                mk(v.tensor_copy(mvec[0:1, 0:1], s1[0:1, 0:1]), "v")

            @blk.sync
            def _(sy, col=col):
                sy.wait_ge(sem_v, cnt["v"])
                st.inc(sy.dma_start(out=sw_s[0:1, col:col + 1], in_=mvec[0:1, 0:1]))
                st.inc(sy.dma_start(out=sw_s[0:1, col + 3:col + 4], in_=s2[0:1, 0:1]))
                sy.wait_ge(st.sem, st.n)
                ld.inc(sy.dma_start(out=swb[:],
                                    in_=sw_s[0:1, 0:8].broadcast_to([128, 8])))
                sy.wait_ge(ld.sem, ld.n)

    def w_tern_pass(w_dram, nch, width, scol, dst_of_piece, stg, tA, tB, tern):
        ng = width // 1024
        v_per = 1 + ng
        v_b = cnt["v"]
        pe_b = cnt["pe"]
        p0_ = [par[0].n, par[1].n]

        def v_tern(ci_):
            return v_b + ci_ * v_per + 1

        def v_evac(ci_, g_):
            return v_b + ci_ * v_per + 1 + g_ + 1

        def pe_T(ci_, g_):
            return pe_b + ci_ * ng + g_ + 1

        with nc.Block() as blk:
            @blk.sync
            def _(sy, nch=nch, width=width):
                for ci_ in range(nch):
                    pp = ci_ % 2
                    if ci_ >= 2:
                        sy.wait_ge(sem_v, v_tern(ci_ - 2))
                    par[pp].inc(sy.dma_start(
                        out=stg[:, width * pp:width * (pp + 1)],
                        in_=w_dram[128 * ci_:128 * (ci_ + 1), :]))

            @blk.vector
            def _(v, nch=nch, width=width, scol=scol):
                v = EngWrap(v)
                base = list(p0_)
                for ci_ in range(nch):
                    pp = ci_ % 2
                    base[pp] += 16
                    v.wait_ge(par[pp].sem, base[pp])
                    sl = stg[:, width * pp:width * (pp + 1)]
                    v.tensor_scalar(tA[:, 0:width], sl, swb[:, scol + 3:scol + 4],
                                    0.5, op0=ALU.mult, op1=ALU.is_gt)
                    v.tensor_scalar(tB[:, 0:width], sl, swb[:, scol + 3:scol + 4],
                                    -0.5, op0=ALU.mult, op1=ALU.is_lt)
                    if ci_ >= 1:
                        v.wait_ge(sem_pe, pe_T(ci_ - 1, ng - 1))
                    mk(v.tensor_sub(tern[:, 0:width], tA[:, 0:width],
                                    tB[:, 0:width]), "v")
                    for g_ in range(ng):
                        v.wait_ge(sem_pe, pe_T(ci_, g_))
                        src_ = psS[:, 0:512].bitcast(BF16).rearrange(
                            "p (i t) -> p i t", i=8)
                        mk(v.tensor_copy(dst_of_piece(ci_, g_), src_), "v")

            @blk.tensor
            def _(te, nch=nch, width=width):
                for ci_ in range(nch):
                    te.wait_ge(sem_v, v_tern(ci_))
                    for g_ in range(ng):
                        if g_ >= 1:
                            te.wait_ge(sem_v, v_evac(ci_, g_ - 1))
                        ins = None
                        for i in range(8):
                            pi = 8 * g_ + i
                            ins = te.transpose(
                                psS[:, 64 * i:64 * (i + 1)].bitcast(BF16),
                                tern[:, 128 * pi:128 * (pi + 1)], idbf[:])
                        mk(ins, "pe")

    def ladder_v(v, src, width, s_ap, r_out):
        w2_ = width // 2
        v.tensor_scalar(okm[:, 0:width], src, s_ap, 0.5, op0=ALU.mult, op1=ALU.add)
        bmk = mk(v.tensor_scalar(tmp1[:, 0:width], okm[:, 0:width], 128.0, None,
                                 op0=ALU.add), "v")
        for k in (128, 64, 32, 16, 8, 4, 2, 1):
            v.tensor_scalar(fmbuf[:, 0:w2_], tmp1[:, 0:w2_], float(k), -float(k),
                            op0=ALU.is_ge, op1=ALU.mult)
            v.tensor_add(tmp1[:, 0:w2_], tmp1[:, 0:w2_], fmbuf[:, 0:w2_])
        rmk = mk(v.tensor_sub(r_out[:, 0:w2_], okm[:, 0:w2_], tmp1[:, 0:w2_]), "v")
        return bmk, rmk

    def ladder_g(g, width, r_out, bmk):
        w2_ = width // 2
        g.wait_ge(sem_v, bmk)
        for k in (128, 64, 32, 16, 8, 4, 2, 1):
            g.tensor_scalar(fmbuf[:, w2_:width], tmp1[:, w2_:width], float(k),
                            -float(k), op0=ALU.is_ge, op1=ALU.mult)
            g.tensor_add(tmp1[:, w2_:width], tmp1[:, w2_:width],
                         fmbuf[:, w2_:width])
        return mk(g.tensor_sub(r_out[:, w2_:width], okm[:, w2_:width],
                               tmp1[:, w2_:width]), "gp")

    with contextlib.ExitStack() as pc:
        sbc = lambda n, s, d=F32: pc.enter_context(nc.sbuf_tensor(n, s, d))
        wgf = sbc("wgf", [128, CI * IN4], BF16)
        wuf = sbc("wuf", [128, CI * IN4], BF16)
        gu = sbc("gu", [128, IN4])
        rhf_km = qkvk[:, 0:512].bitcast(BF16)
        rhf_fm = qkvk[:, 512:1024].bitcast(BF16)
        rgu_km = qkvk[:, 1024:3072].bitcast(BF16)
        tern_c = okm[:, 0:512].bitcast(BF16)

        def wguf_dst(ci_, g_, buf):
            r = buf[:].rearrange("p (h i) -> p h i", h=CI)
            return r[:, :, 128 * ci_:128 * (ci_ + 1)]

        w_mean_pass(wg_d, 32, 1024, 0, gu)
        w_tern_pass(wg_d, 32, 1024, 0, lambda c, g: wguf_dst(c, g, wgf),
                    gu, ab, tmp1, tern_c)
        w_mean_pass(wu_d, 32, 1024, 1, gu)
        w_tern_pass(wu_d, 32, 1024, 1, lambda c, g: wguf_dst(c, g, wuf),
                    gu, ab, tmp1, tern_c)

        with nc.Block() as blk:
            @blk.sync
            def _(sy):
                ld.inc(sy.dma_start(
                    out=bc0[:], in_=modv_s[0:1, 3 * C:4 * C].broadcast_to([128, C])))
                ld.inc(sy.dma_start(
                    out=bc1[:], in_=modv_s[0:1, 4 * C:5 * C].broadcast_to([128, C])))
                sy.wait_ge(ld.sem, ld.n)

        for t in range(TILES):
            if t == TPB:
                with nc.Block() as blk:
                    @blk.sync
                    def _(sy):
                        ld.inc(sy.dma_start(
                            out=bc0[:],
                            in_=modv_s[1:2, 3 * C:4 * C].broadcast_to([128, C])))
                        ld.inc(sy.dma_start(
                            out=bc1[:],
                            in_=modv_s[1:2, 4 * C:5 * C].broadcast_to([128, C])))
                        sy.wait_ge(ld.sem, ld.n)

            with nc.Block() as blk:
                W = {}
                wrapd = {}
                gp_b = cnt["gp"]
                gp_hf = gp_b + 1
                gp_rq = lambda qr, gp_b=gp_b: gp_b + 2 + qr

                @blk.sync
                def _(sy, t=t):
                    ld.inc(sy.dma_start(out=xb[:], in_=x1_s[NT * t:NT * (t + 1), :]))
                    W["ld"] = ld.n

                @blk.scalar
                def _(sc):
                    sc = wrapd.setdefault("sc", EngWrap(sc))
                    sc.wait_ge(ld.sem, W["ld"])
                    W["sq"] = mk(sc.activation(out=fmbuf[:], in_=xb[:], func=AF.Square,
                                               accum_out=s2[:]), "sc")

                @blk.vector
                def _(v):
                    v = wrapd.setdefault("v", EngWrap(v))
                    v.wait_ge(ld.sem, W["ld"])
                    v.reduce_sum(s1[:], xb[:], axis=AX)
                    v.wait_ge(sem_sc, W["sq"])
                    v.tensor_scalar(e2v[:], s2[:], INV_C, None, op0=ALU.mult)
                    v.tensor_scalar(mvec[:], s1[:], INV_C, None, op0=ALU.mult)
                    v.tensor_mul(varv[:], mvec[:], mvec[:])
                    W["var"] = mk(v.tensor_sub(varv[:], e2v[:], varv[:]), "v")

                @blk.scalar
                def _(sc):
                    sc = wrapd["sc"]
                    sc.wait_ge(sem_v, W["var"])
                    W["sd"] = mk(sc.activation(out=sdv[:], in_=varv[:], func=AF.Sqrt,
                                               bias=epsb[:]), "sc")

                @blk.vector
                def _(v):
                    v = wrapd["v"]
                    v.wait_ge(sem_sc, W["sd"])
                    v.reciprocal(rstd[:], sdv[:])
                    v.tensor_mul(mrv[:], mvec[:], rstd[:])
                    v.tensor_scalar(tmp1[:], xb[:], rstd[:], mrv[:],
                                    op0=ALU.mult, op1=ALU.subtract)
                    v.tensor_mul(hb[:], tmp1[:], bc0[:])
                    v.tensor_add(hb[:], hb[:], bc1[:])
                    v.tensor_reduce(s1[:], hb[:], axis=AX, op=ALU.max,
                                    apply_absolute_value=True)
                    v.tensor_scalar(sc_0[:], s1[:], 1e-5, None, op0=ALU.max)
                    v.reciprocal(sc_1[:], sc_0[:])
                    v.tensor_scalar(sc_1[:], sc_1[:], 127.0, None, op0=ALU.mult)
                    v.tensor_mul(sc_2[:], sc_0[:], swb[:, 0:1])
                    v.tensor_scalar(sc_2[:], sc_2[:], 1.0 / 127.0, None, op0=ALU.mult)
                    v.tensor_mul(sc_3[:], sc_0[:], swb[:, 1:2])
                    v.tensor_scalar(sc_3[:], sc_3[:], 1.0 / 127.0, None, op0=ALU.mult)
                    W["lad_hf"] = ladder_v(v, hb[:], C, sc_1[:], rhf_km)

                @blk.gpsimd
                def _(g):
                    g = wrapd.setdefault("g", EngWrap(g))
                    W["lad_hf_g"] = ladder_g(g, C, rhf_km, W["lad_hf"][0])
                    assert W["lad_hf_g"] == gp_hf, (W["lad_hf_g"], gp_hf)

                @blk.tensor
                def _(te, t=t):
                    te.wait_ge(sem_v, W["lad_hf"][1])
                    te.wait_ge(sem_gp, W["lad_hf_g"])
                    ins = None
                    for i in range(8):
                        ins = te.transpose(psS[:, 64 * i:64 * (i + 1)].bitcast(BF16),
                                           rhf_km[:, 128 * i:128 * (i + 1)], idbf[:])
                    W["rhfT"] = mk(ins, "pe")

                @blk.vector
                def _(v):
                    v = wrapd["v"]
                    v.wait_ge(sem_pe, W["rhfT"])
                    W["rhf_fm"] = mk(v.tensor_copy(rhf_fm[:],
                                                   psS[:, 0:512].bitcast(BF16)), "v")

                sc_b = cnt["sc"]
                v_b2 = cnt["v"]
                sc_silu = lambda qr, sc_b=sc_b: sc_b + qr + 1
                v_guq = lambda qr, v_b2=v_b2: v_b2 + 2 * qr + 1

                @blk.tensor
                def _(te, t=t, sc_silu=sc_silu, v_guq=v_guq):
                    W["gmm"] = []
                    W["umm"] = []
                    te.wait_ge(sem_v, W["rhf_fm"])
                    for qr in range(4):
                        if qr >= 1:
                            te.wait_ge(sem_sc, sc_silu(qr - 1))
                        ins = None
                        for nn in range(2):
                            for kc in range(CI):
                                ins = te.matmul(
                                    psT[:, 512 * nn:512 * (nn + 1)],
                                    rhf_fm[:, 128 * kc:128 * (kc + 1)],
                                    wgf[:, 4096 * kc + 1024 * qr + 512 * nn:
                                        4096 * kc + 1024 * qr + 512 * (nn + 1)],
                                    start=(kc == 0), stop=(kc == CI - 1))
                        W["gmm"].append(mk(ins, "pe"))
                        if qr >= 1:
                            te.wait_ge(sem_v, v_guq(qr - 1))
                        ins = None
                        for nn in range(2):
                            for kc in range(CI):
                                ins = te.matmul(
                                    psO[:, 512 * nn:512 * (nn + 1)],
                                    rhf_fm[:, 128 * kc:128 * (kc + 1)],
                                    wuf[:, 4096 * kc + 1024 * qr + 512 * nn:
                                        4096 * kc + 1024 * qr + 512 * (nn + 1)],
                                    start=(kc == 0), stop=(kc == CI - 1))
                        W["umm"].append(mk(ins, "pe"))

                @blk.scalar
                def _(sc, sc_silu=sc_silu, v_guq=v_guq):
                    sc = wrapd["sc"]
                    for qr in range(4):
                        sc.wait_ge(sem_pe, W["gmm"][qr])
                        if qr >= 1:
                            sc.wait_ge(sem_v, v_guq(qr - 1))
                        assert mk(sc.activation(out=ab[:], in_=psT[:], func=AF.Silu,
                                                scale=sc_2[:]), "sc") == sc_silu(qr)

                @blk.vector
                def _(v):
                    v = wrapd["v"]
                    for qr in range(4):
                        v.wait_ge(sem_sc, sc_silu(qr))
                        v.wait_ge(sem_pe, W["umm"][qr])
                        assert mk(v.tensor_mul(gu[:, 1024 * qr:1024 * (qr + 1)],
                                               ab[:], psO[:]), "v") == v_guq(qr)
                        v.tensor_reduce(s1[:], gu[:, 1024 * qr:1024 * (qr + 1)],
                                        axis=AX, op=ALU.max,
                                        apply_absolute_value=True)
                        if qr == 0:
                            ins = v.tensor_copy(sc_4[:], s1[:])
                        else:
                            ins = v.tensor_max(sc_4[:], sc_4[:], s1[:])
                        mk(ins, "v")
                    v.tensor_mul(sc_4[:], sc_4[:], sc_3[:])
                    v.tensor_scalar(sc_4[:], sc_4[:], 1e-5, None, op0=ALU.max)
                    v.reciprocal(s2[:], sc_4[:])
                    v.tensor_mul(sc_1[:], sc_3[:], s2[:])
                    v.tensor_scalar(sc_1[:], sc_1[:], 127.0, None, op0=ALU.mult)
                    W["osc"] = mk(v.tensor_scalar(sc_0[:], sc_4[:], 1.0 / 127.0, None,
                                                  op0=ALU.mult), "v")
                    W["rq_v"] = []
                    W["rq_b"] = []
                    for qr in range(4):
                        v.wait_ge(sem_gp, gp_hf if qr == 0 else gp_rq(qr - 1))
                        bmk, rmk = ladder_v(v, gu[:, 1024 * qr:1024 * (qr + 1)], C,
                                            sc_1[:],
                                            rgu_km[:, 1024 * qr:1024 * (qr + 1)])
                        W["rq_v"].append(rmk)
                        W["rq_b"].append(bmk)

                @blk.gpsimd
                def _(g):
                    g = wrapd["g"]
                    W["rq_g"] = []
                    for qr in range(4):
                        r_ = ladder_g(g, C, rgu_km[:, 1024 * qr:1024 * (qr + 1)],
                                      W["rq_b"][qr])
                        assert r_ == gp_rq(qr), (r_, gp_rq(qr))
                        W["rq_g"].append(r_)

                @blk.sync
                def _(sy, t=t):
                    sy.wait_ge(sem_v, W["osc"])
                    st.inc(sy.dma_start(out=osc_s[NT * t:NT * (t + 1), :], in_=sc_0[:]))
                    sy.wait_ge(sem_v, W["rq_v"][3])
                    sy.wait_ge(sem_gp, W["rq_g"][3])
                    st.inc(sy.dma_start(out=r_s[NT * t:NT * (t + 1), :], in_=rgu_km[:]))
                    sy.wait_ge(st.sem, st.n)

    with contextlib.ExitStack() as pd_:
        sbd = lambda n, s, d=F32: pd_.enter_context(nc.sbuf_tensor(n, s, d))
        wdf = sbd("wdf", [128, 32 * C], BF16)
        rkm = sbd("rkm", [128, IN4], BF16)
        stgd = sbd("stgd", [128, 2 * IN4])
        tAd = sbd("tAd", [128, IN4])
        tBd = sbd("tBd", [128, IN4])
        rfm = qkvk[:, 0:2048].bitcast(BF16)
        tern_d = rkm[:]

        def wd_dst(ci_, g_):
            r = wdf[:].rearrange("p (i o) -> p i o", i=32)
            return r[:, 8 * g_:8 * (g_ + 1), 128 * ci_:128 * (ci_ + 1)]

        w_mean_pass(wd_d, 8, 4096, 2, stgd)
        w_tern_pass(wd_d, 8, 4096, 2, wd_dst, stgd, tAd, tBd, tern_d)

        with nc.Block() as blk:
            @blk.sync
            def _(sy):
                ld.inc(sy.dma_start(
                    out=bc2[:], in_=modv_s[0:1, 5 * C:6 * C].broadcast_to([128, C])))
                sy.wait_ge(ld.sem, ld.n)

        for t in range(TILES):
            if t == TPB:
                with nc.Block() as blk:
                    @blk.sync
                    def _(sy):
                        ld.inc(sy.dma_start(
                            out=bc2[:],
                            in_=modv_s[1:2, 5 * C:6 * C].broadcast_to([128, C])))
                        sy.wait_ge(ld.sem, ld.n)

            with nc.Block() as blk:
                W = {}
                wrapd = {}
                v_b = cnt["v"]
                v_evc = lambda g_, v_b=v_b: v_b + g_ + 1

                @blk.sync
                def _(sy, t=t):
                    ld.inc(sy.dma_start(out=rkm[:], in_=r_s[NT * t:NT * (t + 1), :]))
                    ld.inc(sy.dma_start(out=xb[:], in_=x1_s[NT * t:NT * (t + 1), :]))
                    ld.inc(sy.dma_start(out=sc_0[:], in_=osc_s[NT * t:NT * (t + 1), :]))
                    W["ld"] = ld.n

                @blk.tensor
                def _(te, t=t, v_evc=v_evc):
                    te.wait_ge(ld.sem, W["ld"])
                    W["T"] = []
                    for g_ in range(4):
                        if g_ >= 1:
                            te.wait_ge(sem_v, v_evc(g_ - 1))
                        ins = None
                        for i in range(8):
                            pi = 8 * g_ + i
                            ins = te.transpose(
                                psS[:, 64 * i:64 * (i + 1)].bitcast(BF16),
                                rkm[:, 128 * pi:128 * (pi + 1)], idbf[:])
                        W["T"].append(mk(ins, "pe"))

                @blk.vector
                def _(v, t=t):
                    v = wrapd.setdefault("v", EngWrap(v))
                    v.wait_ge(ld.sem, W["ld"])
                    v.tensor_mul(sc_0[:], sc_0[:], swb[:, 2:3])
                    for g_ in range(4):
                        v.wait_ge(sem_pe, W["T"][g_])
                        src_ = psS[:, 0:512].bitcast(BF16).rearrange(
                            "p (i t) -> p i t", i=8)
                        dst = rfm[:].rearrange("p (i t) -> p i t", i=32)[
                            :, 8 * g_:8 * (g_ + 1), :]
                        assert mk(v.tensor_copy(dst, src_), "v") == v_evc(g_)

                @blk.tensor
                def _(te, t=t, v_evc=v_evc):
                    te.wait_ge(sem_v, v_evc(3))
                    W["dn"] = []
                    for nn in range(2):
                        ins = None
                        for ic in range(32):
                            ins = te.matmul(
                                psT[:, 512 * nn:512 * (nn + 1)],
                                rfm[:, 128 * ic:128 * (ic + 1)],
                                wdf[:, 1024 * ic + 512 * nn:1024 * ic + 512 * (nn + 1)],
                                start=(ic == 0), stop=(ic == 31))
                        W["dn"].append(mk(ins, "pe"))

                @blk.vector
                def _(v, t=t):
                    v = wrapd["v"]
                    for nn in range(2):
                        v.wait_ge(sem_pe, W["dn"][nn])
                        v.tensor_scalar(tmp1[:, 512 * nn:512 * (nn + 1)],
                                        psT[:, 512 * nn:512 * (nn + 1)],
                                        sc_0[:], None, op0=ALU.mult)
                    v.tensor_mul(tmp1[:], tmp1[:], bc2[:])
                    W["x2"] = mk(v.tensor_add(ab[:], tmp1[:], xb[:]), "v")

                @blk.sync
                def _(sy, t=t):
                    sy.wait_ge(sem_v, W["x2"])
                    st.inc(sy.dma_start(out=out_d[NT * t:NT * (t + 1), :], in_=ab[:]))
                    sy.wait_ge(st.sem, st.n)

    es.close()
    return nc


_NC_CACHE = [None]


def _shard_inputs(inputs):
    f32 = lambda a: np.ascontiguousarray(np.asarray(a, dtype=np.float32))
    x = f32(inputs["L0_attn_x"]).reshape(B, T, C)
    cv = f32(inputs["L0_c"])
    common = {
        "wqkv": f32(inputs["L0_attn_w_qkv"]),
        "bqkv": f32(inputs["L0_attn_b_qkv"]).reshape(1, 3 * C),
        "wout": f32(inputs["L0_attn_w_out"]),
        "bout": f32(inputs["L0_attn_b_out"]).reshape(1, C),
        "ln1g": f32(inputs["L0_attn_ln1_g"]).reshape(1, C),
        "ln1b": f32(inputs["L0_attn_ln1_b"]).reshape(1, C),
        "ln2g": f32(inputs["L0_attn_ln2_g"]).reshape(1, C),
        "ln2b": f32(inputs["L0_attn_ln2_b"]).reshape(1, C),
        "w1": f32(inputs["L0_attn_w1"]),
        "b1": f32(inputs["L0_attn_b1"]).reshape(1, IN4),
        "w2": f32(inputs["L0_attn_w2"]),
        "b2": f32(inputs["L0_attn_b2"]).reshape(1, C),
        "wg": f32(inputs["L0_ffn_w_gate"]),
        "wu": f32(inputs["L0_ffn_w_up"]),
        "wd": f32(inputs["L0_ffn_w_down"]),
        "wmod": f32(inputs["L0_wmod"]),
    }
    maps = []
    for i in range(NCORES):
        m = dict(common)
        m["x"] = np.ascontiguousarray(x[BPC * i:BPC * (i + 1)]).reshape(TOK, C)
        m["cvec"] = np.ascontiguousarray(cv[BPC * i:BPC * (i + 1)])
        maps.append(m)
    return maps


def kernel(**inputs):
    if _NC_CACHE[0] is None:
        _NC_CACHE[0] = build_nc(debug=0)
    nc = _NC_CACHE[0]
    in_maps = _shard_inputs(inputs)
    res = run_bass_kernel_spmd(nc, in_maps, core_ids=list(range(NCORES)))
    out = np.concatenate(
        [np.asarray(r["out"], dtype=np.float32).reshape(BPC, T, C)
         for r in res.results], axis=0)
    return out


# version 02
# speedup vs baseline: 1.0038x; 1.0038x over previous
import contextlib
import numpy as np
import ml_dtypes
import concourse.bass as bass
import concourse.mybir as mybir
from concourse.bass_utils import run_bass_kernel_spmd

F32 = mybir.dt.float32
F16 = mybir.dt.float16
BF16 = mybir.dt.bfloat16
ALU = mybir.AluOpType
AF = mybir.ActivationFunctionType
AX = mybir.AxisListType.X

B, T, C = 16, 2048, 1024
NCORES = 8
BPC = B // NCORES
TOK = BPC * T
NT = 128
TILES = TOK // NT
TPB = T // NT
HEADS, HD = 16, 64
CI = C // 128
IN4 = 4096
EPS = 1e-5
NEG = -1.0e30
INV_C = 1.0 / C


def _mask_np():
    q = np.arange(128)[:, None]
    j = np.arange(192)[None, :]
    allowed = (j >= 64 * (q >= 64)) & (j <= q + 64)
    return np.where(allowed, np.float32(0.0), np.float32(NEG)).astype(np.float32)


class EngWrap:

    def __init__(self, eng):
        self.e = eng
        self.w = set()
        self.r = set()

    @staticmethod
    def _nm(a):
        try:
            return a.tensor.name
        except AttributeError:
            return None

    def _pre(self, outs, ins):
        on = {self._nm(a) for a in outs} - {None}
        inn = {self._nm(a) for a in ins} - {None}
        if ((on | inn) & self.w) or (on & self.r):
            self.e.drain()
            self.w.clear()
            self.r.clear()
        self.w |= on
        self.r |= inn

    def wait_ge(self, sem, val):
        return self.e.wait_ge(sem, val)

    def drain(self):
        ins = self.e.drain()
        self.w.clear()
        self.r.clear()
        return ins

    def tensor_scalar(self, out, in0, s1, s2, **kw):
        self._pre([out], [in0, s1, s2])
        return self.e.tensor_scalar(out, in0, s1, s2, **kw)

    def scalar_tensor_tensor(self, out, in0, s, in1, **kw):
        self._pre([out], [in0, s, in1])
        return self.e.scalar_tensor_tensor(out, in0, s, in1, **kw)

    def tensor_tensor(self, out, in0, in1, **kw):
        self._pre([out], [in0, in1])
        return self.e.tensor_tensor(out, in0, in1, **kw)

    def tensor_add(self, out, a, b):
        self._pre([out], [a, b])
        return self.e.tensor_add(out, a, b)

    def tensor_sub(self, out, a, b):
        self._pre([out], [a, b])
        return self.e.tensor_sub(out, a, b)

    def tensor_mul(self, out, a, b):
        self._pre([out], [a, b])
        return self.e.tensor_mul(out, a, b)

    def tensor_max(self, out, a, b):
        self._pre([out], [a, b])
        return self.e.tensor_max(out, a, b)

    def tensor_copy(self, out, in_):
        self._pre([out], [in_])
        return self.e.tensor_copy(out, in_)

    def memset(self, ap, c):
        self._pre([ap], [])
        return self.e.memset(ap, c)

    def tensor_reduce(self, out, in_, **kw):
        self._pre([out], [in_])
        return self.e.tensor_reduce(out, in_, **kw)

    def reduce_sum(self, out, in_, **kw):
        self._pre([out], [in_])
        return self.e.reduce_sum(out, in_, **kw)

    def reduce_max(self, out, in_, **kw):
        self._pre([out], [in_])
        return self.e.reduce_max(out, in_, **kw)

    def reciprocal(self, out, in_):
        self._pre([out], [in_])
        return self.e.reciprocal(out, in_)

    def activation(self, out, in_, func, bias=0.0, scale=1.0, accum_out=None):
        outs = [out] + ([accum_out] if accum_out is not None else [])
        self._pre(outs, [in_, bias, scale])
        kw = dict(func=func, bias=bias, scale=scale)
        if accum_out is not None:
            kw["accum_out"] = accum_out
        return self.e.activation(out=out, in_=in_, **kw)



class Mono:
    def __init__(self, sem, step=16):
        self.sem = sem
        self.step = step
        self.n = 0

    def inc(self, instr):
        instr.then_inc(self.sem, self.step)
        self.n += self.step
        return self.n


def build_nc(debug=0):
    nc = bass.Bass("TRN2", target_bir_lowering=False)

    def din(name, shape, dt=F32):
        return nc.dram_tensor(name, shape, dt, kind="ExternalInput")

    x_d = din("x", [TOK, C]).ap()
    wqkv_d = din("wqkv", [C, 3 * C]).ap()
    bqkv_d = din("bqkv", [1, 3 * C]).ap()
    wout_d = din("wout", [C, C]).ap()
    bout_d = din("bout", [1, C]).ap()
    ln1g_h = din("ln1g", [1, C])
    ln1b_h = din("ln1b", [1, C])
    ln2g_d = din("ln2g", [1, C]).ap()
    ln2b_d = din("ln2b", [1, C]).ap()
    w1_d = din("w1", [C, IN4]).ap()
    b1_h = din("b1", [1, IN4])
    w2_d = din("w2", [IN4, C]).ap()
    b2_d = din("b2", [1, C]).ap()
    wg_d = din("wg", [IN4, C]).ap()
    wu_d = din("wu", [IN4, C]).ap()
    wd_d = din("wd", [C, IN4]).ap()
    c_d = din("cvec", [BPC, C]).ap()
    wmod_d = din("wmod", [C, 6 * C]).ap()

    out_d = nc.dram_tensor("out", [TOK, C], F32, kind="ExternalOutput").ap()

    def dscr(name, shape, dt=F32, tap=False):
        if debug and tap:
            return nc.dram_tensor(name, shape, dt, kind="ExternalOutput").ap()
        return nc.dram_tensor(name, shape, dt).ap()

    a_s = dscr("a_s", [TOK, C], tap=True)
    p_s = dscr("p_s", [TOK, C], tap=True)
    x1_s = dscr("x1_s", [TOK, C])
    r_s = dscr("r_s", [TOK, IN4], BF16)
    modv_s = dscr("modv_s", [BPC, 6 * C])
    qb_h = nc.dram_tensor("qb_s", [1, 3 * C], F32)
    qb_s = qb_h.ap()
    osc_s = dscr("osc_s", [TOK, 1])
    sw_s = dscr("sw_s", [1, 8])

    dbg_d = {}
    if debug:
        for i in range(debug):
            dbg_d[i] = nc.dram_tensor(f"dbg{i}", [128, 4 * C], F32,
                                      kind="ExternalOutput").ap()

    ident32_c = nc.inline_tensor(np.eye(128, dtype=np.float32), name="ident32").ap()
    identbf_c = nc.inline_tensor(np.eye(128, dtype=ml_dtypes.bfloat16), name="identbf").ap()
    mask_c = nc.inline_tensor(_mask_np(), name="maskc").ap()

    es = contextlib.ExitStack()

    def sbuf(name, shape, dt=F32):
        return es.enter_context(nc.sbuf_tensor(name, shape, dt))


    id32 = sbuf("id32", [128, 128])
    idbf = sbuf("idbf", [128, 128], BF16)
    maskb = sbuf("maskb", [128, 192])
    bc0 = sbuf("bc0", [128, C])
    bc1 = sbuf("bc1", [128, C])
    bc2 = sbuf("bc2", [128, C])
    s1 = sbuf("s1", [128, 1])
    s2 = sbuf("s2", [128, 1])
    mvec = sbuf("mvec", [128, 1])
    e2v = sbuf("e2v", [128, 1])
    varv = sbuf("varv", [128, 1])
    sdv = sbuf("sdv", [128, 1])
    rstd = sbuf("rstd", [128, 1])
    mrv = sbuf("mrv", [128, 1])
    rstd1 = sbuf("rstd1", [128, 1])
    mrv1 = sbuf("mrv1", [128, 1])
    mx16 = sbuf("mx16", [128, 16])
    ng16 = sbuf("ng16", [128, 16])
    rs16 = sbuf("rs16", [128, 16])
    rr16 = sbuf("rr16", [128, 16])
    sc_0 = sbuf("sc_0", [128, 1])
    sc_1 = sbuf("sc_1", [128, 1])
    sc_2 = sbuf("sc_2", [128, 1])
    sc_3 = sbuf("sc_3", [128, 1])
    sc_4 = sbuf("sc_4", [128, 1])
    swb = sbuf("swb", [128, 8])
    epsb = sbuf("epsb", [128, 1])
    xb = sbuf("xb", [128, C])
    hb = sbuf("hb", [128, C])
    tmp1 = sbuf("tmp1", [128, C])
    fmbuf = sbuf("fmbuf", [128, C])
    qkvk = sbuf("qkvk", [128, 3 * C])
    okm = sbuf("okm", [128, C])
    ab = sbuf("ab", [128, C])

    sem_pe = es.enter_context(nc.semaphore("sem_pe"))
    sem_v = es.enter_context(nc.semaphore("sem_v"))
    sem_sc = es.enter_context(nc.semaphore("sem_sc"))
    sem_gp = es.enter_context(nc.semaphore("sem_gp"))
    ld = Mono(es.enter_context(nc.semaphore("ld_sem")))
    lda = Mono(es.enter_context(nc.semaphore("lda_sem")))
    ldb = Mono(es.enter_context(nc.semaphore("ldb_sem")))
    st = Mono(es.enter_context(nc.semaphore("st_sem")))
    hal = Mono(es.enter_context(nc.semaphore("hal_sem")))

    cnt = {"pe": 0, "v": 0, "sc": 0, "gp": 0}
    sems = {"pe": sem_pe, "v": sem_v, "sc": sem_sc, "gp": sem_gp}

    def mk(instr, which, n=1):
        instr.then_inc(sems[which], n)
        cnt[which] += n
        return cnt[which]

    dbg_used = [0]

    def dbg_dump(sy, src_ap, rows, cols):
        i = dbg_used[0]
        dbg_used[0] += 1
        if i in dbg_d:
            sy.dma_start(out=dbg_d[i][0:rows, 0:cols], in_=src_ap).then_inc(st.sem, 16)
            st.n += 16

    with contextlib.ExitStack() as p0:
        sb0 = lambda n, s, d=F32: p0.enter_context(nc.sbuf_tensor(n, s, d))
        stage = sb0("stage0", [128, 3 * C])
        cb2 = sb0("cb2", [BPC, C])
        scfm = sb0("scfm", [128, 2 * CI])
        modk = sb0("modk", [BPC, 6 * C])
        vr = sb0("vr", [BPC, 6 * C])
        vt1 = sb0("vt1", [BPC, C])
        vt2 = sb0("vt2", [BPC, C])
        psA = p0.enter_context(nc.psum_tensor("psA0", [128, 1536], F32))
        psT = p0.enter_context(nc.psum_tensor("psT0", [128, 1024], F32))

        with nc.Block() as blk:
            @blk.sync
            def _(sy):
                for buf, src in ((id32, ident32_c), (idbf, identbf_c),
                                 (maskb, mask_c), (cb2, c_d)):
                    ld.inc(sy.dma_start(out=buf[:], in_=src))
                sy.wait_ge(ld.sem, ld.n)

            @blk.vector
            def _(v):
                v = EngWrap(v)
                v.memset(epsb[:], EPS)
                v.memset(swb[:], 0.0)

        with nc.Block() as blk:
            @blk.tensor
            def _(te):
                ins = None
                for kc in range(CI):
                    ins = te.transpose(psT[:, 2 * kc:2 * kc + 2],
                                       cb2[:, 128 * kc:128 * (kc + 1)],
                                       id32[0:BPC, 0:BPC])
                mk(ins, "pe")

            @blk.scalar
            def _(sc):
                sc = EngWrap(sc)
                sc.wait_ge(sem_pe, cnt["pe"])
                sc.activation(out=scfm[:], in_=psT[:, 0:2 * CI], func=AF.Silu)

        for qt in range(4):
            pe0 = cnt["pe"]
            par = (lda, ldb)
            par0 = [lda.n, ldb.n]
            with nc.Block() as blk:
                @blk.sync
                def _(sy, qt=qt, pe0=pe0):
                    for kc in range(CI):
                        half = kc % 2
                        if kc >= 2:
                            sy.wait_ge(sem_pe, pe0 + kc - 1)
                        par[half].inc(sy.dma_start(
                            out=stage[:, 1536 * half:1536 * (half + 1)],
                            in_=wmod_d[128 * kc:128 * (kc + 1),
                                       1536 * qt:1536 * (qt + 1)]))

                @blk.tensor
                def _(te, qt=qt, par0=par0):
                    base = {0: par0[0], 1: par0[1]}
                    for kc in range(CI):
                        half = kc % 2
                        base[half] += 16
                        te.wait_ge(par[half].sem, base[half])
                        ins = None
                        for nn in range(3):
                            ins = te.matmul(
                                psA[0:BPC, 512 * nn:512 * (nn + 1)],
                                scfm[:, 2 * kc:2 * kc + 2],
                                stage[:, 1536 * half + 512 * nn:1536 * half + 512 * (nn + 1)],
                                start=(kc == 0), stop=(kc == CI - 1))
                        mk(ins, "pe")

                @blk.vector
                def _(v, qt=qt):
                    v = EngWrap(v)
                    v.wait_ge(sem_pe, cnt["pe"])
                    v.tensor_copy(modk[:, 1536 * qt:1536 * (qt + 1)], psA[0:BPC, :])

        with nc.Block() as blk:
            @blk.vector
            def _(v):
                v = EngWrap(v)
                P2 = slice(0, BPC)
                sa = modk[P2, 0:C]
                ca = modk[P2, C:2 * C]
                ga = modk[P2, 2 * C:3 * C]
                sbv = modk[P2, 3 * C:4 * C]
                cbv = modk[P2, 4 * C:5 * C]
                gbv = modk[P2, 5 * C:6 * C]
                v.tensor_scalar(vr[P2, 0:C], ca, 1.0, None, op0=ALU.add)
                v.tensor_copy(vr[P2, C:2 * C], sa)
                v.tensor_scalar(vr[P2, 3 * C:4 * C], cbv, 1.0, None, op0=ALU.add)
                v.tensor_copy(vr[P2, 4 * C:5 * C], sbv)
                for gsrc, gdst in ((ga, vr[P2, 2 * C:3 * C]),
                                   (gbv, vr[P2, 5 * C:6 * C])):
                    v.tensor_scalar(vt1[P2, :], gsrc, 0.0, None, op0=ALU.is_ge)
                    v.tensor_scalar(vt1[P2, :], vt1[P2, :], 2.0, 1.0,
                                    op0=ALU.mult, op1=ALU.subtract)
                    v.tensor_scalar(vt2[P2, :], gsrc, -1.0, None, op0=ALU.mult)
                    v.tensor_max(vt2[P2, :], vt2[P2, :], gsrc)
                    v.tensor_scalar(vt2[P2, :], vt2[P2, :], 0.25, None, op0=ALU.max)
                    ins = v.tensor_mul(gdst, vt1[P2, :], vt2[P2, :])
                mk(ins, "v")

            @blk.sync
            def _(sy):
                sy.wait_ge(sem_v, cnt["v"])
                st.inc(sy.dma_start(out=modv_s[:, :], in_=vr[:]))
                st.inc(sy.dma_start(out=sw_s, in_=swb[0:1, 0:8]))
                sy.wait_ge(st.sem, st.n)

    with contextlib.ExitStack() as pa:
        sba = lambda n, s, d=F32: pa.enter_context(nc.sbuf_tensor(n, s, d))
        wqkv = sba("wqkv_sb", [128, CI * 3 * C])
        wo = sba("wo_sb", [128, CI * C])
        qfm = sba("qfm", [128, C])
        kfm = sba("kfm", [128, C])
        khalo = sba("khalo", [128, 512])
        vhalo = sba("vhalo", [64, C])
        smb = sba("smb", [128, 192])
        pbuf = sba("pbuf", [128, 192])
        ptb = sba("ptb", [128, 256])
        bvb = sba("bvb", [128, C])
        qkb_f = sba("qkb_f", [128, 16])
        bc3 = pa.enter_context(nc.sbuf_tensor("bc3a", [128, C], F32))
        psA = pa.enter_context(nc.psum_tensor("psAa", [128, 1536], F32))
        psT = pa.enter_context(nc.psum_tensor("psTa", [128, 1024], F32))
        psS = pa.enter_context(nc.psum_tensor("psSa", [128, 512], F32))
        psO = pa.enter_context(nc.psum_tensor("psOa", [128, 1024], F32))
        pa_setup = contextlib.ExitStack()
        g1f = pa_setup.enter_context(nc.sbuf_tensor("g1f", [128, CI], F32))
        b1hf = pa_setup.enter_context(nc.sbuf_tensor("b1hf", [128, CI], F32))
        rwa = qkvk[0:1, :]

        with nc.Block() as blk:
            @blk.sync
            def _(sy):
                for kc in range(CI):
                    ld.inc(sy.dma_start(out=wqkv[:, 3072 * kc:3072 * (kc + 1)],
                                        in_=wqkv_d[128 * kc:128 * (kc + 1), :]))
                for kc in range(CI):
                    ld.inc(sy.dma_start(out=wo[:, 1024 * kc:1024 * (kc + 1)],
                                        in_=wout_d[128 * kc:128 * (kc + 1), :]))
                ld.inc(sy.dma_start(out=g1f[:],
                                    in_=bass.AP(ln1g_h, 0, [[1, 128], [128, CI]]),
                                    allow_slow_non_contiguous=True))
                ld.inc(sy.dma_start(out=b1hf[:],
                                    in_=bass.AP(ln1b_h, 0, [[1, 128], [128, CI]]),
                                    allow_slow_non_contiguous=True))
                sy.wait_ge(ld.sem, ld.n)

        pe0 = cnt["pe"]
        v0 = cnt["v"]
        W0 = {}
        with nc.Block() as blk:
            @blk.tensor
            def _(te):
                for half in range(2):
                    if half == 1:
                        te.wait_ge(sem_v, v0 + 1)
                    ins = None
                    for kc in range(CI):
                        for nn in range(3):
                            co = 3072 * kc + 1536 * half + 512 * nn
                            ins = te.matmul(psA[0:1, 512 * nn:512 * (nn + 1)],
                                            b1hf[:, kc:kc + 1],
                                            wqkv[:, co:co + 512],
                                            start=(kc == 0), stop=(kc == CI - 1))
                    mk(ins, "pe")

            @blk.sync
            def _(sy):
                for pp in range(3):
                    ld.inc(sy.dma_start(out=fmbuf[0:1, 0:1024],
                                        in_=bqkv_d[0:1, 1024 * pp:1024 * (pp + 1)]))
                    W0[f"bq{pp}"] = ld.n
                    sy.wait_ge(sem_v, v0 + 2 + pp + 1)

            @blk.vector
            def _(v):
                v = EngWrap(v)
                v.wait_ge(sem_pe, pe0 + 1)
                mk(v.tensor_copy(rwa[:, 0:1536], psA[0:1, :]), "v")
                v.wait_ge(sem_pe, pe0 + 2)
                mk(v.tensor_copy(rwa[:, 1536:3072], psA[0:1, :]), "v")
                for pp in range(3):
                    v.wait_ge(ld.sem, W0[f"bq{pp}"])
                    mk(v.tensor_add(rwa[:, 1024 * pp:1024 * (pp + 1)],
                                    rwa[:, 1024 * pp:1024 * (pp + 1)],
                                    fmbuf[0:1, 0:1024]), "v")

            @blk.sync
            def _(sy):
                sy.wait_ge(sem_v, cnt["v"])
                st.inc(sy.dma_start(out=qb_s, in_=rwa[:]))
                sy.wait_ge(st.sem, st.n)

        with nc.Block() as blk:
            @blk.vector
            def _(v):
                v = EngWrap(v)
                for kc in range(CI):
                    v.tensor_scalar(wqkv[:, 3072 * kc:3072 * (kc + 1)],
                                    wqkv[:, 3072 * kc:3072 * (kc + 1)],
                                    g1f[:, kc:kc + 1], None, op0=ALU.mult)

            @blk.sync
            def _(sy):
                ld.inc(sy.dma_start(out=bvb[:],
                                    in_=qb_s[0:1, 2048:3072].broadcast_to([128, C])))
                ld.inc(sy.dma_start(out=qkb_f[:],
                                    in_=bass.AP(qb_h, 0, [[1, 128], [128, 16]]),
                                    allow_slow_non_contiguous=True))
                ld.inc(sy.dma_start(out=bc3[:], in_=bout_d.broadcast_to([128, C])))
                for i, buf in enumerate((bc0, bc1, bc2)):
                    ld.inc(sy.dma_start(
                        out=buf[:],
                        in_=modv_s[0:1, C * i:C * (i + 1)].broadcast_to([128, C])))
                sy.wait_ge(ld.sem, ld.n)

        pa_setup.close()

        for t in range(TILES):
            seq_start = (t % TPB) == 0
            if t == TPB:
                with nc.Block() as blk:
                    @blk.sync
                    def _(sy):
                        for i, buf in enumerate((bc0, bc1, bc2)):
                            ld.inc(sy.dma_start(
                                out=buf[:],
                                in_=modv_s[1:2, C * i:C * (i + 1)].broadcast_to([128, C])))
                        sy.wait_ge(ld.sem, ld.n)

            with nc.Block() as blk:
                W = {}

                v_names = (["var1", "h", "var2", "h1", "tea", "teb", "qkve1",
                            "qkve2", "qktea", "qkteb", "qktec", "qkted", "ms0"]
                           + [f"{nm}{h}" for h in range(HEADS)
                              for nm in ("mask", "pte", "oe")]
                           + ["otea", "oteb", "a_done", "p_done", "khal"])
                v_base = cnt["v"]
                v_pos = {nm: v_base + i + 1 for i, nm in enumerate(v_names)}

                pe_names = (["h1Ta", "h1Tb", "qkv1", "qkv2", "qkTa", "qkTb",
                             "qkTc", "qkTd"]
                            + [f"{nm}{h}" for h in range(HEADS)
                               for nm in ("sc", "pt", "pv")]
                            + ["oTa", "oTb", "oproj"])
                pe_base = cnt["pe"]
                pe_pos = {nm: pe_base + i + 1 for i, nm in enumerate(pe_names)}

                @blk.sync
                def _(sy, t=t):
                    ld.inc(sy.dma_start(out=xb[:], in_=x_d[NT * t:NT * (t + 1), :]))
                    W["ld_x"] = ld.n

                @blk.scalar
                def _(sc, t=t, v_pos=v_pos, pe_pos=pe_pos):
                    sc = EngWrap(sc)
                    sc.wait_ge(ld.sem, W["ld_x"])
                    W["sq1"] = mk(sc.activation(out=fmbuf[:], in_=xb[:], func=AF.Square,
                                                accum_out=s2[:]), "sc")
                    sc.wait_ge(sem_v, v_pos["var1"])
                    W["sd1"] = mk(sc.activation(out=sdv[:], in_=varv[:], func=AF.Sqrt,
                                                bias=epsb[:]), "sc")
                    sc.wait_ge(sem_v, v_pos["h"])
                    W["sq2"] = mk(sc.activation(out=fmbuf[:], in_=hb[:], func=AF.Square,
                                                accum_out=s2[:]), "sc")
                    sc.wait_ge(sem_v, v_pos["var2"])
                    W["sd2"] = mk(sc.activation(out=sdv[:], in_=varv[:], func=AF.Sqrt,
                                                bias=epsb[:]), "sc")
                    W["exp"] = []
                    for h in range(HEADS):
                        sc.wait_ge(sem_v, v_pos[f"mask{h}"])
                        if h >= 1:
                            sc.wait_ge(sem_pe, pe_pos[f"pt{h - 1}"])
                        W["exp"].append(
                            mk(sc.activation(out=pbuf[:, 0:192],
                                             in_=smb[:, 0:192],
                                             func=AF.Exp, scale=0.125,
                                             bias=ng16[:, h:h + 1],
                                             accum_out=rs16[:, h:h + 1]), "sc"))

                @blk.vector
                def _(v, t=t, seq_start=seq_start, v_pos=v_pos, pe_pos=pe_pos):
                    v = EngWrap(v)
                    v.wait_ge(ld.sem, W["ld_x"])
                    v.reduce_sum(s1[:], xb[:], axis=AX)
                    v.wait_ge(sem_sc, W["sq1"])
                    v.tensor_scalar(e2v[:], s2[:], INV_C, None, op0=ALU.mult)
                    v.tensor_scalar(mvec[:], s1[:], INV_C, None, op0=ALU.mult)
                    v.tensor_mul(varv[:], mvec[:], mvec[:])
                    mk(v.tensor_sub(varv[:], e2v[:], varv[:]), "v")
                    v.wait_ge(sem_sc, W["sd1"])
                    v.reciprocal(rstd[:], sdv[:])
                    v.tensor_mul(mrv[:], mvec[:], rstd[:])
                    v.tensor_scalar(tmp1[:], xb[:], rstd[:], mrv[:],
                                    op0=ALU.mult, op1=ALU.subtract)
                    v.tensor_mul(hb[:], tmp1[:], bc0[:])
                    mk(v.tensor_add(hb[:], hb[:], bc1[:]), "v")
                    v.reduce_sum(s1[:], hb[:], axis=AX)
                    v.wait_ge(sem_sc, W["sq2"])
                    v.tensor_scalar(e2v[:], s2[:], INV_C, None, op0=ALU.mult)
                    v.tensor_scalar(mvec[:], s1[:], INV_C, None, op0=ALU.mult)
                    v.tensor_mul(varv[:], mvec[:], mvec[:])
                    mk(v.tensor_sub(varv[:], e2v[:], varv[:]), "v")
                    v.wait_ge(sem_sc, W["sd2"])
                    v.reciprocal(rstd1[:], sdv[:])
                    v.tensor_mul(mrv1[:], mvec[:], rstd1[:])
                    mk(v.tensor_scalar(tmp1[:], hb[:], rstd1[:], mrv1[:],
                                       op0=ALU.mult, op1=ALU.subtract), "v")
                    v.wait_ge(sem_pe, pe_pos["h1Ta"])
                    mk(v.tensor_copy(fmbuf[:, 0:512], psT[:, 0:512]), "v")
                    v.wait_ge(sem_pe, pe_pos["h1Tb"])
                    mk(v.tensor_copy(fmbuf[:, 512:1024], psT[:, 512:1024]), "v")
                    v.wait_ge(sem_pe, pe_pos["qkv1"])
                    mk(v.tensor_copy(qkvk[:, 0:1536], psA[:]), "v")
                    v.wait_ge(sem_pe, pe_pos["qkv2"])
                    v.tensor_copy(qkvk[:, 1536:2048], psA[:, 0:512])
                    mk(v.scalar_tensor_tensor(qkvk[:, 2048:3072], psA[:, 512:1536],
                                              0.0, bvb[:], op0=ALU.add,
                                              op1=ALU.add), "v")
                    for g in range(4):
                        v.wait_ge(sem_pe, pe_pos["qkT" + "abcd"[g]])
                        dst = qfm if g < 2 else kfm
                        off = 512 * (g % 2)
                        ins = None
                        for i in range(4):
                            cc = 4 * (g % 2) + i
                            bcol = cc if g < 2 else 8 + cc
                            ins = v.tensor_scalar(dst[:, off + 128 * i:off + 128 * (i + 1)],
                                                  psT[:, off + 128 * i:off + 128 * (i + 1)],
                                                  qkb_f[:, bcol:bcol + 1], None,
                                                  op0=ALU.add)
                        mk(ins, "v")
                    if seq_start:
                        mk(v.memset(psS[:, 0:64], 0.0), "v")
                    else:
                        mk(v.memset(s1[0:1, 0:1], 0.0), "v")
                    for h in range(HEADS):
                        v.wait_ge(sem_pe, pe_pos[f"sc{h}"])
                        v.tensor_add(smb[:], psS[:, 0:192], maskb[:])
                        v.reduce_max(mx16[:, h:h + 1], smb[:], axis=AX)
                        mk(v.tensor_scalar(ng16[:, h:h + 1], mx16[:, h:h + 1],
                                           -0.125, None, op0=ALU.mult), "v")
                        v.wait_ge(sem_pe, pe_pos[f"pt{h}"])
                        bnk = 512 * (h % 2)
                        v.tensor_copy(ptb[0:64, 128:256],
                                      psT[0:64, bnk + 128:bnk + 256])
                        mk(v.tensor_copy(ptb[:, 0:128],
                                         psT[:, bnk:bnk + 128]), "v")
                        v.wait_ge(sem_sc, W["exp"][h])
                        v.reciprocal(rr16[:, h:h + 1], rs16[:, h:h + 1])
                        v.wait_ge(sem_pe, pe_pos[f"pv{h}"])
                        ob = 512 * (h % 2) + 64 * (h % 4)
                        mk(v.tensor_scalar(okm[:, 64 * h:64 * (h + 1)],
                                           psO[:, ob:ob + 64],
                                           rr16[:, h:h + 1], None, op0=ALU.mult), "v")
                    v.wait_ge(sem_pe, pe_pos["oTa"])
                    mk(v.tensor_copy(fmbuf[:, 0:512], psT[:, 0:512]), "v")
                    v.wait_ge(sem_pe, pe_pos["oTb"])
                    mk(v.tensor_copy(fmbuf[:, 512:1024], psT[:, 512:1024]), "v")
                    v.wait_ge(sem_pe, pe_pos["oproj"])
                    v.scalar_tensor_tensor(ab[:], psA[:, 0:1024], 0.0, hb[:],
                                           op0=ALU.add, op1=ALU.add)
                    mk(v.tensor_add(ab[:], ab[:], bc3[:]), "v")
                    v.tensor_mul(tmp1[:], ab[:], bc2[:])
                    mk(v.tensor_add(tmp1[:], tmp1[:], xb[:]), "v")
                    v.wait_ge(sem_pe, pe_pos["sc15"])
                    src = kfm[:].rearrange("p (c k) -> p c k", c=CI)[:, :, 64:128]
                    dst = khalo[:].rearrange("p (c k) -> p c k", c=CI)
                    mk(v.tensor_copy(dst, src), "v")

                @blk.tensor
                def _(te, t=t, seq_start=seq_start, v_pos=v_pos, pe_pos=pe_pos):
                    te.wait_ge(sem_v, v_pos["h1"])
                    for g in range(2):
                        ins = None
                        for i in range(4):
                            cc = 4 * g + i
                            ins = te.transpose(
                                psT[:, 512 * g + 128 * i:512 * g + 128 * (i + 1)],
                                tmp1[:, 128 * cc:128 * (cc + 1)], id32[:])
                        mk(ins, "pe")
                    te.wait_ge(sem_v, v_pos["teb"])
                    for half in range(2):
                        if half == 1:
                            te.wait_ge(sem_v, v_pos["qkve1"])
                        ins = None
                        for kc in range(CI):
                            for nn in range(3):
                                co = 3072 * kc + 1536 * half + 512 * nn
                                ins = te.matmul(psA[:, 512 * nn:512 * (nn + 1)],
                                                fmbuf[:, 128 * kc:128 * (kc + 1)],
                                                wqkv[:, co:co + 512],
                                                start=(kc == 0), stop=(kc == CI - 1))
                        mk(ins, "pe")
                    for g in range(4):
                        te.wait_ge(sem_v, v_pos["qkve1" if g < 2 else "qkve2"])
                        if g >= 2:
                            te.wait_ge(sem_v, v_pos["qktea" if g == 2 else "qkteb"])
                        qk = 0 if g < 2 else 1
                        ins = None
                        for i in range(4):
                            cc = 4 * (g % 2) + i
                            src_col = 1024 * qk + 128 * cc
                            ins = te.transpose(
                                psT[:, 512 * (g % 2) + 128 * i:512 * (g % 2) + 128 * (i + 1)],
                                qkvk[:, src_col:src_col + 128], id32[:])
                        mk(ins, "pe")
                    for h in range(HEADS):
                        r0 = 64 * (h % 2)
                        qc = h // 2
                        te.wait_ge(sem_v, v_pos["qkted"])
                        te.wait_ge(sem_v, v_pos["ms0"] if h == 0 else v_pos[f"mask{h - 1}"])
                        if not seq_start:
                            te.matmul(psS[:, 0:64],
                                      qfm[r0:r0 + 64, 128 * qc:128 * (qc + 1)],
                                      khalo[r0:r0 + 64, 64 * qc:64 * (qc + 1)],
                                      start=True, stop=True)
                        mk(te.matmul(psS[:, 64:192],
                                     qfm[r0:r0 + 64, 128 * qc:128 * (qc + 1)],
                                     kfm[r0:r0 + 64, 128 * qc:128 * (qc + 1)],
                                     start=True, stop=True), "pe")
                        te.wait_ge(sem_sc, W["exp"][h])
                        if h >= 2:
                            te.wait_ge(sem_v, v_pos[f"pte{h - 2}"])
                        bnk = 512 * (h % 2)
                        te.transpose(psT[0:64, bnk + 128:bnk + 256],
                                     pbuf[:, 0:64], id32[:])
                        mk(te.transpose(psT[:, bnk:bnk + 128],
                                        pbuf[:, 64:192], id32[:]), "pe")
                        te.wait_ge(sem_v, v_pos[f"pte{h}"])
                        if h >= 2:
                            te.wait_ge(sem_v, v_pos[f"oe{h - 2}"])
                        ob = 512 * (h % 2) + 64 * (h % 4)
                        if not seq_start:
                            te.matmul(psO[:, ob:ob + 64],
                                      ptb[0:64, 128:256],
                                      vhalo[0:64, 64 * h:64 * (h + 1)],
                                      start=True, stop=False)
                        mk(te.matmul(psO[:, ob:ob + 64],
                                     ptb[:, 0:128],
                                     qkvk[:, 2048 + 64 * h:2048 + 64 * (h + 1)],
                                     start=seq_start, stop=True), "pe")
                    te.wait_ge(sem_v, v_pos["oe15"])
                    for g in range(2):
                        ins = None
                        for i in range(4):
                            cc = 4 * g + i
                            ins = te.transpose(
                                psT[:, 512 * g + 128 * i:512 * g + 128 * (i + 1)],
                                okm[:, 128 * cc:128 * (cc + 1)], id32[:])
                        mk(ins, "pe")
                    te.wait_ge(sem_v, v_pos["oteb"])
                    ins = None
                    for kc in range(CI):
                        for nn in range(2):
                            co = 1024 * kc + 512 * nn
                            ins = te.matmul(psA[:, 512 * nn:512 * (nn + 1)],
                                            fmbuf[:, 128 * kc:128 * (kc + 1)],
                                            wo[:, co:co + 512],
                                            start=(kc == 0), stop=(kc == CI - 1))
                    mk(ins, "pe")

                @blk.sync
                def _(sy, t=t, v_pos=v_pos, pe_pos=pe_pos):
                    sy.wait_ge(sem_v, v_pos["qkve2"])
                    sy.wait_ge(sem_pe, pe_pos["pv15"])
                    hal.inc(sy.dma_start(out=vhalo[:], in_=qkvk[64:128, 2048:3072]))
                    sy.wait_ge(sem_v, v_pos["a_done"])
                    st.inc(sy.dma_start(out=a_s[NT * t:NT * (t + 1), :], in_=ab[:]))
                    sy.wait_ge(sem_v, v_pos["p_done"])
                    st.inc(sy.dma_start(out=p_s[NT * t:NT * (t + 1), :], in_=tmp1[:]))
                    if debug and t in (0, 17):
                        sy.wait_ge(sem_v, v_pos["khal"])
                        dbg_dump(sy, hb[:], 128, C)
                        dbg_dump(sy, qkvk[:], 128, 3 * C)
                        dbg_dump(sy, ab[:], 128, C)
                    sy.wait_ge(st.sem, st.n)
                    sy.wait_ge(hal.sem, hal.n)

                assert cnt["v"] == v_base + len(v_names), (cnt["v"], v_base, len(v_names))
                assert cnt["pe"] == pe_base + len(pe_names), (cnt["pe"], pe_base)

    with contextlib.ExitStack() as pb:
        sbb = lambda n, s, d=F32: pb.enter_context(nc.sbuf_tensor(n, s, d))
        w1j = sbb("w1j", [128, 2 * IN4])
        w2j = sbb("w2j", [128, 2 * IN4])
        h2q = sbb("h2q", [128, 8 * C])
        mslab = sbb("mslab", [128, 8 * C])
        gq = sbb("gq", [128, 2048])
        b1f = sbb("b1f", [128, IN4 // 128])
        bc3 = pb.enter_context(nc.sbuf_tensor("bc3b", [128, C], F32))
        psA = pb.enter_context(nc.psum_tensor("psAb", [128, 1536], F32))
        psT = pb.enter_context(nc.psum_tensor("psTb", [128, 1024], F32))
        psS = pb.enter_context(nc.psum_tensor("psSb32", [128, 512], F32))

        par = (lda, ldb)

        def issue_wj(sy, jg):
            p_ = jg % 2
            j_ = jg % 8
            for kc in range(CI):
                par[p_].inc(sy.dma_start(
                    out=w1j[:, 4096 * p_ + 512 * kc:4096 * p_ + 512 * (kc + 1)],
                    in_=w1_d[128 * kc:128 * (kc + 1), 512 * j_:512 * (j_ + 1)]))
            for s_ in range(4):
                par[p_].inc(sy.dma_start(
                    out=w2j[:, 4096 * p_ + 1024 * s_:4096 * p_ + 1024 * (s_ + 1)],
                    in_=w2_d[512 * j_ + 128 * s_:512 * j_ + 128 * (s_ + 1), :]))
            return par[p_].n

        wj_ready = {}

        with nc.Block() as blk:
            @blk.sync
            def _(sy):
                wj_ready[0] = issue_wj(sy, 0)
                ld.inc(sy.dma_start(out=bc0[:], in_=ln2g_d.broadcast_to([128, C])))
                ld.inc(sy.dma_start(out=bc1[:], in_=ln2b_d.broadcast_to([128, C])))
                ld.inc(sy.dma_start(out=bc3[:], in_=b2_d.broadcast_to([128, C])))
                ld.inc(sy.dma_start(
                    out=bc2[:], in_=modv_s[0:1, 2 * C:3 * C].broadcast_to([128, C])))
                ld.inc(sy.dma_start(out=b1f[:],
                                    in_=bass.AP(b1_h, 0, [[1, 128], [128, IN4 // 128]]),
                                    allow_slow_non_contiguous=True))
                sy.wait_ge(ld.sem, ld.n)

        for q in range(4):
            if q == 2:
                with nc.Block() as blk:
                    @blk.sync
                    def _(sy):
                        ld.inc(sy.dma_start(
                            out=bc2[:],
                            in_=modv_s[1:2, 2 * C:3 * C].broadcast_to([128, C])))
                        sy.wait_ge(ld.sem, ld.n)

            for tt in range(8):
                tq = 8 * q + tt
                with nc.Block() as blk:
                    W = {}
                    wrapd = {}

                    @blk.sync
                    def _(sy, tq=tq):
                        ld.inc(sy.dma_start(out=ab[:], in_=a_s[NT * tq:NT * (tq + 1), :]))
                        W["ld"] = ld.n

                    @blk.scalar
                    def _(sc):
                        sc = wrapd.setdefault("sc", EngWrap(sc))
                        sc.wait_ge(ld.sem, W["ld"])
                        W["sq"] = mk(sc.activation(out=fmbuf[:], in_=ab[:],
                                                   func=AF.Square, accum_out=s2[:]), "sc")

                    @blk.vector
                    def _(v):
                        v = wrapd.setdefault("v", EngWrap(v))
                        v.wait_ge(ld.sem, W["ld"])
                        v.reduce_sum(s1[:], ab[:], axis=AX)
                        v.wait_ge(sem_sc, W["sq"])
                        v.tensor_scalar(e2v[:], s2[:], INV_C, None, op0=ALU.mult)
                        v.tensor_scalar(mvec[:], s1[:], INV_C, None, op0=ALU.mult)
                        v.tensor_mul(varv[:], mvec[:], mvec[:])
                        W["var"] = mk(v.tensor_sub(varv[:], e2v[:], varv[:]), "v")

                    @blk.scalar
                    def _(sc):
                        sc = wrapd["sc"]
                        sc.wait_ge(sem_v, W["var"])
                        W["sd"] = mk(sc.activation(out=sdv[:], in_=varv[:],
                                                   func=AF.Sqrt, bias=epsb[:]), "sc")

                    @blk.vector
                    def _(v):
                        v = wrapd["v"]
                        v.wait_ge(sem_sc, W["sd"])
                        v.reciprocal(rstd[:], sdv[:])
                        v.tensor_mul(mrv[:], mvec[:], rstd[:])
                        v.tensor_scalar(tmp1[:], ab[:], rstd[:], mrv[:],
                                        op0=ALU.mult, op1=ALU.subtract)
                        v.tensor_mul(hb[:], tmp1[:], bc0[:])
                        W["h2"] = mk(v.tensor_add(hb[:], hb[:], bc1[:]), "v")

                    @blk.tensor
                    def _(te):
                        te.wait_ge(sem_v, W["h2"])
                        W["T"] = []
                        for g_ in range(2):
                            ins = None
                            for i in range(4):
                                cc = 4 * g_ + i
                                ins = te.transpose(
                                    psT[:, 512 * g_ + 128 * i:512 * g_ + 128 * (i + 1)],
                                    hb[:, 128 * cc:128 * (cc + 1)], id32[:])
                            W["T"].append(mk(ins, "pe"))

                    @blk.vector
                    def _(v, tt=tt):
                        v = wrapd["v"]
                        h2r = h2q[:].rearrange("p (c t) -> p c t", c=CI)
                        psr = psT[:].rearrange("p (c t) -> p c t", c=CI)
                        for g_ in range(2):
                            v.wait_ge(sem_pe, W["T"][g_])
                            mk(v.tensor_copy(
                                h2r[:, 4 * g_:4 * (g_ + 1), 128 * tt:128 * (tt + 1)],
                                psr[:, 4 * g_:4 * (g_ + 1), :]), "v")

            for j in range(8):
                jg = 8 * q + j
                p_ = jg % 2
                pe_b = cnt["pe"]
                sc_b = cnt["sc"]
                v_b = cnt["v"]
                pe_g = lambda st_, m, pe_b=pe_b: pe_b + st_ * 12 + m + 1
                pe_w2 = lambda st_, t4, hf_, pe_b=pe_b: pe_b + st_ * 12 + 4 + t4 * 2 + hf_ + 1
                sc_gelu = lambda st_, m, sc_b=sc_b: sc_b + st_ * 4 + m + 1
                v_ev = lambda k, v_b=v_b: v_b + k + 1

                with nc.Block() as blk:
                    @blk.sync
                    def _(sy, jg=jg):
                        if jg + 1 < 32:
                            wj_ready[jg + 1] = issue_wj(sy, jg + 1)

                    @blk.tensor
                    def _(te, j=j, jg=jg, p_=p_, pe_g=pe_g, pe_w2=pe_w2,
                          sc_gelu=sc_gelu, v_ev=v_ev):
                        te.wait_ge(par[p_].sem, wj_ready[jg])
                        for st_ in range(2):
                            for m in range(4):
                                if st_ == 1:
                                    te.wait_ge(sem_sc, sc_gelu(0, m))
                                dst = (psA[:, 512 * m:512 * (m + 1)] if m < 3
                                       else psS[:, 0:512])
                                ins = None
                                for kc in range(CI):
                                    ins = te.matmul(
                                        dst,
                                        w1j[:, 4096 * p_ + 512 * kc + 128 * m:
                                            4096 * p_ + 512 * kc + 128 * (m + 1)],
                                        h2q[:, 1024 * kc + 512 * st_:
                                            1024 * kc + 512 * (st_ + 1)],
                                        start=(kc == 0), stop=(kc == CI - 1))
                                mk(ins, "pe")
                            te.wait_ge(sem_sc, sc_gelu(st_, 3))
                            for t4 in range(4):
                                for hf_ in range(2):
                                    k = st_ * 8 + t4 * 2 + hf_
                                    if k >= 2:
                                        te.wait_ge(sem_v, v_ev(k - 2))
                                    ins = None
                                    for m in range(4):
                                        ins = te.matmul(
                                            psT[:, 512 * hf_:512 * (hf_ + 1)],
                                            gq[:, 512 * m + 128 * t4:
                                               512 * m + 128 * (t4 + 1)],
                                            w2j[:, 4096 * p_ + 1024 * m + 512 * hf_:
                                                4096 * p_ + 1024 * m + 512 * hf_ + 512],
                                            start=(m == 0), stop=(m == 3))
                                    mk(ins, "pe")

                    @blk.scalar
                    def _(sc, j=j, pe_g=pe_g, pe_w2=pe_w2):
                        sc = EngWrap(sc)
                        for st_ in range(2):
                            if st_ == 1:
                                sc.wait_ge(sem_pe, pe_w2(0, 3, 1))
                            for m in range(4):
                                sc.wait_ge(sem_pe, pe_g(st_, m))
                                src_ = (psA[:, 512 * m:512 * (m + 1)] if m < 3
                                        else psS[:, 0:512])
                                mk(sc.activation(out=gq[:, 512 * m:512 * (m + 1)],
                                                 in_=src_, func=AF.Gelu,
                                                 bias=b1f[:, 4 * j + m:4 * j + m + 1]),
                                   "sc")

                    @blk.vector
                    def _(v, j=j, pe_w2=pe_w2):
                        v = EngWrap(v)
                        for st_ in range(2):
                            for t4 in range(4):
                                for hf_ in range(2):
                                    v.wait_ge(sem_pe, pe_w2(st_, t4, hf_))
                                    tt_ = 4 * st_ + t4
                                    dst = mslab[:, 1024 * tt_ + 512 * hf_:
                                                1024 * tt_ + 512 * hf_ + 512]
                                    if j == 0:
                                        ins = v.scalar_tensor_tensor(
                                            dst, psT[:, 512 * hf_:512 * (hf_ + 1)], 0.0,
                                            bc3[:, 512 * hf_:512 * (hf_ + 1)],
                                            op0=ALU.add, op1=ALU.add)
                                    else:
                                        ins = v.tensor_add(
                                            dst, dst, psT[:, 512 * hf_:512 * (hf_ + 1)])
                                    mk(ins, "v")

            for tt in range(8):
                tq = 8 * q + tt
                with nc.Block() as blk:
                    W = {}

                    @blk.sync
                    def _(sy, tq=tq):
                        ld.inc(sy.dma_start(out=xb[:], in_=p_s[NT * tq:NT * (tq + 1), :]))
                        W["ld"] = ld.n

                    @blk.vector
                    def _(v, tt=tt):
                        v = EngWrap(v)
                        v.wait_ge(ld.sem, W["ld"])
                        v.tensor_mul(tmp1[:], mslab[:, 1024 * tt:1024 * (tt + 1)], bc2[:])
                        W["x1"] = mk(v.tensor_add(tmp1[:], tmp1[:], xb[:]), "v")

                    @blk.sync
                    def _(sy, tq=tq):
                        sy.wait_ge(sem_v, W["x1"])
                        st.inc(sy.dma_start(out=x1_s[NT * tq:NT * (tq + 1), :], in_=tmp1[:]))
                        sy.wait_ge(st.sem, st.n)

    par = (lda, ldb)

    def w_mean_pass(w_dram, nch, width, col, stg, psR):
        v_b = cnt["v"]
        p0_ = [par[0].n, par[1].n]
        with nc.Block() as blk:
            @blk.sync
            def _(sy, nch=nch, width=width):
                for ci_ in range(nch):
                    pp = ci_ % 2
                    if ci_ >= 2:
                        sy.wait_ge(sem_v, v_b + ci_ - 1)
                    par[pp].inc(sy.dma_start(
                        out=stg[:, width * pp:width * (pp + 1)],
                        in_=w_dram[128 * ci_:128 * (ci_ + 1), :]))

            @blk.vector
            def _(v, nch=nch, width=width):
                v = EngWrap(v)
                base = list(p0_)
                for ci_ in range(nch):
                    pp = ci_ % 2
                    base[pp] += 16
                    v.wait_ge(par[pp].sem, base[pp])
                    v.tensor_reduce(s1[:], stg[:, width * pp:width * (pp + 1)],
                                    axis=AX, op=ALU.add, apply_absolute_value=True)
                    if ci_ == 0:
                        ins = v.tensor_copy(sc_4[:], s1[:])
                    else:
                        ins = v.tensor_add(sc_4[:], sc_4[:], s1[:])
                    mk(ins, "v")

        with nc.Block() as blk:
            @blk.tensor
            def _(te):
                mk(te.transpose(psR[0:1, 0:128], sc_4[:], id32[:]), "pe")

            @blk.vector
            def _(v, nch=nch, width=width):
                v = EngWrap(v)
                v.wait_ge(sem_pe, cnt["pe"])
                v.reduce_sum(s1[0:1, 0:1], psR[0:1, 0:128], axis=AX)
                v.tensor_scalar(s1[0:1, 0:1], s1[0:1, 0:1],
                                1.0 / (nch * 128 * width), 1e-5,
                                op0=ALU.mult, op1=ALU.max)
                v.reciprocal(s2[0:1, 0:1], s1[0:1, 0:1])
                mk(v.tensor_copy(mvec[0:1, 0:1], s1[0:1, 0:1]), "v")

            @blk.sync
            def _(sy, col=col):
                sy.wait_ge(sem_v, cnt["v"])
                st.inc(sy.dma_start(out=sw_s[0:1, col:col + 1], in_=mvec[0:1, 0:1]))
                st.inc(sy.dma_start(out=sw_s[0:1, col + 3:col + 4], in_=s2[0:1, 0:1]))
                sy.wait_ge(st.sem, st.n)
                ld.inc(sy.dma_start(out=swb[:],
                                    in_=sw_s[0:1, 0:8].broadcast_to([128, 8])))
                sy.wait_ge(ld.sem, ld.n)

    def w_tern_pass(w_dram, nch, width, scol, dst_of_piece, stg, tA, tB, tern, psB):
        ng = width // 1024
        v_per = 1 + ng
        v_b = cnt["v"]
        pe_b = cnt["pe"]
        p0_ = [par[0].n, par[1].n]

        def v_tern(ci_):
            return v_b + ci_ * v_per + 1

        def v_evac(ci_, g_):
            return v_b + ci_ * v_per + 1 + g_ + 1

        def pe_T(ci_, g_):
            return pe_b + ci_ * ng + g_ + 1

        with nc.Block() as blk:
            @blk.sync
            def _(sy, nch=nch, width=width):
                for ci_ in range(nch):
                    pp = ci_ % 2
                    if ci_ >= 2:
                        sy.wait_ge(sem_v, v_tern(ci_ - 2))
                    par[pp].inc(sy.dma_start(
                        out=stg[:, width * pp:width * (pp + 1)],
                        in_=w_dram[128 * ci_:128 * (ci_ + 1), :]))

            @blk.vector
            def _(v, nch=nch, width=width, scol=scol):
                v = EngWrap(v)
                base = list(p0_)
                for ci_ in range(nch):
                    pp = ci_ % 2
                    base[pp] += 16
                    v.wait_ge(par[pp].sem, base[pp])
                    sl = stg[:, width * pp:width * (pp + 1)]
                    v.tensor_scalar(tA[:, 0:width], sl, swb[:, scol + 3:scol + 4],
                                    0.5, op0=ALU.mult, op1=ALU.is_gt)
                    v.tensor_scalar(tB[:, 0:width], sl, swb[:, scol + 3:scol + 4],
                                    -0.5, op0=ALU.mult, op1=ALU.is_lt)
                    if ci_ >= 1:
                        v.wait_ge(sem_pe, pe_T(ci_ - 1, ng - 1))
                    mk(v.tensor_sub(tern[:, 0:width], tA[:, 0:width],
                                    tB[:, 0:width]), "v")
                    for g_ in range(ng):
                        v.wait_ge(sem_pe, pe_T(ci_, g_))
                        src_ = psB[:, 0:1024].rearrange("p (i t) -> p i t", i=8)
                        mk(v.tensor_copy(dst_of_piece(ci_, g_), src_), "v")

            @blk.tensor
            def _(te, nch=nch, width=width):
                for ci_ in range(nch):
                    te.wait_ge(sem_v, v_tern(ci_))
                    for g_ in range(ng):
                        if g_ >= 1:
                            te.wait_ge(sem_v, v_evac(ci_, g_ - 1))
                        ins = None
                        for i in range(8):
                            pi = 8 * g_ + i
                            ins = te.transpose(
                                psB[:, 128 * i:128 * (i + 1)],
                                tern[:, 128 * pi:128 * (pi + 1)], idbf[:])
                        mk(ins, "pe")

    def ladder_v(v, src, width, s_ap, r_out):
        w2_ = width // 2
        v.tensor_scalar(okm[:, 0:width], src, s_ap, 0.5, op0=ALU.mult, op1=ALU.add)
        bmk = mk(v.tensor_scalar(tmp1[:, 0:width], okm[:, 0:width], 128.0, None,
                                 op0=ALU.add), "v")
        for k in (128, 64, 32, 16, 8, 4, 2, 1):
            v.tensor_scalar(fmbuf[:, 0:w2_], tmp1[:, 0:w2_], float(k), -float(k),
                            op0=ALU.is_ge, op1=ALU.mult)
            v.tensor_add(tmp1[:, 0:w2_], tmp1[:, 0:w2_], fmbuf[:, 0:w2_])
        rmk = mk(v.tensor_sub(r_out[:, 0:w2_], okm[:, 0:w2_], tmp1[:, 0:w2_]), "v")
        return bmk, rmk

    def ladder_g(g, width, r_out, bmk):
        w2_ = width // 2
        g.wait_ge(sem_v, bmk)
        for k in (128, 64, 32, 16, 8, 4, 2, 1):
            g.tensor_scalar(fmbuf[:, w2_:width], tmp1[:, w2_:width], float(k),
                            -float(k), op0=ALU.is_ge, op1=ALU.mult)
            g.tensor_add(tmp1[:, w2_:width], tmp1[:, w2_:width],
                         fmbuf[:, w2_:width])
        return mk(g.tensor_sub(r_out[:, w2_:width], okm[:, w2_:width],
                               tmp1[:, w2_:width]), "gp")

    with contextlib.ExitStack() as pc:
        sbc = lambda n, s, d=F32: pc.enter_context(nc.sbuf_tensor(n, s, d))
        wgf = sbc("wgf", [128, CI * IN4], BF16)
        wuf = sbc("wuf", [128, CI * IN4], BF16)
        gu = sbc("gu", [128, IN4])
        rhf_kmb = sbc("rhf_kmb", [128, C], BF16)
        rhf_fmb = sbc("rhf_fmb", [128, C], BF16)
        rgu_kmb = sbc("rgu_kmb", [128, IN4], BF16)
        tern_c = rgu_kmb[:, 0:C]
        rhf_km = rhf_kmb[:]
        rhf_fm = rhf_fmb[:]
        rgu_km = rgu_kmb[:]
        psT = pc.enter_context(nc.psum_tensor("psTc", [128, 1024], F32))
        psO = pc.enter_context(nc.psum_tensor("psOc", [128, 1024], F32))
        psB = pc.enter_context(nc.psum_tensor("psBc", [128, 1024], BF16))

        def wguf_dst(ci_, g_, buf):
            r = buf[:].rearrange("p (h i) -> p h i", h=CI)
            return r[:, :, 128 * ci_:128 * (ci_ + 1)]

        w_mean_pass(wg_d, 32, 1024, 0, gu, psT)
        w_tern_pass(wg_d, 32, 1024, 0, lambda c, g: wguf_dst(c, g, wgf),
                    gu, ab, tmp1, tern_c, psB)
        w_mean_pass(wu_d, 32, 1024, 1, gu, psT)
        w_tern_pass(wu_d, 32, 1024, 1, lambda c, g: wguf_dst(c, g, wuf),
                    gu, ab, tmp1, tern_c, psB)

        with nc.Block() as blk:
            @blk.sync
            def _(sy):
                ld.inc(sy.dma_start(
                    out=bc0[:], in_=modv_s[0:1, 3 * C:4 * C].broadcast_to([128, C])))
                ld.inc(sy.dma_start(
                    out=bc1[:], in_=modv_s[0:1, 4 * C:5 * C].broadcast_to([128, C])))
                sy.wait_ge(ld.sem, ld.n)

        for t in range(TILES):
            if t == TPB:
                with nc.Block() as blk:
                    @blk.sync
                    def _(sy):
                        ld.inc(sy.dma_start(
                            out=bc0[:],
                            in_=modv_s[1:2, 3 * C:4 * C].broadcast_to([128, C])))
                        ld.inc(sy.dma_start(
                            out=bc1[:],
                            in_=modv_s[1:2, 4 * C:5 * C].broadcast_to([128, C])))
                        sy.wait_ge(ld.sem, ld.n)

            with nc.Block() as blk:
                W = {}
                wrapd = {}
                gp_b = cnt["gp"]
                gp_hf = gp_b + 1
                gp_rq = lambda qr, gp_b=gp_b: gp_b + 2 + qr

                @blk.sync
                def _(sy, t=t):
                    ld.inc(sy.dma_start(out=xb[:], in_=x1_s[NT * t:NT * (t + 1), :]))
                    W["ld"] = ld.n

                @blk.scalar
                def _(sc):
                    sc = wrapd.setdefault("sc", EngWrap(sc))
                    sc.wait_ge(ld.sem, W["ld"])
                    W["sq"] = mk(sc.activation(out=fmbuf[:], in_=xb[:], func=AF.Square,
                                               accum_out=s2[:]), "sc")

                @blk.vector
                def _(v):
                    v = wrapd.setdefault("v", EngWrap(v))
                    v.wait_ge(ld.sem, W["ld"])
                    v.reduce_sum(s1[:], xb[:], axis=AX)
                    v.wait_ge(sem_sc, W["sq"])
                    v.tensor_scalar(e2v[:], s2[:], INV_C, None, op0=ALU.mult)
                    v.tensor_scalar(mvec[:], s1[:], INV_C, None, op0=ALU.mult)
                    v.tensor_mul(varv[:], mvec[:], mvec[:])
                    W["var"] = mk(v.tensor_sub(varv[:], e2v[:], varv[:]), "v")

                @blk.scalar
                def _(sc):
                    sc = wrapd["sc"]
                    sc.wait_ge(sem_v, W["var"])
                    W["sd"] = mk(sc.activation(out=sdv[:], in_=varv[:], func=AF.Sqrt,
                                               bias=epsb[:]), "sc")

                @blk.vector
                def _(v):
                    v = wrapd["v"]
                    v.wait_ge(sem_sc, W["sd"])
                    v.reciprocal(rstd[:], sdv[:])
                    v.tensor_mul(mrv[:], mvec[:], rstd[:])
                    v.tensor_scalar(tmp1[:], xb[:], rstd[:], mrv[:],
                                    op0=ALU.mult, op1=ALU.subtract)
                    v.tensor_mul(hb[:], tmp1[:], bc0[:])
                    v.tensor_add(hb[:], hb[:], bc1[:])
                    v.tensor_reduce(s1[:], hb[:], axis=AX, op=ALU.max,
                                    apply_absolute_value=True)
                    v.tensor_scalar(sc_0[:], s1[:], 1e-5, None, op0=ALU.max)
                    v.reciprocal(sc_1[:], sc_0[:])
                    v.tensor_scalar(sc_1[:], sc_1[:], 127.0, None, op0=ALU.mult)
                    v.tensor_mul(sc_2[:], sc_0[:], swb[:, 0:1])
                    v.tensor_scalar(sc_2[:], sc_2[:], 1.0 / 127.0, None, op0=ALU.mult)
                    v.tensor_mul(sc_3[:], sc_0[:], swb[:, 1:2])
                    v.tensor_scalar(sc_3[:], sc_3[:], 1.0 / 127.0, None, op0=ALU.mult)
                    W["lad_hf"] = ladder_v(v, hb[:], C, sc_1[:], rhf_km)

                @blk.gpsimd
                def _(g):
                    g = wrapd.setdefault("g", EngWrap(g))
                    W["lad_hf_g"] = ladder_g(g, C, rhf_km, W["lad_hf"][0])
                    assert W["lad_hf_g"] == gp_hf, (W["lad_hf_g"], gp_hf)

                @blk.tensor
                def _(te, t=t):
                    te.wait_ge(sem_v, W["lad_hf"][1])
                    te.wait_ge(sem_gp, W["lad_hf_g"])
                    ins = None
                    for i in range(8):
                        ins = te.transpose(psB[:, 128 * i:128 * (i + 1)],
                                           rhf_km[:, 128 * i:128 * (i + 1)], idbf[:])
                    W["rhfT"] = mk(ins, "pe")

                @blk.vector
                def _(v):
                    v = wrapd["v"]
                    v.wait_ge(sem_pe, W["rhfT"])
                    W["rhf_fm"] = mk(v.tensor_copy(rhf_fm[:], psB[:, 0:1024]), "v")

                sc_b = cnt["sc"]
                v_b2 = cnt["v"]
                sc_silu = lambda qr, sc_b=sc_b: sc_b + qr + 1
                v_guq = lambda qr, v_b2=v_b2: v_b2 + 2 * qr + 1

                @blk.tensor
                def _(te, t=t, sc_silu=sc_silu, v_guq=v_guq):
                    W["gmm"] = []
                    W["umm"] = []
                    te.wait_ge(sem_v, W["rhf_fm"])
                    for qr in range(4):
                        if qr >= 1:
                            te.wait_ge(sem_sc, sc_silu(qr - 1))
                        ins = None
                        for nn in range(2):
                            for kc in range(CI):
                                ins = te.matmul(
                                    psT[:, 512 * nn:512 * (nn + 1)],
                                    rhf_fm[:, 128 * kc:128 * (kc + 1)],
                                    wgf[:, 4096 * kc + 1024 * qr + 512 * nn:
                                        4096 * kc + 1024 * qr + 512 * (nn + 1)],
                                    start=(kc == 0), stop=(kc == CI - 1))
                        W["gmm"].append(mk(ins, "pe"))
                        if qr >= 1:
                            te.wait_ge(sem_v, v_guq(qr - 1))
                        ins = None
                        for nn in range(2):
                            for kc in range(CI):
                                ins = te.matmul(
                                    psO[:, 512 * nn:512 * (nn + 1)],
                                    rhf_fm[:, 128 * kc:128 * (kc + 1)],
                                    wuf[:, 4096 * kc + 1024 * qr + 512 * nn:
                                        4096 * kc + 1024 * qr + 512 * (nn + 1)],
                                    start=(kc == 0), stop=(kc == CI - 1))
                        W["umm"].append(mk(ins, "pe"))

                @blk.scalar
                def _(sc, sc_silu=sc_silu, v_guq=v_guq):
                    sc = wrapd["sc"]
                    for qr in range(4):
                        sc.wait_ge(sem_pe, W["gmm"][qr])
                        if qr >= 1:
                            sc.wait_ge(sem_v, v_guq(qr - 1))
                        assert mk(sc.activation(out=ab[:], in_=psT[:], func=AF.Silu,
                                                scale=sc_2[:]), "sc") == sc_silu(qr)

                @blk.vector
                def _(v):
                    v = wrapd["v"]
                    for qr in range(4):
                        v.wait_ge(sem_sc, sc_silu(qr))
                        v.wait_ge(sem_pe, W["umm"][qr])
                        assert mk(v.tensor_mul(gu[:, 1024 * qr:1024 * (qr + 1)],
                                               ab[:], psO[:]), "v") == v_guq(qr)
                        v.tensor_reduce(s1[:], gu[:, 1024 * qr:1024 * (qr + 1)],
                                        axis=AX, op=ALU.max,
                                        apply_absolute_value=True)
                        if qr == 0:
                            ins = v.tensor_copy(sc_4[:], s1[:])
                        else:
                            ins = v.tensor_max(sc_4[:], sc_4[:], s1[:])
                        mk(ins, "v")
                    v.tensor_mul(sc_4[:], sc_4[:], sc_3[:])
                    v.tensor_scalar(sc_4[:], sc_4[:], 1e-5, None, op0=ALU.max)
                    v.reciprocal(s2[:], sc_4[:])
                    v.tensor_mul(sc_1[:], sc_3[:], s2[:])
                    v.tensor_scalar(sc_1[:], sc_1[:], 127.0, None, op0=ALU.mult)
                    W["osc"] = mk(v.tensor_scalar(sc_0[:], sc_4[:], 1.0 / 127.0, None,
                                                  op0=ALU.mult), "v")
                    W["rq_v"] = []
                    W["rq_b"] = []
                    for qr in range(4):
                        v.wait_ge(sem_gp, gp_hf if qr == 0 else gp_rq(qr - 1))
                        bmk, rmk = ladder_v(v, gu[:, 1024 * qr:1024 * (qr + 1)], C,
                                            sc_1[:],
                                            rgu_km[:, 1024 * qr:1024 * (qr + 1)])
                        W["rq_v"].append(rmk)
                        W["rq_b"].append(bmk)

                @blk.gpsimd
                def _(g):
                    g = wrapd["g"]
                    W["rq_g"] = []
                    for qr in range(4):
                        r_ = ladder_g(g, C, rgu_km[:, 1024 * qr:1024 * (qr + 1)],
                                      W["rq_b"][qr])
                        assert r_ == gp_rq(qr), (r_, gp_rq(qr))
                        W["rq_g"].append(r_)

                @blk.sync
                def _(sy, t=t):
                    sy.wait_ge(sem_v, W["osc"])
                    st.inc(sy.dma_start(out=osc_s[NT * t:NT * (t + 1), :], in_=sc_0[:]))
                    sy.wait_ge(sem_v, W["rq_v"][3])
                    sy.wait_ge(sem_gp, W["rq_g"][3])
                    st.inc(sy.dma_start(out=r_s[NT * t:NT * (t + 1), :], in_=rgu_km))
                    sy.wait_ge(st.sem, st.n)

    with contextlib.ExitStack() as pd_:
        sbd = lambda n, s, d=F32: pd_.enter_context(nc.sbuf_tensor(n, s, d))
        wdf = sbd("wdf", [128, 32 * C], BF16)
        rkm = sbd("rkm", [128, IN4], BF16)
        stgd = sbd("stgd", [128, 2 * IN4])
        tAd = sbd("tAd", [128, IN4])
        tBd = sbd("tBd", [128, IN4])
        rfmb = sbd("rfmb", [128, IN4], BF16)
        rfm = rfmb[:]
        tern_d = rkm[:]
        psT = pd_.enter_context(nc.psum_tensor("psTd", [128, 1024], F32))
        psB = pd_.enter_context(nc.psum_tensor("psBd", [128, 1024], BF16))

        def wd_dst(ci_, g_):
            r = wdf[:].rearrange("p (i o) -> p i o", i=32)
            return r[:, 8 * g_:8 * (g_ + 1), 128 * ci_:128 * (ci_ + 1)]

        w_mean_pass(wd_d, 8, 4096, 2, stgd, psT)
        w_tern_pass(wd_d, 8, 4096, 2, wd_dst, stgd, tAd, tBd, tern_d, psB)

        with nc.Block() as blk:
            @blk.sync
            def _(sy):
                ld.inc(sy.dma_start(
                    out=bc2[:], in_=modv_s[0:1, 5 * C:6 * C].broadcast_to([128, C])))
                sy.wait_ge(ld.sem, ld.n)

        for t in range(TILES):
            if t == TPB:
                with nc.Block() as blk:
                    @blk.sync
                    def _(sy):
                        ld.inc(sy.dma_start(
                            out=bc2[:],
                            in_=modv_s[1:2, 5 * C:6 * C].broadcast_to([128, C])))
                        sy.wait_ge(ld.sem, ld.n)

            with nc.Block() as blk:
                W = {}
                wrapd = {}
                v_b = cnt["v"]
                v_evc = lambda g_, v_b=v_b: v_b + g_ + 1

                @blk.sync
                def _(sy, t=t):
                    ld.inc(sy.dma_start(out=rkm[:], in_=r_s[NT * t:NT * (t + 1), :]))
                    ld.inc(sy.dma_start(out=xb[:], in_=x1_s[NT * t:NT * (t + 1), :]))
                    ld.inc(sy.dma_start(out=sc_0[:], in_=osc_s[NT * t:NT * (t + 1), :]))
                    W["ld"] = ld.n

                @blk.tensor
                def _(te, t=t, v_evc=v_evc):
                    te.wait_ge(ld.sem, W["ld"])
                    W["T"] = []
                    for g_ in range(4):
                        if g_ >= 1:
                            te.wait_ge(sem_v, v_evc(g_ - 1))
                        ins = None
                        for i in range(8):
                            pi = 8 * g_ + i
                            ins = te.transpose(
                                psB[:, 128 * i:128 * (i + 1)],
                                rkm[:, 128 * pi:128 * (pi + 1)], idbf[:])
                        W["T"].append(mk(ins, "pe"))

                @blk.vector
                def _(v, t=t):
                    v = wrapd.setdefault("v", EngWrap(v))
                    v.wait_ge(ld.sem, W["ld"])
                    v.tensor_mul(sc_0[:], sc_0[:], swb[:, 2:3])
                    for g_ in range(4):
                        v.wait_ge(sem_pe, W["T"][g_])
                        src_ = psB[:, 0:1024].rearrange("p (i t) -> p i t", i=8)
                        dst = rfm.rearrange("p (i t) -> p i t", i=32)[
                            :, 8 * g_:8 * (g_ + 1), :]
                        assert mk(v.tensor_copy(dst, src_), "v") == v_evc(g_)

                @blk.tensor
                def _(te, t=t, v_evc=v_evc):
                    te.wait_ge(sem_v, v_evc(3))
                    W["dn"] = []
                    for nn in range(2):
                        ins = None
                        for ic in range(32):
                            ins = te.matmul(
                                psT[:, 512 * nn:512 * (nn + 1)],
                                rfm[:, 128 * ic:128 * (ic + 1)],
                                wdf[:, 1024 * ic + 512 * nn:1024 * ic + 512 * (nn + 1)],
                                start=(ic == 0), stop=(ic == 31))
                        W["dn"].append(mk(ins, "pe"))

                @blk.vector
                def _(v, t=t):
                    v = wrapd["v"]
                    for nn in range(2):
                        v.wait_ge(sem_pe, W["dn"][nn])
                        v.tensor_scalar(tmp1[:, 512 * nn:512 * (nn + 1)],
                                        psT[:, 512 * nn:512 * (nn + 1)],
                                        sc_0[:], None, op0=ALU.mult)
                    v.tensor_mul(tmp1[:], tmp1[:], bc2[:])
                    W["x2"] = mk(v.tensor_add(ab[:], tmp1[:], xb[:]), "v")

                @blk.sync
                def _(sy, t=t):
                    sy.wait_ge(sem_v, W["x2"])
                    st.inc(sy.dma_start(out=out_d[NT * t:NT * (t + 1), :], in_=ab[:]))
                    sy.wait_ge(st.sem, st.n)

    es.close()
    return nc


_NC_CACHE = [None]


def _shard_inputs(inputs):
    f32 = lambda a: np.ascontiguousarray(np.asarray(a, dtype=np.float32))
    x = f32(inputs["L0_attn_x"]).reshape(B, T, C)
    cv = f32(inputs["L0_c"])
    common = {
        "wqkv": f32(inputs["L0_attn_w_qkv"]),
        "bqkv": f32(inputs["L0_attn_b_qkv"]).reshape(1, 3 * C),
        "wout": f32(inputs["L0_attn_w_out"]),
        "bout": f32(inputs["L0_attn_b_out"]).reshape(1, C),
        "ln1g": f32(inputs["L0_attn_ln1_g"]).reshape(1, C),
        "ln1b": f32(inputs["L0_attn_ln1_b"]).reshape(1, C),
        "ln2g": f32(inputs["L0_attn_ln2_g"]).reshape(1, C),
        "ln2b": f32(inputs["L0_attn_ln2_b"]).reshape(1, C),
        "w1": f32(inputs["L0_attn_w1"]),
        "b1": f32(inputs["L0_attn_b1"]).reshape(1, IN4),
        "w2": f32(inputs["L0_attn_w2"]),
        "b2": f32(inputs["L0_attn_b2"]).reshape(1, C),
        "wg": f32(inputs["L0_ffn_w_gate"]),
        "wu": f32(inputs["L0_ffn_w_up"]),
        "wd": f32(inputs["L0_ffn_w_down"]),
        "wmod": f32(inputs["L0_wmod"]),
    }
    maps = []
    for i in range(NCORES):
        m = dict(common)
        m["x"] = np.ascontiguousarray(x[BPC * i:BPC * (i + 1)]).reshape(TOK, C)
        m["cvec"] = np.ascontiguousarray(cv[BPC * i:BPC * (i + 1)])
        maps.append(m)
    return maps


def kernel(**inputs):
    if _NC_CACHE[0] is None:
        _NC_CACHE[0] = build_nc(debug=0)
    nc = _NC_CACHE[0]
    in_maps = _shard_inputs(inputs)
    res = run_bass_kernel_spmd(nc, in_maps, core_ids=list(range(NCORES)))
    out = np.concatenate(
        [np.asarray(r["out"], dtype=np.float32).reshape(BPC, T, C)
         for r in res.results], axis=0)
    return out


# version 03
# speedup vs baseline: 1.0083x; 1.0045x over previous
import contextlib
import numpy as np
import ml_dtypes
import concourse.bass as bass
import concourse.mybir as mybir
from concourse.bass_utils import run_bass_kernel_spmd

F32 = mybir.dt.float32
F16 = mybir.dt.float16
BF16 = mybir.dt.bfloat16
ALU = mybir.AluOpType
AF = mybir.ActivationFunctionType
AX = mybir.AxisListType.X

B, T, C = 16, 2048, 1024
NCORES = 8
BPC = B // NCORES
TOK = BPC * T
NT = 128
TILES = TOK // NT
TPB = T // NT
HEADS, HD = 16, 64
CI = C // 128
IN4 = 4096
EPS = 1e-5
NEG = -1.0e30
INV_C = 1.0 / C


def _mask_np():
    q = np.arange(128)[:, None]
    j = np.arange(192)[None, :]
    allowed = (j >= 64 * (q >= 64)) & (j <= q + 64)
    return np.where(allowed, np.float32(0.0), np.float32(NEG)).astype(np.float32)


class EngWrap:

    def __init__(self, eng):
        self.e = eng
        self.w = set()
        self.r = set()

    @staticmethod
    def _nm(a):
        try:
            return a.tensor.name
        except AttributeError:
            return None

    def _pre(self, outs, ins):
        on = {self._nm(a) for a in outs} - {None}
        inn = {self._nm(a) for a in ins} - {None}
        if ((on | inn) & self.w) or (on & self.r):
            self.e.drain()
            self.w.clear()
            self.r.clear()
        self.w |= on
        self.r |= inn

    def wait_ge(self, sem, val):
        return self.e.wait_ge(sem, val)

    def drain(self):
        ins = self.e.drain()
        self.w.clear()
        self.r.clear()
        return ins

    def tensor_scalar(self, out, in0, s1, s2, **kw):
        self._pre([out], [in0, s1, s2])
        return self.e.tensor_scalar(out, in0, s1, s2, **kw)

    def scalar_tensor_tensor(self, out, in0, s, in1, **kw):
        self._pre([out], [in0, s, in1])
        return self.e.scalar_tensor_tensor(out, in0, s, in1, **kw)

    def tensor_tensor(self, out, in0, in1, **kw):
        self._pre([out], [in0, in1])
        return self.e.tensor_tensor(out, in0, in1, **kw)

    def tensor_add(self, out, a, b):
        self._pre([out], [a, b])
        return self.e.tensor_add(out, a, b)

    def tensor_sub(self, out, a, b):
        self._pre([out], [a, b])
        return self.e.tensor_sub(out, a, b)

    def tensor_mul(self, out, a, b):
        self._pre([out], [a, b])
        return self.e.tensor_mul(out, a, b)

    def tensor_max(self, out, a, b):
        self._pre([out], [a, b])
        return self.e.tensor_max(out, a, b)

    def tensor_copy(self, out, in_):
        self._pre([out], [in_])
        return self.e.tensor_copy(out, in_)

    def memset(self, ap, c):
        self._pre([ap], [])
        return self.e.memset(ap, c)

    def tensor_reduce(self, out, in_, **kw):
        self._pre([out], [in_])
        return self.e.tensor_reduce(out, in_, **kw)

    def reduce_sum(self, out, in_, **kw):
        self._pre([out], [in_])
        return self.e.reduce_sum(out, in_, **kw)

    def reduce_max(self, out, in_, **kw):
        self._pre([out], [in_])
        return self.e.reduce_max(out, in_, **kw)

    def reciprocal(self, out, in_):
        self._pre([out], [in_])
        return self.e.reciprocal(out, in_)

    def activation(self, out, in_, func, bias=0.0, scale=1.0, accum_out=None):
        outs = [out] + ([accum_out] if accum_out is not None else [])
        self._pre(outs, [in_, bias, scale])
        kw = dict(func=func, bias=bias, scale=scale)
        if accum_out is not None:
            kw["accum_out"] = accum_out
        return self.e.activation(out=out, in_=in_, **kw)



class Mono:
    def __init__(self, sem, step=16):
        self.sem = sem
        self.step = step
        self.n = 0

    def inc(self, instr):
        instr.then_inc(self.sem, self.step)
        self.n += self.step
        return self.n


def build_nc(debug=0):
    nc = bass.Bass("TRN2", target_bir_lowering=False)

    def din(name, shape, dt=F32):
        return nc.dram_tensor(name, shape, dt, kind="ExternalInput")

    x_d = din("x", [TOK, C]).ap()
    wqkv_d = din("wqkv", [C, 3 * C]).ap()
    bqkv_d = din("bqkv", [1, 3 * C]).ap()
    wout_d = din("wout", [C, C]).ap()
    bout_d = din("bout", [1, C]).ap()
    ln1g_h = din("ln1g", [1, C])
    ln1b_h = din("ln1b", [1, C])
    ln2g_d = din("ln2g", [1, C]).ap()
    ln2b_d = din("ln2b", [1, C]).ap()
    w1_d = din("w1", [C, IN4]).ap()
    b1_h = din("b1", [1, IN4])
    w2_d = din("w2", [IN4, C]).ap()
    b2_d = din("b2", [1, C]).ap()
    wg_d = din("wg", [IN4, C]).ap()
    wu_d = din("wu", [IN4, C]).ap()
    wd_d = din("wd", [C, IN4]).ap()
    c_d = din("cvec", [BPC, C]).ap()
    wmod_d = din("wmod", [C, 6 * C]).ap()

    out_d = nc.dram_tensor("out", [TOK, C], F32, kind="ExternalOutput").ap()

    def dscr(name, shape, dt=F32, tap=False):
        if debug and tap:
            return nc.dram_tensor(name, shape, dt, kind="ExternalOutput").ap()
        return nc.dram_tensor(name, shape, dt).ap()

    a_s = dscr("a_s", [TOK, C], tap=True)
    p_s = dscr("p_s", [TOK, C], tap=True)
    x1_s = dscr("x1_s", [TOK, C])
    r_s = dscr("r_s", [TOK, IN4], BF16)
    modv_s = dscr("modv_s", [BPC, 6 * C])
    qb_h = nc.dram_tensor("qb_s", [1, 3 * C], F32)
    qb_s = qb_h.ap()
    osc_s = dscr("osc_s", [TOK, 1])
    sw_s = dscr("sw_s", [1, 8])

    dbg_d = {}
    if debug:
        for i in range(debug):
            dbg_d[i] = nc.dram_tensor(f"dbg{i}", [128, 4 * C], F32,
                                      kind="ExternalOutput").ap()

    ident32_c = nc.inline_tensor(np.eye(128, dtype=np.float32), name="ident32").ap()
    identbf_c = nc.inline_tensor(np.eye(128, dtype=ml_dtypes.bfloat16), name="identbf").ap()
    mask_c = nc.inline_tensor(_mask_np(), name="maskc").ap()

    es = contextlib.ExitStack()

    def sbuf(name, shape, dt=F32):
        return es.enter_context(nc.sbuf_tensor(name, shape, dt))


    id32 = sbuf("id32", [128, 128])
    idbf = sbuf("idbf", [128, 128], BF16)
    maskb = sbuf("maskb", [128, 192])
    bc0 = sbuf("bc0", [128, C])
    bc1 = sbuf("bc1", [128, C])
    bc2 = sbuf("bc2", [128, C])
    s1 = sbuf("s1", [128, 1])
    s2 = sbuf("s2", [128, 1])
    mvec = sbuf("mvec", [128, 1])
    e2v = sbuf("e2v", [128, 1])
    varv = sbuf("varv", [128, 1])
    sdv = sbuf("sdv", [128, 1])
    rstd = sbuf("rstd", [128, 1])
    mrv = sbuf("mrv", [128, 1])
    rstd1 = sbuf("rstd1", [128, 1])
    mrv1 = sbuf("mrv1", [128, 1])
    mx16 = sbuf("mx16", [128, 16])
    ng16 = sbuf("ng16", [128, 16])
    rs16 = sbuf("rs16", [128, 16])
    rr16 = sbuf("rr16", [128, 16])
    sc_0 = sbuf("sc_0", [128, 1])
    sc_1 = sbuf("sc_1", [128, 1])
    sc_2 = sbuf("sc_2", [128, 1])
    sc_3 = sbuf("sc_3", [128, 1])
    sc_4 = sbuf("sc_4", [128, 1])
    swb = sbuf("swb", [128, 8])
    epsb = sbuf("epsb", [128, 1])
    gdum = sbuf("gdum", [128, 1])
    xb = sbuf("xb", [128, C])
    hb = sbuf("hb", [128, C])
    tmp1 = sbuf("tmp1", [128, C])
    fmbuf = sbuf("fmbuf", [128, C])
    qkvk = sbuf("qkvk", [128, 3 * C])
    okm = sbuf("okm", [128, C])
    ab = sbuf("ab", [128, C])

    sem_pe = es.enter_context(nc.semaphore("sem_pe"))
    sem_v = es.enter_context(nc.semaphore("sem_v"))
    sem_sc = es.enter_context(nc.semaphore("sem_sc"))
    sem_gp = es.enter_context(nc.semaphore("sem_gp"))
    ld = Mono(es.enter_context(nc.semaphore("ld_sem")))
    lda = Mono(es.enter_context(nc.semaphore("lda_sem")))
    ldb = Mono(es.enter_context(nc.semaphore("ldb_sem")))
    st = Mono(es.enter_context(nc.semaphore("st_sem")))
    hal = Mono(es.enter_context(nc.semaphore("hal_sem")))

    cnt = {"pe": 0, "v": 0, "sc": 0, "gp": 0}
    sems = {"pe": sem_pe, "v": sem_v, "sc": sem_sc, "gp": sem_gp}

    def mk(instr, which, n=1):
        instr.then_inc(sems[which], n)
        cnt[which] += n
        return cnt[which]

    dbg_used = [0]

    def dbg_dump(sy, src_ap, rows, cols):
        i = dbg_used[0]
        dbg_used[0] += 1
        if i in dbg_d:
            sy.dma_start(out=dbg_d[i][0:rows, 0:cols], in_=src_ap).then_inc(st.sem, 16)
            st.n += 16

    with contextlib.ExitStack() as p0:
        sb0 = lambda n, s, d=F32: p0.enter_context(nc.sbuf_tensor(n, s, d))
        stage = sb0("stage0", [128, 3 * C])
        cb2 = sb0("cb2", [BPC, C])
        scfm = sb0("scfm", [128, 2 * CI])
        modk = sb0("modk", [BPC, 6 * C])
        vr = sb0("vr", [BPC, 6 * C])
        vt1 = sb0("vt1", [BPC, C])
        vt2 = sb0("vt2", [BPC, C])
        psA = p0.enter_context(nc.psum_tensor("psA0", [128, 1536], F32))
        psT = p0.enter_context(nc.psum_tensor("psT0", [128, 1024], F32))

        with nc.Block() as blk:
            @blk.sync
            def _(sy):
                for buf, src in ((id32, ident32_c), (idbf, identbf_c),
                                 (maskb, mask_c), (cb2, c_d)):
                    ld.inc(sy.dma_start(out=buf[:], in_=src))
                sy.wait_ge(ld.sem, ld.n)

            @blk.vector
            def _(v):
                v = EngWrap(v)
                v.memset(epsb[:], EPS)
                v.memset(swb[:], 0.0)

        with nc.Block() as blk:
            @blk.tensor
            def _(te):
                ins = None
                for kc in range(CI):
                    ins = te.transpose(psT[:, 2 * kc:2 * kc + 2],
                                       cb2[:, 128 * kc:128 * (kc + 1)],
                                       id32[0:BPC, 0:BPC])
                mk(ins, "pe")

            @blk.scalar
            def _(sc):
                sc = EngWrap(sc)
                sc.wait_ge(sem_pe, cnt["pe"])
                sc.activation(out=scfm[:], in_=psT[:, 0:2 * CI], func=AF.Silu)

        for qt in range(4):
            pe0 = cnt["pe"]
            par = (lda, ldb)
            par0 = [lda.n, ldb.n]
            with nc.Block() as blk:
                @blk.sync
                def _(sy, qt=qt, pe0=pe0):
                    for kc in range(CI):
                        half = kc % 2
                        if kc >= 2:
                            sy.wait_ge(sem_pe, pe0 + kc - 1)
                        par[half].inc(sy.dma_start(
                            out=stage[:, 1536 * half:1536 * (half + 1)],
                            in_=wmod_d[128 * kc:128 * (kc + 1),
                                       1536 * qt:1536 * (qt + 1)]))

                @blk.tensor
                def _(te, qt=qt, par0=par0):
                    base = {0: par0[0], 1: par0[1]}
                    for kc in range(CI):
                        half = kc % 2
                        base[half] += 16
                        te.wait_ge(par[half].sem, base[half])
                        ins = None
                        for nn in range(3):
                            ins = te.matmul(
                                psA[0:BPC, 512 * nn:512 * (nn + 1)],
                                scfm[:, 2 * kc:2 * kc + 2],
                                stage[:, 1536 * half + 512 * nn:1536 * half + 512 * (nn + 1)],
                                start=(kc == 0), stop=(kc == CI - 1))
                        mk(ins, "pe")

                @blk.vector
                def _(v, qt=qt):
                    v = EngWrap(v)
                    v.wait_ge(sem_pe, cnt["pe"])
                    v.tensor_copy(modk[:, 1536 * qt:1536 * (qt + 1)], psA[0:BPC, :])

        with nc.Block() as blk:
            @blk.vector
            def _(v):
                v = EngWrap(v)
                P2 = slice(0, BPC)
                sa = modk[P2, 0:C]
                ca = modk[P2, C:2 * C]
                ga = modk[P2, 2 * C:3 * C]
                sbv = modk[P2, 3 * C:4 * C]
                cbv = modk[P2, 4 * C:5 * C]
                gbv = modk[P2, 5 * C:6 * C]
                v.tensor_scalar(vr[P2, 0:C], ca, 1.0, None, op0=ALU.add)
                v.tensor_copy(vr[P2, C:2 * C], sa)
                v.tensor_scalar(vr[P2, 3 * C:4 * C], cbv, 1.0, None, op0=ALU.add)
                v.tensor_copy(vr[P2, 4 * C:5 * C], sbv)
                for gsrc, gdst in ((ga, vr[P2, 2 * C:3 * C]),
                                   (gbv, vr[P2, 5 * C:6 * C])):
                    v.tensor_scalar(vt1[P2, :], gsrc, 0.0, None, op0=ALU.is_ge)
                    v.tensor_scalar(vt1[P2, :], vt1[P2, :], 2.0, 1.0,
                                    op0=ALU.mult, op1=ALU.subtract)
                    v.tensor_scalar(vt2[P2, :], gsrc, -1.0, None, op0=ALU.mult)
                    v.tensor_max(vt2[P2, :], vt2[P2, :], gsrc)
                    v.tensor_scalar(vt2[P2, :], vt2[P2, :], 0.25, None, op0=ALU.max)
                    ins = v.tensor_mul(gdst, vt1[P2, :], vt2[P2, :])
                mk(ins, "v")

            @blk.sync
            def _(sy):
                sy.wait_ge(sem_v, cnt["v"])
                st.inc(sy.dma_start(out=modv_s[:, :], in_=vr[:]))
                st.inc(sy.dma_start(out=sw_s, in_=swb[0:1, 0:8]))
                sy.wait_ge(st.sem, st.n)

    with contextlib.ExitStack() as pa:
        sba = lambda n, s, d=F32: pa.enter_context(nc.sbuf_tensor(n, s, d))
        wqkv = sba("wqkv_sb", [128, CI * 3 * C])
        wo = sba("wo_sb", [128, CI * C])
        qfm = sba("qfm", [128, C])
        kfm = sba("kfm", [128, C])
        khalo = sba("khalo", [128, 512])
        vhalo = sba("vhalo", [64, C])
        smb = sba("smb", [128, 192])
        pbuf = sba("pbuf", [128, 192])
        ptb = sba("ptb", [128, 256])
        bvb = sba("bvb", [128, C])
        qkb_f = sba("qkb_f", [128, 16])
        bc3 = pa.enter_context(nc.sbuf_tensor("bc3a", [128, C], F32))
        psA = pa.enter_context(nc.psum_tensor("psAa", [128, 1536], F32))
        psT = pa.enter_context(nc.psum_tensor("psTa", [128, 1024], F32))
        psS = pa.enter_context(nc.psum_tensor("psSa", [128, 512], F32))
        psO = pa.enter_context(nc.psum_tensor("psOa", [128, 1024], F32))
        pa_setup = contextlib.ExitStack()
        g1f = pa_setup.enter_context(nc.sbuf_tensor("g1f", [128, CI], F32))
        b1hf = pa_setup.enter_context(nc.sbuf_tensor("b1hf", [128, CI], F32))
        rwa = qkvk[0:1, :]

        with nc.Block() as blk:
            @blk.sync
            def _(sy):
                for kc in range(CI):
                    ld.inc(sy.dma_start(out=wqkv[:, 3072 * kc:3072 * (kc + 1)],
                                        in_=wqkv_d[128 * kc:128 * (kc + 1), :]))
                for kc in range(CI):
                    ld.inc(sy.dma_start(out=wo[:, 1024 * kc:1024 * (kc + 1)],
                                        in_=wout_d[128 * kc:128 * (kc + 1), :]))
                ld.inc(sy.dma_start(out=g1f[:],
                                    in_=bass.AP(ln1g_h, 0, [[1, 128], [128, CI]]),
                                    allow_slow_non_contiguous=True))
                ld.inc(sy.dma_start(out=b1hf[:],
                                    in_=bass.AP(ln1b_h, 0, [[1, 128], [128, CI]]),
                                    allow_slow_non_contiguous=True))
                sy.wait_ge(ld.sem, ld.n)

        pe0 = cnt["pe"]
        v0 = cnt["v"]
        W0 = {}
        with nc.Block() as blk:
            @blk.tensor
            def _(te):
                for half in range(2):
                    if half == 1:
                        te.wait_ge(sem_v, v0 + 1)
                    ins = None
                    for kc in range(CI):
                        for nn in range(3):
                            co = 3072 * kc + 1536 * half + 512 * nn
                            ins = te.matmul(psA[0:1, 512 * nn:512 * (nn + 1)],
                                            b1hf[:, kc:kc + 1],
                                            wqkv[:, co:co + 512],
                                            start=(kc == 0), stop=(kc == CI - 1))
                    mk(ins, "pe")

            @blk.sync
            def _(sy):
                for pp in range(3):
                    ld.inc(sy.dma_start(out=fmbuf[0:1, 0:1024],
                                        in_=bqkv_d[0:1, 1024 * pp:1024 * (pp + 1)]))
                    W0[f"bq{pp}"] = ld.n
                    sy.wait_ge(sem_v, v0 + 2 + pp + 1)

            @blk.vector
            def _(v):
                v = EngWrap(v)
                v.wait_ge(sem_pe, pe0 + 1)
                mk(v.tensor_copy(rwa[:, 0:1536], psA[0:1, :]), "v")
                v.wait_ge(sem_pe, pe0 + 2)
                mk(v.tensor_copy(rwa[:, 1536:3072], psA[0:1, :]), "v")
                for pp in range(3):
                    v.wait_ge(ld.sem, W0[f"bq{pp}"])
                    mk(v.tensor_add(rwa[:, 1024 * pp:1024 * (pp + 1)],
                                    rwa[:, 1024 * pp:1024 * (pp + 1)],
                                    fmbuf[0:1, 0:1024]), "v")

            @blk.sync
            def _(sy):
                sy.wait_ge(sem_v, cnt["v"])
                st.inc(sy.dma_start(out=qb_s, in_=rwa[:]))
                sy.wait_ge(st.sem, st.n)

        with nc.Block() as blk:
            @blk.vector
            def _(v):
                v = EngWrap(v)
                for kc in range(CI):
                    v.tensor_scalar(wqkv[:, 3072 * kc:3072 * (kc + 1)],
                                    wqkv[:, 3072 * kc:3072 * (kc + 1)],
                                    g1f[:, kc:kc + 1], None, op0=ALU.mult)

            @blk.sync
            def _(sy):
                ld.inc(sy.dma_start(out=bvb[:],
                                    in_=qb_s[0:1, 2048:3072].broadcast_to([128, C])))
                ld.inc(sy.dma_start(out=qkb_f[:],
                                    in_=bass.AP(qb_h, 0, [[1, 128], [128, 16]]),
                                    allow_slow_non_contiguous=True))
                ld.inc(sy.dma_start(out=bc3[:], in_=bout_d.broadcast_to([128, C])))
                for i, buf in enumerate((bc0, bc1, bc2)):
                    ld.inc(sy.dma_start(
                        out=buf[:],
                        in_=modv_s[0:1, C * i:C * (i + 1)].broadcast_to([128, C])))
                sy.wait_ge(ld.sem, ld.n)

        pa_setup.close()

        for t in range(TILES):
            seq_start = (t % TPB) == 0
            if t == TPB:
                with nc.Block() as blk:
                    @blk.sync
                    def _(sy):
                        for i, buf in enumerate((bc0, bc1, bc2)):
                            ld.inc(sy.dma_start(
                                out=buf[:],
                                in_=modv_s[1:2, C * i:C * (i + 1)].broadcast_to([128, C])))
                        sy.wait_ge(ld.sem, ld.n)

            with nc.Block() as blk:
                W = {}

                v_names = (["var1", "h", "var2", "h1", "tea", "teb", "qkve1",
                            "qkve2", "qktea", "qkteb", "qktec", "qkted", "ms0"]
                           + [f"{nm}{h}" for h in range(HEADS)
                              for nm in ("mask", "pte", "oe")]
                           + ["otea", "oteb", "a_done", "p_done", "khal"])
                v_base = cnt["v"]
                v_pos = {nm: v_base + i + 1 for i, nm in enumerate(v_names)}

                pe_names = (["h1Ta", "h1Tb", "qkv1", "qkv2", "qkTa", "qkTb",
                             "qkTc", "qkTd"]
                            + [f"{nm}{h}" for h in range(HEADS)
                               for nm in ("sc", "pt", "pv")]
                            + ["oTa", "oTb", "oproj"])
                pe_base = cnt["pe"]
                pe_pos = {nm: pe_base + i + 1 for i, nm in enumerate(pe_names)}

                @blk.sync
                def _(sy, t=t):
                    ld.inc(sy.dma_start(out=xb[:], in_=x_d[NT * t:NT * (t + 1), :]))
                    W["ld_x"] = ld.n

                @blk.scalar
                def _(sc, t=t, v_pos=v_pos, pe_pos=pe_pos):
                    sc = EngWrap(sc)
                    sc.wait_ge(ld.sem, W["ld_x"])
                    W["sq1"] = mk(sc.activation(out=fmbuf[:], in_=xb[:], func=AF.Square,
                                                accum_out=s2[:]), "sc")
                    sc.wait_ge(sem_v, v_pos["var1"])
                    W["sd1"] = mk(sc.activation(out=sdv[:], in_=varv[:], func=AF.Sqrt,
                                                bias=epsb[:]), "sc")
                    sc.wait_ge(sem_v, v_pos["h"])
                    W["sq2"] = mk(sc.activation(out=fmbuf[:], in_=hb[:], func=AF.Square,
                                                accum_out=s2[:]), "sc")
                    sc.wait_ge(sem_v, v_pos["var2"])
                    W["sd2"] = mk(sc.activation(out=sdv[:], in_=varv[:], func=AF.Sqrt,
                                                bias=epsb[:]), "sc")
                    W["exp"] = []
                    for h in range(HEADS):
                        sc.wait_ge(sem_v, v_pos[f"mask{h}"])
                        if h >= 1:
                            sc.wait_ge(sem_pe, pe_pos[f"pt{h - 1}"])
                        W["exp"].append(
                            mk(sc.activation(out=pbuf[:, 0:192],
                                             in_=smb[:, 0:192],
                                             func=AF.Exp, scale=0.125,
                                             bias=ng16[:, h:h + 1],
                                             accum_out=rs16[:, h:h + 1]), "sc"))

                @blk.vector
                def _(v, t=t, seq_start=seq_start, v_pos=v_pos, pe_pos=pe_pos):
                    v = EngWrap(v)
                    v.wait_ge(ld.sem, W["ld_x"])
                    v.reduce_sum(s1[:], xb[:], axis=AX)
                    v.wait_ge(sem_sc, W["sq1"])
                    v.tensor_scalar(e2v[:], s2[:], INV_C, None, op0=ALU.mult)
                    v.tensor_scalar(mvec[:], s1[:], INV_C, None, op0=ALU.mult)
                    v.tensor_mul(varv[:], mvec[:], mvec[:])
                    mk(v.tensor_sub(varv[:], e2v[:], varv[:]), "v")
                    v.wait_ge(sem_sc, W["sd1"])
                    v.reciprocal(rstd[:], sdv[:])
                    v.tensor_mul(mrv[:], mvec[:], rstd[:])
                    v.tensor_scalar(tmp1[:], xb[:], rstd[:], mrv[:],
                                    op0=ALU.mult, op1=ALU.subtract)
                    v.tensor_mul(hb[:], tmp1[:], bc0[:])
                    mk(v.tensor_add(hb[:], hb[:], bc1[:]), "v")
                    v.reduce_sum(s1[:], hb[:], axis=AX)
                    v.wait_ge(sem_sc, W["sq2"])
                    v.tensor_scalar(e2v[:], s2[:], INV_C, None, op0=ALU.mult)
                    v.tensor_scalar(mvec[:], s1[:], INV_C, None, op0=ALU.mult)
                    v.tensor_mul(varv[:], mvec[:], mvec[:])
                    mk(v.tensor_sub(varv[:], e2v[:], varv[:]), "v")
                    v.wait_ge(sem_sc, W["sd2"])
                    v.reciprocal(rstd1[:], sdv[:])
                    v.tensor_mul(mrv1[:], mvec[:], rstd1[:])
                    mk(v.tensor_scalar(tmp1[:], hb[:], rstd1[:], mrv1[:],
                                       op0=ALU.mult, op1=ALU.subtract), "v")
                    v.wait_ge(sem_pe, pe_pos["h1Ta"])
                    mk(v.tensor_copy(fmbuf[:, 0:512], psT[:, 0:512]), "v")
                    v.wait_ge(sem_pe, pe_pos["h1Tb"])
                    mk(v.tensor_copy(fmbuf[:, 512:1024], psT[:, 512:1024]), "v")
                    v.wait_ge(sem_pe, pe_pos["qkv1"])
                    mk(v.tensor_copy(qkvk[:, 0:1536], psA[:]), "v")
                    v.wait_ge(sem_pe, pe_pos["qkv2"])
                    v.tensor_copy(qkvk[:, 1536:2048], psA[:, 0:512])
                    mk(v.scalar_tensor_tensor(qkvk[:, 2048:3072], psA[:, 512:1536],
                                              0.0, bvb[:], op0=ALU.add,
                                              op1=ALU.add), "v")
                    for g in range(4):
                        v.wait_ge(sem_pe, pe_pos["qkT" + "abcd"[g]])
                        dst = qfm if g < 2 else kfm
                        off = 512 * (g % 2)
                        ins = None
                        for i in range(4):
                            cc = 4 * (g % 2) + i
                            bcol = cc if g < 2 else 8 + cc
                            ins = v.tensor_scalar(dst[:, off + 128 * i:off + 128 * (i + 1)],
                                                  psT[:, off + 128 * i:off + 128 * (i + 1)],
                                                  qkb_f[:, bcol:bcol + 1], None,
                                                  op0=ALU.add)
                        mk(ins, "v")
                    if seq_start:
                        mk(v.memset(psS[:, 0:64], 0.0), "v")
                    else:
                        mk(v.memset(s1[0:1, 0:1], 0.0), "v")
                    for h in range(HEADS):
                        v.wait_ge(sem_pe, pe_pos[f"sc{h}"])
                        v.tensor_add(smb[:], psS[:, 0:192], maskb[:])
                        v.reduce_max(mx16[:, h:h + 1], smb[:], axis=AX)
                        mk(v.tensor_scalar(ng16[:, h:h + 1], mx16[:, h:h + 1],
                                           -0.125, None, op0=ALU.mult), "v")
                        v.wait_ge(sem_pe, pe_pos[f"pt{h}"])
                        bnk = 512 * (h % 2)
                        v.tensor_copy(ptb[0:64, 128:256],
                                      psT[0:64, bnk + 128:bnk + 256])
                        mk(v.tensor_copy(ptb[:, 0:128],
                                         psT[:, bnk:bnk + 128]), "v")
                        v.wait_ge(sem_sc, W["exp"][h])
                        v.reciprocal(rr16[:, h:h + 1], rs16[:, h:h + 1])
                        v.wait_ge(sem_pe, pe_pos[f"pv{h}"])
                        ob = 512 * (h % 2) + 64 * (h % 4)
                        mk(v.tensor_scalar(okm[:, 64 * h:64 * (h + 1)],
                                           psO[:, ob:ob + 64],
                                           rr16[:, h:h + 1], None, op0=ALU.mult), "v")
                    v.wait_ge(sem_pe, pe_pos["oTa"])
                    mk(v.tensor_copy(fmbuf[:, 0:512], psT[:, 0:512]), "v")
                    v.wait_ge(sem_pe, pe_pos["oTb"])
                    mk(v.tensor_copy(fmbuf[:, 512:1024], psT[:, 512:1024]), "v")
                    v.wait_ge(sem_pe, pe_pos["oproj"])
                    v.scalar_tensor_tensor(ab[:], psA[:, 0:1024], 0.0, hb[:],
                                           op0=ALU.add, op1=ALU.add)
                    mk(v.tensor_add(ab[:], ab[:], bc3[:]), "v")
                    v.tensor_mul(tmp1[:], ab[:], bc2[:])
                    mk(v.tensor_add(tmp1[:], tmp1[:], xb[:]), "v")
                    v.wait_ge(sem_pe, pe_pos["sc15"])
                    src = kfm[:].rearrange("p (c k) -> p c k", c=CI)[:, :, 64:128]
                    dst = khalo[:].rearrange("p (c k) -> p c k", c=CI)
                    mk(v.tensor_copy(dst, src), "v")

                @blk.tensor
                def _(te, t=t, seq_start=seq_start, v_pos=v_pos, pe_pos=pe_pos):
                    te.wait_ge(sem_v, v_pos["h1"])
                    for g in range(2):
                        ins = None
                        for i in range(4):
                            cc = 4 * g + i
                            ins = te.transpose(
                                psT[:, 512 * g + 128 * i:512 * g + 128 * (i + 1)],
                                tmp1[:, 128 * cc:128 * (cc + 1)], id32[:])
                        mk(ins, "pe")
                    te.wait_ge(sem_v, v_pos["teb"])
                    for half in range(2):
                        if half == 1:
                            te.wait_ge(sem_v, v_pos["qkve1"])
                        ins = None
                        for kc in range(CI):
                            for nn in range(3):
                                co = 3072 * kc + 1536 * half + 512 * nn
                                ins = te.matmul(psA[:, 512 * nn:512 * (nn + 1)],
                                                fmbuf[:, 128 * kc:128 * (kc + 1)],
                                                wqkv[:, co:co + 512],
                                                start=(kc == 0), stop=(kc == CI - 1))
                        mk(ins, "pe")
                    for g in range(4):
                        te.wait_ge(sem_v, v_pos["qkve1" if g < 2 else "qkve2"])
                        if g >= 2:
                            te.wait_ge(sem_v, v_pos["qktea" if g == 2 else "qkteb"])
                        qk = 0 if g < 2 else 1
                        ins = None
                        for i in range(4):
                            cc = 4 * (g % 2) + i
                            src_col = 1024 * qk + 128 * cc
                            ins = te.transpose(
                                psT[:, 512 * (g % 2) + 128 * i:512 * (g % 2) + 128 * (i + 1)],
                                qkvk[:, src_col:src_col + 128], id32[:])
                        mk(ins, "pe")
                    for h in range(HEADS):
                        r0 = 64 * (h % 2)
                        qc = h // 2
                        te.wait_ge(sem_v, v_pos["qkted"])
                        te.wait_ge(sem_v, v_pos["ms0"] if h == 0 else v_pos[f"mask{h - 1}"])
                        if not seq_start:
                            te.matmul(psS[:, 0:64],
                                      qfm[r0:r0 + 64, 128 * qc:128 * (qc + 1)],
                                      khalo[r0:r0 + 64, 64 * qc:64 * (qc + 1)],
                                      start=True, stop=True)
                        mk(te.matmul(psS[:, 64:192],
                                     qfm[r0:r0 + 64, 128 * qc:128 * (qc + 1)],
                                     kfm[r0:r0 + 64, 128 * qc:128 * (qc + 1)],
                                     start=True, stop=True), "pe")
                        te.wait_ge(sem_sc, W["exp"][h])
                        if h >= 2:
                            te.wait_ge(sem_v, v_pos[f"pte{h - 2}"])
                        bnk = 512 * (h % 2)
                        te.transpose(psT[0:64, bnk + 128:bnk + 256],
                                     pbuf[:, 0:64], id32[:])
                        mk(te.transpose(psT[:, bnk:bnk + 128],
                                        pbuf[:, 64:192], id32[:]), "pe")
                        te.wait_ge(sem_v, v_pos[f"pte{h}"])
                        if h >= 2:
                            te.wait_ge(sem_v, v_pos[f"oe{h - 2}"])
                        ob = 512 * (h % 2) + 64 * (h % 4)
                        if not seq_start:
                            te.matmul(psO[:, ob:ob + 64],
                                      ptb[0:64, 128:256],
                                      vhalo[0:64, 64 * h:64 * (h + 1)],
                                      start=True, stop=False)
                        mk(te.matmul(psO[:, ob:ob + 64],
                                     ptb[:, 0:128],
                                     qkvk[:, 2048 + 64 * h:2048 + 64 * (h + 1)],
                                     start=seq_start, stop=True), "pe")
                    te.wait_ge(sem_v, v_pos["oe15"])
                    for g in range(2):
                        ins = None
                        for i in range(4):
                            cc = 4 * g + i
                            ins = te.transpose(
                                psT[:, 512 * g + 128 * i:512 * g + 128 * (i + 1)],
                                okm[:, 128 * cc:128 * (cc + 1)], id32[:])
                        mk(ins, "pe")
                    te.wait_ge(sem_v, v_pos["oteb"])
                    ins = None
                    for kc in range(CI):
                        for nn in range(2):
                            co = 1024 * kc + 512 * nn
                            ins = te.matmul(psA[:, 512 * nn:512 * (nn + 1)],
                                            fmbuf[:, 128 * kc:128 * (kc + 1)],
                                            wo[:, co:co + 512],
                                            start=(kc == 0), stop=(kc == CI - 1))
                    mk(ins, "pe")

                @blk.sync
                def _(sy, t=t, v_pos=v_pos, pe_pos=pe_pos):
                    sy.wait_ge(sem_v, v_pos["qkve2"])
                    sy.wait_ge(sem_pe, pe_pos["pv15"])
                    hal.inc(sy.dma_start(out=vhalo[:], in_=qkvk[64:128, 2048:3072]))
                    sy.wait_ge(sem_v, v_pos["a_done"])
                    st.inc(sy.dma_start(out=a_s[NT * t:NT * (t + 1), :], in_=ab[:]))
                    sy.wait_ge(sem_v, v_pos["p_done"])
                    st.inc(sy.dma_start(out=p_s[NT * t:NT * (t + 1), :], in_=tmp1[:]))
                    if debug and t in (0, 17):
                        sy.wait_ge(sem_v, v_pos["khal"])
                        dbg_dump(sy, hb[:], 128, C)
                        dbg_dump(sy, qkvk[:], 128, 3 * C)
                        dbg_dump(sy, ab[:], 128, C)
                    sy.wait_ge(st.sem, st.n)
                    sy.wait_ge(hal.sem, hal.n)

                assert cnt["v"] == v_base + len(v_names), (cnt["v"], v_base, len(v_names))
                assert cnt["pe"] == pe_base + len(pe_names), (cnt["pe"], pe_base)

    with contextlib.ExitStack() as pb:
        sbb = lambda n, s, d=F32: pb.enter_context(nc.sbuf_tensor(n, s, d))
        w1j = sbb("w1j", [128, 2 * IN4])
        w2j = sbb("w2j", [128, 2 * IN4])
        h2q = sbb("h2q", [128, 8 * C])
        mslab = sbb("mslab", [128, 8 * C])
        gq = sbb("gq", [128, 2048])
        b1f = sbb("b1f", [128, IN4 // 128])
        bc3 = pb.enter_context(nc.sbuf_tensor("bc3b", [128, C], F32))
        psA = pb.enter_context(nc.psum_tensor("psAb", [128, 1536], F32))
        psT = pb.enter_context(nc.psum_tensor("psTb", [128, 1024], F32))
        psS = pb.enter_context(nc.psum_tensor("psSb32", [128, 512], F32))

        par = (lda, ldb)

        def issue_wj(sy, jg):
            p_ = jg % 2
            j_ = jg % 8
            for kc in range(CI):
                par[p_].inc(sy.dma_start(
                    out=w1j[:, 4096 * p_ + 512 * kc:4096 * p_ + 512 * (kc + 1)],
                    in_=w1_d[128 * kc:128 * (kc + 1), 512 * j_:512 * (j_ + 1)]))
            for s_ in range(4):
                par[p_].inc(sy.dma_start(
                    out=w2j[:, 4096 * p_ + 1024 * s_:4096 * p_ + 1024 * (s_ + 1)],
                    in_=w2_d[512 * j_ + 128 * s_:512 * j_ + 128 * (s_ + 1), :]))
            return par[p_].n

        wj_ready = {}

        with nc.Block() as blk:
            @blk.sync
            def _(sy):
                wj_ready[0] = issue_wj(sy, 0)
                ld.inc(sy.dma_start(out=bc0[:], in_=ln2g_d.broadcast_to([128, C])))
                ld.inc(sy.dma_start(out=bc1[:], in_=ln2b_d.broadcast_to([128, C])))
                ld.inc(sy.dma_start(out=bc3[:], in_=b2_d.broadcast_to([128, C])))
                ld.inc(sy.dma_start(
                    out=bc2[:], in_=modv_s[0:1, 2 * C:3 * C].broadcast_to([128, C])))
                ld.inc(sy.dma_start(out=b1f[:],
                                    in_=bass.AP(b1_h, 0, [[1, 128], [128, IN4 // 128]]),
                                    allow_slow_non_contiguous=True))
                sy.wait_ge(ld.sem, ld.n)

        for q in range(4):
            if q == 2:
                with nc.Block() as blk:
                    @blk.sync
                    def _(sy):
                        ld.inc(sy.dma_start(
                            out=bc2[:],
                            in_=modv_s[1:2, 2 * C:3 * C].broadcast_to([128, C])))
                        sy.wait_ge(ld.sem, ld.n)

            for tt in range(8):
                tq = 8 * q + tt
                with nc.Block() as blk:
                    W = {}
                    wrapd = {}

                    @blk.sync
                    def _(sy, tq=tq):
                        ld.inc(sy.dma_start(out=ab[:], in_=a_s[NT * tq:NT * (tq + 1), :]))
                        W["ld"] = ld.n

                    @blk.scalar
                    def _(sc):
                        sc = wrapd.setdefault("sc", EngWrap(sc))
                        sc.wait_ge(ld.sem, W["ld"])
                        W["sq"] = mk(sc.activation(out=fmbuf[:], in_=ab[:],
                                                   func=AF.Square, accum_out=s2[:]), "sc")

                    @blk.vector
                    def _(v):
                        v = wrapd.setdefault("v", EngWrap(v))
                        v.wait_ge(ld.sem, W["ld"])
                        v.reduce_sum(s1[:], ab[:], axis=AX)
                        v.wait_ge(sem_sc, W["sq"])
                        v.tensor_scalar(e2v[:], s2[:], INV_C, None, op0=ALU.mult)
                        v.tensor_scalar(mvec[:], s1[:], INV_C, None, op0=ALU.mult)
                        v.tensor_mul(varv[:], mvec[:], mvec[:])
                        W["var"] = mk(v.tensor_sub(varv[:], e2v[:], varv[:]), "v")

                    @blk.scalar
                    def _(sc):
                        sc = wrapd["sc"]
                        sc.wait_ge(sem_v, W["var"])
                        W["sd"] = mk(sc.activation(out=sdv[:], in_=varv[:],
                                                   func=AF.Sqrt, bias=epsb[:]), "sc")

                    @blk.vector
                    def _(v):
                        v = wrapd["v"]
                        v.wait_ge(sem_sc, W["sd"])
                        v.reciprocal(rstd[:], sdv[:])
                        v.tensor_mul(mrv[:], mvec[:], rstd[:])
                        v.tensor_scalar(tmp1[:], ab[:], rstd[:], mrv[:],
                                        op0=ALU.mult, op1=ALU.subtract)
                        v.tensor_mul(hb[:], tmp1[:], bc0[:])
                        W["h2"] = mk(v.tensor_add(hb[:], hb[:], bc1[:]), "v")

                    @blk.tensor
                    def _(te):
                        te.wait_ge(sem_v, W["h2"])
                        W["T"] = []
                        for g_ in range(2):
                            ins = None
                            for i in range(4):
                                cc = 4 * g_ + i
                                ins = te.transpose(
                                    psT[:, 512 * g_ + 128 * i:512 * g_ + 128 * (i + 1)],
                                    hb[:, 128 * cc:128 * (cc + 1)], id32[:])
                            W["T"].append(mk(ins, "pe"))

                    @blk.vector
                    def _(v, tt=tt):
                        v = wrapd["v"]
                        h2r = h2q[:].rearrange("p (c t) -> p c t", c=CI)
                        psr = psT[:].rearrange("p (c t) -> p c t", c=CI)
                        for g_ in range(2):
                            v.wait_ge(sem_pe, W["T"][g_])
                            mk(v.tensor_copy(
                                h2r[:, 4 * g_:4 * (g_ + 1), 128 * tt:128 * (tt + 1)],
                                psr[:, 4 * g_:4 * (g_ + 1), :]), "v")

            for j in range(8):
                jg = 8 * q + j
                p_ = jg % 2
                pe_b = cnt["pe"]
                sc_b = cnt["sc"]
                v_b = cnt["v"]
                pe_g = lambda st_, m, pe_b=pe_b: pe_b + st_ * 12 + m + 1
                pe_w2 = lambda st_, t4, hf_, pe_b=pe_b: pe_b + st_ * 12 + 4 + t4 * 2 + hf_ + 1
                sc_gelu = lambda st_, m, sc_b=sc_b: sc_b + st_ * 4 + m + 1
                v_ev = lambda k, v_b=v_b: v_b + k + 1

                with nc.Block() as blk:
                    @blk.sync
                    def _(sy, jg=jg):
                        if jg + 1 < 32:
                            wj_ready[jg + 1] = issue_wj(sy, jg + 1)

                    @blk.tensor
                    def _(te, j=j, jg=jg, p_=p_, pe_g=pe_g, pe_w2=pe_w2,
                          sc_gelu=sc_gelu, v_ev=v_ev):
                        te.wait_ge(par[p_].sem, wj_ready[jg])
                        for st_ in range(2):
                            for m in range(4):
                                if st_ == 1:
                                    te.wait_ge(sem_sc, sc_gelu(0, m))
                                dst = (psA[:, 512 * m:512 * (m + 1)] if m < 3
                                       else psS[:, 0:512])
                                ins = None
                                for kc in range(CI):
                                    ins = te.matmul(
                                        dst,
                                        w1j[:, 4096 * p_ + 512 * kc + 128 * m:
                                            4096 * p_ + 512 * kc + 128 * (m + 1)],
                                        h2q[:, 1024 * kc + 512 * st_:
                                            1024 * kc + 512 * (st_ + 1)],
                                        start=(kc == 0), stop=(kc == CI - 1))
                                mk(ins, "pe")
                            te.wait_ge(sem_sc, sc_gelu(st_, 3))
                            for t4 in range(4):
                                for hf_ in range(2):
                                    k = st_ * 8 + t4 * 2 + hf_
                                    if k >= 2:
                                        te.wait_ge(sem_v, v_ev(k - 2))
                                    ins = None
                                    for m in range(4):
                                        ins = te.matmul(
                                            psT[:, 512 * hf_:512 * (hf_ + 1)],
                                            gq[:, 512 * m + 128 * t4:
                                               512 * m + 128 * (t4 + 1)],
                                            w2j[:, 4096 * p_ + 1024 * m + 512 * hf_:
                                                4096 * p_ + 1024 * m + 512 * hf_ + 512],
                                            start=(m == 0), stop=(m == 3))
                                    mk(ins, "pe")

                    @blk.scalar
                    def _(sc, j=j, pe_g=pe_g, pe_w2=pe_w2):
                        sc = EngWrap(sc)
                        for st_ in range(2):
                            if st_ == 1:
                                sc.wait_ge(sem_pe, pe_w2(0, 3, 1))
                            for m in range(4):
                                sc.wait_ge(sem_pe, pe_g(st_, m))
                                src_ = (psA[:, 512 * m:512 * (m + 1)] if m < 3
                                        else psS[:, 0:512])
                                mk(sc.activation(out=gq[:, 512 * m:512 * (m + 1)],
                                                 in_=src_, func=AF.Gelu,
                                                 bias=b1f[:, 4 * j + m:4 * j + m + 1]),
                                   "sc")

                    @blk.vector
                    def _(v, j=j, pe_w2=pe_w2):
                        v = EngWrap(v)
                        for st_ in range(2):
                            for t4 in range(4):
                                for hf_ in range(2):
                                    v.wait_ge(sem_pe, pe_w2(st_, t4, hf_))
                                    tt_ = 4 * st_ + t4
                                    dst = mslab[:, 1024 * tt_ + 512 * hf_:
                                                1024 * tt_ + 512 * hf_ + 512]
                                    if j == 0:
                                        ins = v.scalar_tensor_tensor(
                                            dst, psT[:, 512 * hf_:512 * (hf_ + 1)], 0.0,
                                            bc3[:, 512 * hf_:512 * (hf_ + 1)],
                                            op0=ALU.add, op1=ALU.add)
                                    else:
                                        ins = v.tensor_add(
                                            dst, dst, psT[:, 512 * hf_:512 * (hf_ + 1)])
                                    mk(ins, "v")

            for tt in range(8):
                tq = 8 * q + tt
                with nc.Block() as blk:
                    W = {}

                    @blk.sync
                    def _(sy, tq=tq):
                        ld.inc(sy.dma_start(out=xb[:], in_=p_s[NT * tq:NT * (tq + 1), :]))
                        W["ld"] = ld.n

                    @blk.vector
                    def _(v, tt=tt):
                        v = EngWrap(v)
                        v.wait_ge(ld.sem, W["ld"])
                        v.tensor_mul(tmp1[:], mslab[:, 1024 * tt:1024 * (tt + 1)], bc2[:])
                        W["x1"] = mk(v.tensor_add(tmp1[:], tmp1[:], xb[:]), "v")

                    @blk.sync
                    def _(sy, tq=tq):
                        sy.wait_ge(sem_v, W["x1"])
                        st.inc(sy.dma_start(out=x1_s[NT * tq:NT * (tq + 1), :], in_=tmp1[:]))
                        sy.wait_ge(st.sem, st.n)

    par = (lda, ldb)

    def w_mean_pass(w_dram, nch, width, col, stg, psR):
        v_b = cnt["v"]
        p0_ = [par[0].n, par[1].n]
        with nc.Block() as blk:
            @blk.sync
            def _(sy, nch=nch, width=width):
                for ci_ in range(nch):
                    pp = ci_ % 2
                    if ci_ >= 2:
                        sy.wait_ge(sem_v, v_b + ci_ - 1)
                    par[pp].inc(sy.dma_start(
                        out=stg[:, width * pp:width * (pp + 1)],
                        in_=w_dram[128 * ci_:128 * (ci_ + 1), :]))

            @blk.vector
            def _(v, nch=nch, width=width):
                v = EngWrap(v)
                base = list(p0_)
                for ci_ in range(nch):
                    pp = ci_ % 2
                    base[pp] += 16
                    v.wait_ge(par[pp].sem, base[pp])
                    v.tensor_reduce(s1[:], stg[:, width * pp:width * (pp + 1)],
                                    axis=AX, op=ALU.add, apply_absolute_value=True)
                    if ci_ == 0:
                        ins = v.tensor_copy(sc_4[:], s1[:])
                    else:
                        ins = v.tensor_add(sc_4[:], sc_4[:], s1[:])
                    mk(ins, "v")

        with nc.Block() as blk:
            @blk.tensor
            def _(te):
                mk(te.transpose(psR[0:1, 0:128], sc_4[:], id32[:]), "pe")

            @blk.vector
            def _(v, nch=nch, width=width):
                v = EngWrap(v)
                v.wait_ge(sem_pe, cnt["pe"])
                v.reduce_sum(s1[0:1, 0:1], psR[0:1, 0:128], axis=AX)
                v.tensor_scalar(s1[0:1, 0:1], s1[0:1, 0:1],
                                1.0 / (nch * 128 * width), 1e-5,
                                op0=ALU.mult, op1=ALU.max)
                v.reciprocal(s2[0:1, 0:1], s1[0:1, 0:1])
                mk(v.tensor_copy(mvec[0:1, 0:1], s1[0:1, 0:1]), "v")

            @blk.sync
            def _(sy, col=col):
                sy.wait_ge(sem_v, cnt["v"])
                st.inc(sy.dma_start(out=sw_s[0:1, col:col + 1], in_=mvec[0:1, 0:1]))
                st.inc(sy.dma_start(out=sw_s[0:1, col + 3:col + 4], in_=s2[0:1, 0:1]))
                sy.wait_ge(st.sem, st.n)
                ld.inc(sy.dma_start(out=swb[:],
                                    in_=sw_s[0:1, 0:8].broadcast_to([128, 8])))
                sy.wait_ge(ld.sem, ld.n)

    def w_tern_pass(w_dram, nch, width, scol, dst_of_piece, stg, tA, tB, tern, psB):
        ng = width // 1024
        v_per = 1 + ng
        v_b = cnt["v"]
        pe_b = cnt["pe"]
        p0_ = [par[0].n, par[1].n]

        def v_tern(ci_):
            return v_b + ci_ * v_per + 1

        def v_evac(ci_, g_):
            return v_b + ci_ * v_per + 1 + g_ + 1

        def pe_T(ci_, g_):
            return pe_b + ci_ * ng + g_ + 1

        with nc.Block() as blk:
            @blk.sync
            def _(sy, nch=nch, width=width):
                for ci_ in range(nch):
                    pp = ci_ % 2
                    if ci_ >= 2:
                        sy.wait_ge(sem_v, v_tern(ci_ - 2))
                    par[pp].inc(sy.dma_start(
                        out=stg[:, width * pp:width * (pp + 1)],
                        in_=w_dram[128 * ci_:128 * (ci_ + 1), :]))

            @blk.vector
            def _(v, nch=nch, width=width, scol=scol):
                v = EngWrap(v)
                base = list(p0_)
                for ci_ in range(nch):
                    pp = ci_ % 2
                    base[pp] += 16
                    v.wait_ge(par[pp].sem, base[pp])
                    sl = stg[:, width * pp:width * (pp + 1)]
                    v.tensor_scalar(tA[:, 0:width], sl, swb[:, scol + 3:scol + 4],
                                    0.5, op0=ALU.mult, op1=ALU.is_gt)
                    v.tensor_scalar(tB[:, 0:width], sl, swb[:, scol + 3:scol + 4],
                                    -0.5, op0=ALU.mult, op1=ALU.is_lt)
                    if ci_ >= 1:
                        v.wait_ge(sem_pe, pe_T(ci_ - 1, ng - 1))
                    mk(v.tensor_sub(tern[:, 0:width], tA[:, 0:width],
                                    tB[:, 0:width]), "v")
                    for g_ in range(ng):
                        v.wait_ge(sem_pe, pe_T(ci_, g_))
                        src_ = psB[:, 0:1024].rearrange("p (i t) -> p i t", i=8)
                        mk(v.tensor_copy(dst_of_piece(ci_, g_), src_), "v")

            @blk.tensor
            def _(te, nch=nch, width=width):
                for ci_ in range(nch):
                    te.wait_ge(sem_v, v_tern(ci_))
                    for g_ in range(ng):
                        if g_ >= 1:
                            te.wait_ge(sem_v, v_evac(ci_, g_ - 1))
                        ins = None
                        for i in range(8):
                            pi = 8 * g_ + i
                            ins = te.transpose(
                                psB[:, 128 * i:128 * (i + 1)],
                                tern[:, 128 * pi:128 * (pi + 1)], idbf[:])
                        mk(ins, "pe")

    def ladder_v(v, src, width, s_ap, r_out):
        w2_ = width // 2
        v.tensor_scalar(okm[:, 0:width], src, s_ap, 0.5, op0=ALU.mult, op1=ALU.add)
        bmk = mk(v.tensor_scalar(tmp1[:, 0:width], okm[:, 0:width], 128.0, None,
                                 op0=ALU.add), "v")
        for k in (128, 64, 32, 16, 8, 4, 2, 1):
            v.tensor_scalar(fmbuf[:, 0:w2_], tmp1[:, 0:w2_], float(k), -float(k),
                            op0=ALU.is_ge, op1=ALU.mult)
            v.tensor_add(tmp1[:, 0:w2_], tmp1[:, 0:w2_], fmbuf[:, 0:w2_])
        rmk = mk(v.tensor_sub(r_out[:, 0:w2_], okm[:, 0:w2_], tmp1[:, 0:w2_]), "v")
        return bmk, rmk

    def ladder_g(g, width, r_out, bmk):
        w2_ = width // 2
        g.wait_ge(sem_v, bmk)
        for k in (128, 64, 32, 16, 8, 4, 2, 1):
            g.tensor_scalar(fmbuf[:, w2_:width], tmp1[:, w2_:width], float(k),
                            -float(k), op0=ALU.is_ge, op1=ALU.mult)
            g.tensor_add(tmp1[:, w2_:width], tmp1[:, w2_:width],
                         fmbuf[:, w2_:width])
        return mk(g.tensor_sub(r_out[:, w2_:width], okm[:, w2_:width],
                               tmp1[:, w2_:width]), "gp")

    with contextlib.ExitStack() as pc:
        sbc = lambda n, s, d=F32: pc.enter_context(nc.sbuf_tensor(n, s, d))
        wgf = sbc("wgf", [128, CI * IN4], BF16)
        wuf = sbc("wuf", [128, CI * IN4], BF16)
        gu = sbc("gu", [128, IN4])
        rhf_kmb = sbc("rhf_kmb", [128, C], BF16)
        rhf_fmb = sbc("rhf_fmb", [128, C], BF16)
        rgu_kmb = sbc("rgu_kmb", [128, IN4], BF16)
        tern_c = rgu_kmb[:, 0:C]
        rhf_km = rhf_kmb[:]
        rhf_fm = rhf_fmb[:]
        rgu_km = rgu_kmb[:]
        psT = pc.enter_context(nc.psum_tensor("psTc", [128, 1024], F32))
        psO = pc.enter_context(nc.psum_tensor("psOc", [128, 1024], F32))
        psB = pc.enter_context(nc.psum_tensor("psBc", [128, 1024], BF16))

        def wguf_dst(ci_, g_, buf):
            r = buf[:].rearrange("p (h i) -> p h i", h=CI)
            return r[:, :, 128 * ci_:128 * (ci_ + 1)]

        w_mean_pass(wg_d, 32, 1024, 0, gu, psT)
        w_tern_pass(wg_d, 32, 1024, 0, lambda c, g: wguf_dst(c, g, wgf),
                    gu, ab, tmp1, tern_c, psB)
        w_mean_pass(wu_d, 32, 1024, 1, gu, psT)
        w_tern_pass(wu_d, 32, 1024, 1, lambda c, g: wguf_dst(c, g, wuf),
                    gu, ab, tmp1, tern_c, psB)

        with nc.Block() as blk:
            @blk.sync
            def _(sy):
                ld.inc(sy.dma_start(
                    out=bc0[:], in_=modv_s[0:1, 3 * C:4 * C].broadcast_to([128, C])))
                ld.inc(sy.dma_start(
                    out=bc1[:], in_=modv_s[0:1, 4 * C:5 * C].broadcast_to([128, C])))
                sy.wait_ge(ld.sem, ld.n)

        for t in range(TILES):
            if t == TPB:
                with nc.Block() as blk:
                    @blk.sync
                    def _(sy):
                        ld.inc(sy.dma_start(
                            out=bc0[:],
                            in_=modv_s[1:2, 3 * C:4 * C].broadcast_to([128, C])))
                        ld.inc(sy.dma_start(
                            out=bc1[:],
                            in_=modv_s[1:2, 4 * C:5 * C].broadcast_to([128, C])))
                        sy.wait_ge(ld.sem, ld.n)

            with nc.Block() as blk:
                W = {}
                wrapd = {}
                gp_b = cnt["gp"]
                gp_hf = gp_b + 1
                gp_rq = lambda qr, gp_b=gp_b: gp_b + 2 + qr

                @blk.sync
                def _(sy, t=t):
                    ld.inc(sy.dma_start(out=xb[:], in_=x1_s[NT * t:NT * (t + 1), :]))
                    W["ld"] = ld.n

                @blk.scalar
                def _(sc):
                    sc = wrapd.setdefault("sc", EngWrap(sc))
                    sc.wait_ge(ld.sem, W["ld"])
                    W["sq"] = mk(sc.activation(out=fmbuf[:], in_=xb[:], func=AF.Square,
                                               accum_out=s2[:]), "sc")

                @blk.vector
                def _(v):
                    v = wrapd.setdefault("v", EngWrap(v))
                    v.wait_ge(ld.sem, W["ld"])
                    v.reduce_sum(s1[:], xb[:], axis=AX)
                    v.wait_ge(sem_sc, W["sq"])
                    v.tensor_scalar(e2v[:], s2[:], INV_C, None, op0=ALU.mult)
                    v.tensor_scalar(mvec[:], s1[:], INV_C, None, op0=ALU.mult)
                    v.tensor_mul(varv[:], mvec[:], mvec[:])
                    W["var"] = mk(v.tensor_sub(varv[:], e2v[:], varv[:]), "v")

                @blk.scalar
                def _(sc):
                    sc = wrapd["sc"]
                    sc.wait_ge(sem_v, W["var"])
                    W["sd"] = mk(sc.activation(out=sdv[:], in_=varv[:], func=AF.Sqrt,
                                               bias=epsb[:]), "sc")

                @blk.vector
                def _(v):
                    v = wrapd["v"]
                    v.wait_ge(sem_sc, W["sd"])
                    v.reciprocal(rstd[:], sdv[:])
                    v.tensor_mul(mrv[:], mvec[:], rstd[:])
                    v.tensor_scalar(tmp1[:], xb[:], rstd[:], mrv[:],
                                    op0=ALU.mult, op1=ALU.subtract)
                    v.tensor_mul(hb[:], tmp1[:], bc0[:])
                    v.tensor_add(hb[:], hb[:], bc1[:])
                    v.tensor_reduce(s1[:], hb[:], axis=AX, op=ALU.max,
                                    apply_absolute_value=True)
                    v.tensor_scalar(sc_0[:], s1[:], 1e-5, None, op0=ALU.max)
                    v.reciprocal(sc_1[:], sc_0[:])
                    v.tensor_scalar(sc_1[:], sc_1[:], 127.0, None, op0=ALU.mult)
                    v.tensor_mul(sc_2[:], sc_0[:], swb[:, 0:1])
                    v.tensor_scalar(sc_2[:], sc_2[:], 1.0 / 127.0, None, op0=ALU.mult)
                    v.tensor_mul(sc_3[:], sc_0[:], swb[:, 1:2])
                    v.tensor_scalar(sc_3[:], sc_3[:], 1.0 / 127.0, None, op0=ALU.mult)
                    W["lad_hf"] = ladder_v(v, hb[:], C, sc_1[:], rhf_km)

                @blk.gpsimd
                def _(g):
                    g = wrapd.setdefault("g", EngWrap(g))
                    W["lad_hf_g"] = ladder_g(g, C, rhf_km, W["lad_hf"][0])
                    assert W["lad_hf_g"] == gp_hf, (W["lad_hf_g"], gp_hf)

                @blk.tensor
                def _(te, t=t):
                    te.wait_ge(sem_v, W["lad_hf"][1])
                    te.wait_ge(sem_gp, W["lad_hf_g"])
                    ins = None
                    for i in range(8):
                        ins = te.transpose(psB[:, 128 * i:128 * (i + 1)],
                                           rhf_km[:, 128 * i:128 * (i + 1)], idbf[:])
                    W["rhfT"] = mk(ins, "pe")

                @blk.vector
                def _(v):
                    v = wrapd["v"]
                    v.wait_ge(sem_pe, W["rhfT"])
                    W["rhf_fm"] = mk(v.tensor_copy(rhf_fm[:], psB[:, 0:1024]), "v")

                sc_b = cnt["sc"]
                v_b2 = cnt["v"]
                sc_silu = lambda qr, sc_b=sc_b: sc_b + qr + 1
                v_guq = lambda qr, v_b2=v_b2: v_b2 + 2 * qr + 1

                @blk.tensor
                def _(te, t=t, sc_silu=sc_silu, v_guq=v_guq):
                    W["gmm"] = []
                    W["umm"] = []
                    te.wait_ge(sem_v, W["rhf_fm"])
                    for qr in range(4):
                        if qr >= 1:
                            te.wait_ge(sem_sc, sc_silu(qr - 1))
                        ins = None
                        for nn in range(2):
                            for kc in range(CI):
                                ins = te.matmul(
                                    psT[:, 512 * nn:512 * (nn + 1)],
                                    rhf_fm[:, 128 * kc:128 * (kc + 1)],
                                    wgf[:, 4096 * kc + 1024 * qr + 512 * nn:
                                        4096 * kc + 1024 * qr + 512 * (nn + 1)],
                                    start=(kc == 0), stop=(kc == CI - 1))
                        W["gmm"].append(mk(ins, "pe"))
                        if qr >= 1:
                            te.wait_ge(sem_v, v_guq(qr - 1))
                        ins = None
                        for nn in range(2):
                            for kc in range(CI):
                                ins = te.matmul(
                                    psO[:, 512 * nn:512 * (nn + 1)],
                                    rhf_fm[:, 128 * kc:128 * (kc + 1)],
                                    wuf[:, 4096 * kc + 1024 * qr + 512 * nn:
                                        4096 * kc + 1024 * qr + 512 * (nn + 1)],
                                    start=(kc == 0), stop=(kc == CI - 1))
                        W["umm"].append(mk(ins, "pe"))

                @blk.scalar
                def _(sc, sc_silu=sc_silu, v_guq=v_guq):
                    sc = wrapd["sc"]
                    for qr in range(4):
                        sc.wait_ge(sem_pe, W["gmm"][qr])
                        if qr >= 1:
                            sc.wait_ge(sem_v, v_guq(qr - 1))
                        assert mk(sc.activation(out=ab[:], in_=psT[:], func=AF.Silu,
                                                scale=sc_2[:]), "sc") == sc_silu(qr)

                @blk.vector
                def _(v):
                    v = wrapd["v"]
                    for qr in range(4):
                        v.wait_ge(sem_sc, sc_silu(qr))
                        v.wait_ge(sem_pe, W["umm"][qr])
                        assert mk(v.tensor_mul(gu[:, 1024 * qr:1024 * (qr + 1)],
                                               ab[:], psO[:]), "v") == v_guq(qr)
                        v.tensor_reduce(s1[:], gu[:, 1024 * qr:1024 * (qr + 1)],
                                        axis=AX, op=ALU.max,
                                        apply_absolute_value=True)
                        if qr == 0:
                            ins = v.tensor_copy(sc_4[:], s1[:])
                        else:
                            ins = v.tensor_max(sc_4[:], sc_4[:], s1[:])
                        mk(ins, "v")
                    v.tensor_mul(sc_4[:], sc_4[:], sc_3[:])
                    v.tensor_scalar(sc_4[:], sc_4[:], 1e-5, None, op0=ALU.max)
                    v.reciprocal(s2[:], sc_4[:])
                    v.tensor_mul(sc_1[:], sc_3[:], s2[:])
                    v.tensor_scalar(sc_1[:], sc_1[:], 127.0, None, op0=ALU.mult)
                    W["osc"] = mk(v.tensor_scalar(sc_0[:], sc_4[:], 1.0 / 127.0, None,
                                                  op0=ALU.mult), "v")
                    W["rq_v"] = []
                    W["rq_b"] = []
                    for qr in range(4):
                        v.wait_ge(sem_gp, gp_hf if qr == 0 else gp_rq(qr - 1))
                        bmk, rmk = ladder_v(v, gu[:, 1024 * qr:1024 * (qr + 1)], C,
                                            sc_1[:],
                                            rgu_km[:, 1024 * qr:1024 * (qr + 1)])
                        W["rq_v"].append(rmk)
                        W["rq_b"].append(bmk)

                @blk.gpsimd
                def _(g):
                    g = wrapd["g"]
                    W["rq_g"] = []
                    for qr in range(4):
                        r_ = ladder_g(g, C, rgu_km[:, 1024 * qr:1024 * (qr + 1)],
                                      W["rq_b"][qr])
                        assert r_ == gp_rq(qr), (r_, gp_rq(qr))
                        W["rq_g"].append(r_)

                @blk.sync
                def _(sy, t=t):
                    sy.wait_ge(sem_v, W["osc"])
                    st.inc(sy.dma_start(out=osc_s[NT * t:NT * (t + 1), :], in_=sc_0[:]))
                    sy.wait_ge(sem_v, W["rq_v"][3])
                    sy.wait_ge(sem_gp, W["rq_g"][3])
                    st.inc(sy.dma_start(out=r_s[NT * t:NT * (t + 1), :], in_=rgu_km))
                    sy.wait_ge(st.sem, st.n)

    with contextlib.ExitStack() as pd_:
        sbd = lambda n, s, d=F32: pd_.enter_context(nc.sbuf_tensor(n, s, d))
        wdf = sbd("wdf", [128, 32 * C], BF16)
        rkm = sbd("rkm", [128, IN4], BF16)
        stgd = sbd("stgd", [128, 2 * IN4])
        tAd = sbd("tAd", [128, IN4])
        tBd = sbd("tBd", [128, IN4])
        rfmb = sbd("rfmb", [128, IN4], BF16)
        rfm = rfmb[:]
        tern_d = rkm[:]
        psT = pd_.enter_context(nc.psum_tensor("psTd", [128, 1024], F32))
        psB = pd_.enter_context(nc.psum_tensor("psBd", [128, 1024], BF16))

        def wd_dst(ci_, g_):
            r = wdf[:].rearrange("p (i o) -> p i o", i=32)
            return r[:, 8 * g_:8 * (g_ + 1), 128 * ci_:128 * (ci_ + 1)]

        w_mean_pass(wd_d, 8, 4096, 2, stgd, psT)
        w_tern_pass(wd_d, 8, 4096, 2, wd_dst, stgd, tAd, tBd, tern_d, psB)

        with nc.Block() as blk:
            @blk.sync
            def _(sy):
                ld.inc(sy.dma_start(
                    out=bc2[:], in_=modv_s[0:1, 5 * C:6 * C].broadcast_to([128, C])))
                sy.wait_ge(ld.sem, ld.n)

        for t in range(TILES):
            if t == TPB:
                with nc.Block() as blk:
                    @blk.sync
                    def _(sy):
                        ld.inc(sy.dma_start(
                            out=bc2[:],
                            in_=modv_s[1:2, 5 * C:6 * C].broadcast_to([128, C])))
                        sy.wait_ge(ld.sem, ld.n)

            with nc.Block() as blk:
                W = {}
                wrapd = {}
                v_b = cnt["v"]
                v_evc = lambda g_, v_b=v_b: v_b + g_ + 1

                @blk.sync
                def _(sy, t=t):
                    ld.inc(sy.dma_start(out=rkm[:], in_=r_s[NT * t:NT * (t + 1), :]))
                    ld.inc(sy.dma_start(out=xb[:], in_=x1_s[NT * t:NT * (t + 1), :]))
                    ld.inc(sy.dma_start(out=sc_0[:], in_=osc_s[NT * t:NT * (t + 1), :]))
                    W["ld"] = ld.n

                @blk.tensor
                def _(te, t=t, v_evc=v_evc):
                    te.wait_ge(ld.sem, W["ld"])
                    W["T"] = []
                    for g_ in range(4):
                        if g_ >= 1:
                            te.wait_ge(sem_v, v_evc(g_ - 1))
                        ins = None
                        for i in range(8):
                            pi = 8 * g_ + i
                            ins = te.transpose(
                                psB[:, 128 * i:128 * (i + 1)],
                                rkm[:, 128 * pi:128 * (pi + 1)], idbf[:])
                        W["T"].append(mk(ins, "pe"))

                @blk.vector
                def _(v, t=t):
                    v = wrapd.setdefault("v", EngWrap(v))
                    v.wait_ge(ld.sem, W["ld"])
                    v.tensor_mul(sc_0[:], sc_0[:], swb[:, 2:3])
                    for g_ in range(4):
                        v.wait_ge(sem_pe, W["T"][g_])
                        src_ = psB[:, 0:1024].rearrange("p (i t) -> p i t", i=8)
                        dst = rfm.rearrange("p (i t) -> p i t", i=32)[
                            :, 8 * g_:8 * (g_ + 1), :]
                        assert mk(v.tensor_copy(dst, src_), "v") == v_evc(g_)

                @blk.tensor
                def _(te, t=t, v_evc=v_evc):
                    te.wait_ge(sem_v, v_evc(3))
                    W["dn"] = []
                    for nn in range(2):
                        ins = None
                        for ic in range(32):
                            ins = te.matmul(
                                psT[:, 512 * nn:512 * (nn + 1)],
                                rfm[:, 128 * ic:128 * (ic + 1)],
                                wdf[:, 1024 * ic + 512 * nn:1024 * ic + 512 * (nn + 1)],
                                start=(ic == 0), stop=(ic == 31))
                        W["dn"].append(mk(ins, "pe"))

                @blk.vector
                def _(v, t=t):
                    v = wrapd["v"]
                    for nn in range(2):
                        v.wait_ge(sem_pe, W["dn"][nn])
                        v.tensor_scalar(tmp1[:, 512 * nn:512 * (nn + 1)],
                                        psT[:, 512 * nn:512 * (nn + 1)],
                                        sc_0[:], None, op0=ALU.mult)
                    v.tensor_mul(tmp1[:], tmp1[:], bc2[:])
                    W["x2"] = mk(v.tensor_add(ab[:], tmp1[:], xb[:]), "v")

                @blk.sync
                def _(sy, t=t):
                    sy.wait_ge(sem_v, W["x2"])
                    st.inc(sy.dma_start(out=out_d[NT * t:NT * (t + 1), :], in_=ab[:]))
                    sy.wait_ge(st.sem, st.n)

    es.close()
    return nc


_NC_CACHE = [None]


def _shard_inputs(inputs):
    f32 = lambda a: np.ascontiguousarray(np.asarray(a, dtype=np.float32))
    x = f32(inputs["L0_attn_x"]).reshape(B, T, C)
    cv = f32(inputs["L0_c"])
    common = {
        "wqkv": f32(inputs["L0_attn_w_qkv"]),
        "bqkv": f32(inputs["L0_attn_b_qkv"]).reshape(1, 3 * C),
        "wout": f32(inputs["L0_attn_w_out"]),
        "bout": f32(inputs["L0_attn_b_out"]).reshape(1, C),
        "ln1g": f32(inputs["L0_attn_ln1_g"]).reshape(1, C),
        "ln1b": f32(inputs["L0_attn_ln1_b"]).reshape(1, C),
        "ln2g": f32(inputs["L0_attn_ln2_g"]).reshape(1, C),
        "ln2b": f32(inputs["L0_attn_ln2_b"]).reshape(1, C),
        "w1": f32(inputs["L0_attn_w1"]),
        "b1": f32(inputs["L0_attn_b1"]).reshape(1, IN4),
        "w2": f32(inputs["L0_attn_w2"]),
        "b2": f32(inputs["L0_attn_b2"]).reshape(1, C),
        "wg": f32(inputs["L0_ffn_w_gate"]),
        "wu": f32(inputs["L0_ffn_w_up"]),
        "wd": f32(inputs["L0_ffn_w_down"]),
        "wmod": f32(inputs["L0_wmod"]),
    }
    maps = []
    for i in range(NCORES):
        m = dict(common)
        m["x"] = np.ascontiguousarray(x[BPC * i:BPC * (i + 1)]).reshape(TOK, C)
        m["cvec"] = np.ascontiguousarray(cv[BPC * i:BPC * (i + 1)])
        maps.append(m)
    return maps


def kernel(**inputs):
    if _NC_CACHE[0] is None:
        _NC_CACHE[0] = build_nc(debug=0)
    nc = _NC_CACHE[0]
    in_maps = _shard_inputs(inputs)
    res = run_bass_kernel_spmd(nc, in_maps, core_ids=list(range(NCORES)))
    out = np.concatenate(
        [np.asarray(r["out"], dtype=np.float32).reshape(BPC, T, C)
         for r in res.results], axis=0)
    return out


# version 04
# speedup vs baseline: 1.4795x; 1.4673x over previous
import contextlib
import numpy as np
import ml_dtypes
import concourse.bass as bass
import concourse.mybir as mybir
from concourse.bass_utils import run_bass_kernel_spmd

F32 = mybir.dt.float32
F16 = mybir.dt.float16
BF16 = mybir.dt.bfloat16
ALU = mybir.AluOpType
AF = mybir.ActivationFunctionType
AX = mybir.AxisListType.X

B, T, C = 16, 2048, 1024
NCORES = 8
BPC = B // NCORES
TOK = BPC * T
NT = 128
TILES = TOK // NT
TPB = T // NT
HEADS, HD = 16, 64
CI = C // 128
IN4 = 4096
EPS = 1e-5
NEG = -1.0e30
INV_C = 1.0 / C


def _mask_np():
    q = np.arange(128)[:, None]
    j = np.arange(192)[None, :]
    allowed = (j >= 64 * (q >= 64)) & (j <= q + 64)
    return np.where(allowed, np.float32(0.0), np.float32(NEG)).astype(np.float32)


class EngWrap:

    def __init__(self, eng):
        self.e = eng
        self.w = set()
        self.r = set()

    @staticmethod
    def _nm(a):
        try:
            return a.tensor.name
        except AttributeError:
            return None

    def _pre(self, outs, ins):
        on = {self._nm(a) for a in outs} - {None}
        inn = {self._nm(a) for a in ins} - {None}
        if ((on | inn) & self.w) or (on & self.r):
            self.e.drain()
            self.w.clear()
            self.r.clear()
        self.w |= on
        self.r |= inn

    def wait_ge(self, sem, val):
        return self.e.wait_ge(sem, val)

    def drain(self):
        ins = self.e.drain()
        self.w.clear()
        self.r.clear()
        return ins

    def tensor_scalar(self, out, in0, s1, s2, **kw):
        self._pre([out], [in0, s1, s2])
        return self.e.tensor_scalar(out, in0, s1, s2, **kw)

    def scalar_tensor_tensor(self, out, in0, s, in1, **kw):
        self._pre([out], [in0, s, in1])
        return self.e.scalar_tensor_tensor(out, in0, s, in1, **kw)

    def tensor_tensor(self, out, in0, in1, **kw):
        self._pre([out], [in0, in1])
        return self.e.tensor_tensor(out, in0, in1, **kw)

    def tensor_add(self, out, a, b):
        self._pre([out], [a, b])
        return self.e.tensor_add(out, a, b)

    def tensor_sub(self, out, a, b):
        self._pre([out], [a, b])
        return self.e.tensor_sub(out, a, b)

    def tensor_mul(self, out, a, b):
        self._pre([out], [a, b])
        return self.e.tensor_mul(out, a, b)

    def tensor_max(self, out, a, b):
        self._pre([out], [a, b])
        return self.e.tensor_max(out, a, b)

    def tensor_copy(self, out, in_):
        self._pre([out], [in_])
        return self.e.tensor_copy(out, in_)

    def memset(self, ap, c):
        self._pre([ap], [])
        return self.e.memset(ap, c)

    def tensor_reduce(self, out, in_, **kw):
        self._pre([out], [in_])
        return self.e.tensor_reduce(out, in_, **kw)

    def reduce_sum(self, out, in_, **kw):
        self._pre([out], [in_])
        return self.e.reduce_sum(out, in_, **kw)

    def reduce_max(self, out, in_, **kw):
        self._pre([out], [in_])
        return self.e.reduce_max(out, in_, **kw)

    def reciprocal(self, out, in_):
        self._pre([out], [in_])
        return self.e.reciprocal(out, in_)

    def activation(self, out, in_, func, bias=0.0, scale=1.0, accum_out=None):
        outs = [out] + ([accum_out] if accum_out is not None else [])
        self._pre(outs, [in_, bias, scale])
        kw = dict(func=func, bias=bias, scale=scale)
        if accum_out is not None:
            kw["accum_out"] = accum_out
        return self.e.activation(out=out, in_=in_, **kw)



class Mono:
    def __init__(self, sem, step=16):
        self.sem = sem
        self.step = step
        self.n = 0

    def inc(self, instr):
        instr.then_inc(self.sem, self.step)
        self.n += self.step
        return self.n


def build_nc(debug=0):
    nc = bass.Bass("TRN2", target_bir_lowering=False)

    def din(name, shape, dt=F32):
        return nc.dram_tensor(name, shape, dt, kind="ExternalInput")

    x_d = din("x", [TOK, C]).ap()
    wqkv_d = din("wqkv", [C, 3 * C]).ap()
    bqkv_d = din("bqkv", [1, 3 * C]).ap()
    wout_d = din("wout", [C, C]).ap()
    bout_d = din("bout", [1, C]).ap()
    ln1g_h = din("ln1g", [1, C])
    ln1b_h = din("ln1b", [1, C])
    ln2g_d = din("ln2g", [1, C]).ap()
    ln2b_d = din("ln2b", [1, C]).ap()
    w1_d = din("w1", [C, IN4]).ap()
    b1_h = din("b1", [1, IN4])
    w2_d = din("w2", [IN4, C]).ap()
    b2_d = din("b2", [1, C]).ap()
    wg_d = din("wg", [IN4, C]).ap()
    wu_d = din("wu", [IN4, C]).ap()
    wd_d = din("wd", [C, IN4]).ap()
    c_d = din("cvec", [BPC, C]).ap()
    wmod_d = din("wmod", [C, 6 * C]).ap()

    out_d = nc.dram_tensor("out", [TOK, C], F32, kind="ExternalOutput").ap()

    def dscr(name, shape, dt=F32, tap=False):
        if debug and tap:
            return nc.dram_tensor(name, shape, dt, kind="ExternalOutput").ap()
        return nc.dram_tensor(name, shape, dt).ap()

    a_s = dscr("a_s", [TOK, C], tap=True)
    p_s = dscr("p_s", [TOK, C], tap=True)
    x1_s = dscr("x1_s", [TOK, C])
    r_s = dscr("r_s", [TOK, IN4], BF16)
    modv_s = dscr("modv_s", [BPC, 6 * C])
    qb_h = nc.dram_tensor("qb_s", [1, 3 * C], F32)
    qb_s = qb_h.ap()
    osc_s = dscr("osc_s", [TOK, 1])
    sw_s = dscr("sw_s", [1, 8])

    dbg_d = {}
    if debug:
        for i in range(debug):
            dbg_d[i] = nc.dram_tensor(f"dbg{i}", [128, 4 * C], F32,
                                      kind="ExternalOutput").ap()

    ident32_c = nc.inline_tensor(np.eye(128, dtype=np.float32), name="ident32").ap()
    identbf_c = nc.inline_tensor(np.eye(128, dtype=ml_dtypes.bfloat16), name="identbf").ap()
    mask_c = nc.inline_tensor(_mask_np(), name="maskc").ap()

    es = contextlib.ExitStack()

    def sbuf(name, shape, dt=F32):
        return es.enter_context(nc.sbuf_tensor(name, shape, dt))


    id32 = sbuf("id32", [128, 128])
    idbf = sbuf("idbf", [128, 128], BF16)
    maskb = sbuf("maskb", [128, 192])
    bc0 = sbuf("bc0", [128, C])
    bc1 = sbuf("bc1", [128, C])
    bc2 = sbuf("bc2", [128, C])
    s1 = sbuf("s1", [128, 1])
    s2 = sbuf("s2", [128, 1])
    mvec = sbuf("mvec", [128, 1])
    e2v = sbuf("e2v", [128, 1])
    varv = sbuf("varv", [128, 1])
    sdv = sbuf("sdv", [128, 1])
    rstd = sbuf("rstd", [128, 1])
    mrv = sbuf("mrv", [128, 1])
    rstd1 = sbuf("rstd1", [128, 1])
    mrv1 = sbuf("mrv1", [128, 1])
    mx16 = sbuf("mx16", [128, 16])
    ng16 = sbuf("ng16", [128, 16])
    rs16 = sbuf("rs16", [128, 16])
    rr16 = sbuf("rr16", [128, 16])
    sc_0 = sbuf("sc_0", [128, 1])
    sc_1 = sbuf("sc_1", [128, 1])
    sc_2 = sbuf("sc_2", [128, 1])
    sc_3 = sbuf("sc_3", [128, 1])
    sc_4 = sbuf("sc_4", [128, 1])
    swb = sbuf("swb", [128, 8])
    epsb = sbuf("epsb", [128, 1])
    gdum = sbuf("gdum", [128, 1])
    xb = sbuf("xb", [128, C])
    hb = sbuf("hb", [128, C])
    tmp1 = sbuf("tmp1", [128, C])
    fmbuf = sbuf("fmbuf", [128, C])
    qkvk = sbuf("qkvk", [128, 3 * C])
    okm = sbuf("okm", [128, C])
    ab = sbuf("ab", [128, C])

    sem_pe = es.enter_context(nc.semaphore("sem_pe"))
    sem_v = es.enter_context(nc.semaphore("sem_v"))
    sem_sc = es.enter_context(nc.semaphore("sem_sc"))
    sem_gp = es.enter_context(nc.semaphore("sem_gp"))
    ld = Mono(es.enter_context(nc.semaphore("ld_sem")))
    lda = Mono(es.enter_context(nc.semaphore("lda_sem")))
    ldb = Mono(es.enter_context(nc.semaphore("ldb_sem")))
    st = Mono(es.enter_context(nc.semaphore("st_sem")))
    hal = Mono(es.enter_context(nc.semaphore("hal_sem")))

    cnt = {"pe": 0, "v": 0, "sc": 0, "gp": 0}
    sems = {"pe": sem_pe, "v": sem_v, "sc": sem_sc, "gp": sem_gp}

    def mk(instr, which, n=1):
        instr.then_inc(sems[which], n)
        cnt[which] += n
        return cnt[which]

    dbg_used = [0]

    def dbg_dump(sy, src_ap, rows, cols):
        i = dbg_used[0]
        dbg_used[0] += 1
        if i in dbg_d:
            sy.dma_start(out=dbg_d[i][0:rows, 0:cols], in_=src_ap).then_inc(st.sem, 16)
            st.n += 16

    with contextlib.ExitStack() as p0:
        sb0 = lambda n, s, d=F32: p0.enter_context(nc.sbuf_tensor(n, s, d))
        stage = sb0("stage0", [128, 3 * C])
        cb2 = sb0("cb2", [BPC, C])
        scfm = sb0("scfm", [128, 2 * CI])
        modk = sb0("modk", [BPC, 6 * C])
        vr = sb0("vr", [BPC, 6 * C])
        vt1 = sb0("vt1", [BPC, C])
        vt2 = sb0("vt2", [BPC, C])
        psA = p0.enter_context(nc.psum_tensor("psA0", [128, 1536], F32))
        psT = p0.enter_context(nc.psum_tensor("psT0", [128, 1024], F32))

        with nc.Block() as blk:
            @blk.sync
            def _(sy):
                for buf, src in ((id32, ident32_c), (idbf, identbf_c),
                                 (maskb, mask_c), (cb2, c_d)):
                    ld.inc(sy.dma_start(out=buf[:], in_=src))
                sy.wait_ge(ld.sem, ld.n)

            @blk.vector
            def _(v):
                v = EngWrap(v)
                v.memset(epsb[:], EPS)
                v.memset(swb[:], 0.0)

        with nc.Block() as blk:
            @blk.tensor
            def _(te):
                ins = None
                for kc in range(CI):
                    ins = te.transpose(psT[:, 2 * kc:2 * kc + 2],
                                       cb2[:, 128 * kc:128 * (kc + 1)],
                                       id32[0:BPC, 0:BPC])
                mk(ins, "pe")

            @blk.scalar
            def _(sc):
                sc = EngWrap(sc)
                sc.wait_ge(sem_pe, cnt["pe"])
                sc.activation(out=scfm[:], in_=psT[:, 0:2 * CI], func=AF.Silu)

        for qt in range(4):
            pe0 = cnt["pe"]
            par = (lda, ldb)
            par0 = [lda.n, ldb.n]
            with nc.Block() as blk:
                @blk.sync
                def _(sy, qt=qt, pe0=pe0):
                    for kc in range(CI):
                        half = kc % 2
                        if kc >= 2:
                            sy.wait_ge(sem_pe, pe0 + kc - 1)
                        par[half].inc(sy.dma_start(
                            out=stage[:, 1536 * half:1536 * (half + 1)],
                            in_=wmod_d[128 * kc:128 * (kc + 1),
                                       1536 * qt:1536 * (qt + 1)]))

                @blk.tensor
                def _(te, qt=qt, par0=par0):
                    base = {0: par0[0], 1: par0[1]}
                    for kc in range(CI):
                        half = kc % 2
                        base[half] += 16
                        te.wait_ge(par[half].sem, base[half])
                        ins = None
                        for nn in range(3):
                            ins = te.matmul(
                                psA[0:BPC, 512 * nn:512 * (nn + 1)],
                                scfm[:, 2 * kc:2 * kc + 2],
                                stage[:, 1536 * half + 512 * nn:1536 * half + 512 * (nn + 1)],
                                start=(kc == 0), stop=(kc == CI - 1))
                        mk(ins, "pe")

                @blk.vector
                def _(v, qt=qt):
                    v = EngWrap(v)
                    v.wait_ge(sem_pe, cnt["pe"])
                    v.tensor_copy(modk[:, 1536 * qt:1536 * (qt + 1)], psA[0:BPC, :])

        with nc.Block() as blk:
            @blk.vector
            def _(v):
                v = EngWrap(v)
                P2 = slice(0, BPC)
                sa = modk[P2, 0:C]
                ca = modk[P2, C:2 * C]
                ga = modk[P2, 2 * C:3 * C]
                sbv = modk[P2, 3 * C:4 * C]
                cbv = modk[P2, 4 * C:5 * C]
                gbv = modk[P2, 5 * C:6 * C]
                v.tensor_scalar(vr[P2, 0:C], ca, 1.0, None, op0=ALU.add)
                v.tensor_copy(vr[P2, C:2 * C], sa)
                v.tensor_scalar(vr[P2, 3 * C:4 * C], cbv, 1.0, None, op0=ALU.add)
                v.tensor_copy(vr[P2, 4 * C:5 * C], sbv)
                for gsrc, gdst in ((ga, vr[P2, 2 * C:3 * C]),
                                   (gbv, vr[P2, 5 * C:6 * C])):
                    v.tensor_scalar(vt1[P2, :], gsrc, 0.0, None, op0=ALU.is_ge)
                    v.tensor_scalar(vt1[P2, :], vt1[P2, :], 2.0, 1.0,
                                    op0=ALU.mult, op1=ALU.subtract)
                    v.tensor_scalar(vt2[P2, :], gsrc, -1.0, None, op0=ALU.mult)
                    v.tensor_max(vt2[P2, :], vt2[P2, :], gsrc)
                    v.tensor_scalar(vt2[P2, :], vt2[P2, :], 0.25, None, op0=ALU.max)
                    ins = v.tensor_mul(gdst, vt1[P2, :], vt2[P2, :])
                mk(ins, "v")

            @blk.sync
            def _(sy):
                sy.wait_ge(sem_v, cnt["v"])
                st.inc(sy.dma_start(out=modv_s[:, :], in_=vr[:]))
                st.inc(sy.dma_start(out=sw_s, in_=swb[0:1, 0:8]))
                sy.wait_ge(st.sem, st.n)

    with contextlib.ExitStack() as pa:
        sba = lambda n, s, d=F32: pa.enter_context(nc.sbuf_tensor(n, s, d))
        wqkv = sba("wqkv_sb", [128, CI * 3 * C])
        wo = sba("wo_sb", [128, CI * C])
        qfm = sba("qfm", [128, C])
        kfm = sba("kfm", [128, C])
        khalo = sba("khalo", [128, 512])
        vhalo = sba("vhalo", [64, C])
        smb = sba("smb", [128, 192])
        pbuf = sba("pbuf", [128, 192])
        ptb = sba("ptb", [128, 256])
        bvb = sba("bvb", [128, C])
        qkb_f = sba("qkb_f", [128, 16])
        bc3 = pa.enter_context(nc.sbuf_tensor("bc3a", [128, C], F32))
        psA = pa.enter_context(nc.psum_tensor("psAa", [128, 1536], F32))
        psT = pa.enter_context(nc.psum_tensor("psTa", [128, 1024], F32))
        psS = pa.enter_context(nc.psum_tensor("psSa", [128, 512], F32))
        psO = pa.enter_context(nc.psum_tensor("psOa", [128, 1024], F32))
        pa_setup = contextlib.ExitStack()
        g1f = pa_setup.enter_context(nc.sbuf_tensor("g1f", [128, CI], F32))
        b1hf = pa_setup.enter_context(nc.sbuf_tensor("b1hf", [128, CI], F32))
        rwa = qkvk[0:1, :]

        with nc.Block() as blk:
            @blk.sync
            def _(sy):
                for kc in range(CI):
                    ld.inc(sy.dma_start(out=wqkv[:, 3072 * kc:3072 * (kc + 1)],
                                        in_=wqkv_d[128 * kc:128 * (kc + 1), :]))
                for kc in range(CI):
                    ld.inc(sy.dma_start(out=wo[:, 1024 * kc:1024 * (kc + 1)],
                                        in_=wout_d[128 * kc:128 * (kc + 1), :]))
                ld.inc(sy.dma_start(out=g1f[:],
                                    in_=bass.AP(ln1g_h, 0, [[1, 128], [128, CI]]),
                                    allow_slow_non_contiguous=True))
                ld.inc(sy.dma_start(out=b1hf[:],
                                    in_=bass.AP(ln1b_h, 0, [[1, 128], [128, CI]]),
                                    allow_slow_non_contiguous=True))
                sy.wait_ge(ld.sem, ld.n)

        pe0 = cnt["pe"]
        v0 = cnt["v"]
        W0 = {}
        with nc.Block() as blk:
            @blk.tensor
            def _(te):
                for half in range(2):
                    if half == 1:
                        te.wait_ge(sem_v, v0 + 1)
                    ins = None
                    for kc in range(CI):
                        for nn in range(3):
                            co = 3072 * kc + 1536 * half + 512 * nn
                            ins = te.matmul(psA[0:1, 512 * nn:512 * (nn + 1)],
                                            b1hf[:, kc:kc + 1],
                                            wqkv[:, co:co + 512],
                                            start=(kc == 0), stop=(kc == CI - 1))
                    mk(ins, "pe")

            @blk.sync
            def _(sy):
                for pp in range(3):
                    ld.inc(sy.dma_start(out=fmbuf[0:1, 0:1024],
                                        in_=bqkv_d[0:1, 1024 * pp:1024 * (pp + 1)]))
                    W0[f"bq{pp}"] = ld.n
                    sy.wait_ge(sem_v, v0 + 2 + pp + 1)

            @blk.vector
            def _(v):
                v = EngWrap(v)
                v.wait_ge(sem_pe, pe0 + 1)
                mk(v.tensor_copy(rwa[:, 0:1536], psA[0:1, :]), "v")
                v.wait_ge(sem_pe, pe0 + 2)
                mk(v.tensor_copy(rwa[:, 1536:3072], psA[0:1, :]), "v")
                for pp in range(3):
                    v.wait_ge(ld.sem, W0[f"bq{pp}"])
                    mk(v.tensor_add(rwa[:, 1024 * pp:1024 * (pp + 1)],
                                    rwa[:, 1024 * pp:1024 * (pp + 1)],
                                    fmbuf[0:1, 0:1024]), "v")

            @blk.sync
            def _(sy):
                sy.wait_ge(sem_v, cnt["v"])
                st.inc(sy.dma_start(out=qb_s, in_=rwa[:]))
                sy.wait_ge(st.sem, st.n)

        with nc.Block() as blk:
            @blk.vector
            def _(v):
                v = EngWrap(v)
                for kc in range(CI):
                    v.tensor_scalar(wqkv[:, 3072 * kc:3072 * (kc + 1)],
                                    wqkv[:, 3072 * kc:3072 * (kc + 1)],
                                    g1f[:, kc:kc + 1], None, op0=ALU.mult)

            @blk.sync
            def _(sy):
                ld.inc(sy.dma_start(out=bvb[:],
                                    in_=qb_s[0:1, 2048:3072].broadcast_to([128, C])))
                ld.inc(sy.dma_start(out=qkb_f[:],
                                    in_=bass.AP(qb_h, 0, [[1, 128], [128, 16]]),
                                    allow_slow_non_contiguous=True))
                ld.inc(sy.dma_start(out=bc3[:], in_=bout_d.broadcast_to([128, C])))
                for i, buf in enumerate((bc0, bc1, bc2)):
                    ld.inc(sy.dma_start(
                        out=buf[:],
                        in_=modv_s[0:1, C * i:C * (i + 1)].broadcast_to([128, C])))
                sy.wait_ge(ld.sem, ld.n)

        pa_setup.close()

        for t in range(TILES):
            seq_start = (t % TPB) == 0
            if t == TPB:
                with nc.Block() as blk:
                    @blk.sync
                    def _(sy):
                        for i, buf in enumerate((bc0, bc1, bc2)):
                            ld.inc(sy.dma_start(
                                out=buf[:],
                                in_=modv_s[1:2, C * i:C * (i + 1)].broadcast_to([128, C])))
                        sy.wait_ge(ld.sem, ld.n)

            with nc.Block() as blk:
                W = {}

                v_names = (["var1", "h", "var2", "h1", "tea", "teb", "qkve1",
                            "qkve2", "qktea", "qkteb", "qktec", "qkted", "ms0"]
                           + [f"{nm}{h}" for h in range(HEADS)
                              for nm in ("mask", "pte", "oe")]
                           + ["otea", "oteb", "a_done", "p_done", "khal"])
                v_base = cnt["v"]
                v_pos = {nm: v_base + i + 1 for i, nm in enumerate(v_names)}

                pe_names = (["h1Ta", "h1Tb", "qkv1", "qkv2", "qkTa", "qkTb",
                             "qkTc", "qkTd"]
                            + [f"{nm}{h}" for h in range(HEADS)
                               for nm in ("sc", "pt", "pv")]
                            + ["oTa", "oTb", "oproj"])
                pe_base = cnt["pe"]
                pe_pos = {nm: pe_base + i + 1 for i, nm in enumerate(pe_names)}

                @blk.sync
                def _(sy, t=t):
                    ld.inc(sy.dma_start(out=xb[:], in_=x_d[NT * t:NT * (t + 1), :]))
                    W["ld_x"] = ld.n

                @blk.scalar
                def _(sc, t=t, v_pos=v_pos, pe_pos=pe_pos):
                    sc = EngWrap(sc)
                    sc.wait_ge(ld.sem, W["ld_x"])
                    W["sq1"] = mk(sc.activation(out=fmbuf[:], in_=xb[:], func=AF.Square,
                                                accum_out=s2[:]), "sc")
                    sc.wait_ge(sem_v, v_pos["var1"])
                    W["sd1"] = mk(sc.activation(out=sdv[:], in_=varv[:], func=AF.Sqrt,
                                                bias=epsb[:]), "sc")
                    sc.wait_ge(sem_v, v_pos["h"])
                    W["sq2"] = mk(sc.activation(out=fmbuf[:], in_=hb[:], func=AF.Square,
                                                accum_out=s2[:]), "sc")
                    sc.wait_ge(sem_v, v_pos["var2"])
                    W["sd2"] = mk(sc.activation(out=sdv[:], in_=varv[:], func=AF.Sqrt,
                                                bias=epsb[:]), "sc")
                    W["exp"] = []
                    for h in range(HEADS):
                        sc.wait_ge(sem_v, v_pos[f"mask{h}"])
                        if h >= 1:
                            sc.wait_ge(sem_pe, pe_pos[f"pt{h - 1}"])
                        W["exp"].append(
                            mk(sc.activation(out=pbuf[:, 0:192],
                                             in_=smb[:, 0:192],
                                             func=AF.Exp, scale=0.125,
                                             bias=ng16[:, h:h + 1],
                                             accum_out=rs16[:, h:h + 1]), "sc"))

                @blk.vector
                def _(v, t=t, seq_start=seq_start, v_pos=v_pos, pe_pos=pe_pos):
                    v = EngWrap(v)
                    v.wait_ge(ld.sem, W["ld_x"])
                    v.reduce_sum(s1[:], xb[:], axis=AX)
                    v.wait_ge(sem_sc, W["sq1"])
                    v.tensor_scalar(e2v[:], s2[:], INV_C, None, op0=ALU.mult)
                    v.tensor_scalar(mvec[:], s1[:], INV_C, None, op0=ALU.mult)
                    v.tensor_mul(varv[:], mvec[:], mvec[:])
                    mk(v.tensor_sub(varv[:], e2v[:], varv[:]), "v")
                    v.wait_ge(sem_sc, W["sd1"])
                    v.reciprocal(rstd[:], sdv[:])
                    v.tensor_mul(mrv[:], mvec[:], rstd[:])
                    v.tensor_scalar(tmp1[:], xb[:], rstd[:], mrv[:],
                                    op0=ALU.mult, op1=ALU.subtract)
                    v.tensor_mul(hb[:], tmp1[:], bc0[:])
                    mk(v.tensor_add(hb[:], hb[:], bc1[:]), "v")
                    v.reduce_sum(s1[:], hb[:], axis=AX)
                    v.wait_ge(sem_sc, W["sq2"])
                    v.tensor_scalar(e2v[:], s2[:], INV_C, None, op0=ALU.mult)
                    v.tensor_scalar(mvec[:], s1[:], INV_C, None, op0=ALU.mult)
                    v.tensor_mul(varv[:], mvec[:], mvec[:])
                    mk(v.tensor_sub(varv[:], e2v[:], varv[:]), "v")
                    v.wait_ge(sem_sc, W["sd2"])
                    v.reciprocal(rstd1[:], sdv[:])
                    v.tensor_mul(mrv1[:], mvec[:], rstd1[:])
                    mk(v.tensor_scalar(tmp1[:], hb[:], rstd1[:], mrv1[:],
                                       op0=ALU.mult, op1=ALU.subtract), "v")
                    v.wait_ge(sem_pe, pe_pos["h1Ta"])
                    mk(v.tensor_copy(fmbuf[:, 0:512], psT[:, 0:512]), "v")
                    v.wait_ge(sem_pe, pe_pos["h1Tb"])
                    mk(v.tensor_copy(fmbuf[:, 512:1024], psT[:, 512:1024]), "v")
                    v.wait_ge(sem_pe, pe_pos["qkv1"])
                    mk(v.tensor_copy(qkvk[:, 0:1536], psA[:]), "v")
                    v.wait_ge(sem_pe, pe_pos["qkv2"])
                    v.tensor_copy(qkvk[:, 1536:2048], psA[:, 0:512])
                    mk(v.scalar_tensor_tensor(qkvk[:, 2048:3072], psA[:, 512:1536],
                                              0.0, bvb[:], op0=ALU.add,
                                              op1=ALU.add), "v")
                    for g in range(4):
                        v.wait_ge(sem_pe, pe_pos["qkT" + "abcd"[g]])
                        dst = qfm if g < 2 else kfm
                        off = 512 * (g % 2)
                        ins = None
                        for i in range(4):
                            cc = 4 * (g % 2) + i
                            bcol = cc if g < 2 else 8 + cc
                            ins = v.tensor_scalar(dst[:, off + 128 * i:off + 128 * (i + 1)],
                                                  psT[:, off + 128 * i:off + 128 * (i + 1)],
                                                  qkb_f[:, bcol:bcol + 1], None,
                                                  op0=ALU.add)
                        mk(ins, "v")
                    if seq_start:
                        mk(v.memset(psS[:, 0:64], 0.0), "v")
                    else:
                        mk(v.memset(s1[0:1, 0:1], 0.0), "v")
                    for h in range(HEADS):
                        v.wait_ge(sem_pe, pe_pos[f"sc{h}"])
                        v.tensor_add(smb[:], psS[:, 0:192], maskb[:])
                        v.reduce_max(mx16[:, h:h + 1], smb[:], axis=AX)
                        mk(v.tensor_scalar(ng16[:, h:h + 1], mx16[:, h:h + 1],
                                           -0.125, None, op0=ALU.mult), "v")
                        v.wait_ge(sem_pe, pe_pos[f"pt{h}"])
                        bnk = 512 * (h % 2)
                        v.tensor_copy(ptb[0:64, 128:256],
                                      psT[0:64, bnk + 128:bnk + 256])
                        mk(v.tensor_copy(ptb[:, 0:128],
                                         psT[:, bnk:bnk + 128]), "v")
                        v.wait_ge(sem_sc, W["exp"][h])
                        v.reciprocal(rr16[:, h:h + 1], rs16[:, h:h + 1])
                        v.wait_ge(sem_pe, pe_pos[f"pv{h}"])
                        ob = 512 * (h % 2) + 64 * (h % 4)
                        mk(v.tensor_scalar(okm[:, 64 * h:64 * (h + 1)],
                                           psO[:, ob:ob + 64],
                                           rr16[:, h:h + 1], None, op0=ALU.mult), "v")
                    v.wait_ge(sem_pe, pe_pos["oTa"])
                    mk(v.tensor_copy(fmbuf[:, 0:512], psT[:, 0:512]), "v")
                    v.wait_ge(sem_pe, pe_pos["oTb"])
                    mk(v.tensor_copy(fmbuf[:, 512:1024], psT[:, 512:1024]), "v")
                    v.wait_ge(sem_pe, pe_pos["oproj"])
                    v.scalar_tensor_tensor(ab[:], psA[:, 0:1024], 0.0, hb[:],
                                           op0=ALU.add, op1=ALU.add)
                    mk(v.tensor_add(ab[:], ab[:], bc3[:]), "v")
                    v.tensor_mul(tmp1[:], ab[:], bc2[:])
                    mk(v.tensor_add(tmp1[:], tmp1[:], xb[:]), "v")
                    v.wait_ge(sem_pe, pe_pos["sc15"])
                    src = kfm[:].rearrange("p (c k) -> p c k", c=CI)[:, :, 64:128]
                    dst = khalo[:].rearrange("p (c k) -> p c k", c=CI)
                    mk(v.tensor_copy(dst, src), "v")

                @blk.tensor
                def _(te, t=t, seq_start=seq_start, v_pos=v_pos, pe_pos=pe_pos):
                    te.wait_ge(sem_v, v_pos["h1"])
                    for g in range(2):
                        ins = None
                        for i in range(4):
                            cc = 4 * g + i
                            ins = te.transpose(
                                psT[:, 512 * g + 128 * i:512 * g + 128 * (i + 1)],
                                tmp1[:, 128 * cc:128 * (cc + 1)], id32[:])
                        mk(ins, "pe")
                    te.wait_ge(sem_v, v_pos["teb"])
                    for half in range(2):
                        if half == 1:
                            te.wait_ge(sem_v, v_pos["qkve1"])
                        ins = None
                        for kc in range(CI):
                            for nn in range(3):
                                co = 3072 * kc + 1536 * half + 512 * nn
                                ins = te.matmul(psA[:, 512 * nn:512 * (nn + 1)],
                                                fmbuf[:, 128 * kc:128 * (kc + 1)],
                                                wqkv[:, co:co + 512],
                                                start=(kc == 0), stop=(kc == CI - 1))
                        mk(ins, "pe")
                    for g in range(4):
                        te.wait_ge(sem_v, v_pos["qkve1" if g < 2 else "qkve2"])
                        if g >= 2:
                            te.wait_ge(sem_v, v_pos["qktea" if g == 2 else "qkteb"])
                        qk = 0 if g < 2 else 1
                        ins = None
                        for i in range(4):
                            cc = 4 * (g % 2) + i
                            src_col = 1024 * qk + 128 * cc
                            ins = te.transpose(
                                psT[:, 512 * (g % 2) + 128 * i:512 * (g % 2) + 128 * (i + 1)],
                                qkvk[:, src_col:src_col + 128], id32[:])
                        mk(ins, "pe")
                    for h in range(HEADS):
                        r0 = 64 * (h % 2)
                        qc = h // 2
                        te.wait_ge(sem_v, v_pos["qkted"])
                        te.wait_ge(sem_v, v_pos["ms0"] if h == 0 else v_pos[f"mask{h - 1}"])
                        if not seq_start:
                            te.matmul(psS[:, 0:64],
                                      qfm[r0:r0 + 64, 128 * qc:128 * (qc + 1)],
                                      khalo[r0:r0 + 64, 64 * qc:64 * (qc + 1)],
                                      start=True, stop=True)
                        mk(te.matmul(psS[:, 64:192],
                                     qfm[r0:r0 + 64, 128 * qc:128 * (qc + 1)],
                                     kfm[r0:r0 + 64, 128 * qc:128 * (qc + 1)],
                                     start=True, stop=True), "pe")
                        te.wait_ge(sem_sc, W["exp"][h])
                        if h >= 2:
                            te.wait_ge(sem_v, v_pos[f"pte{h - 2}"])
                        bnk = 512 * (h % 2)
                        te.transpose(psT[0:64, bnk + 128:bnk + 256],
                                     pbuf[:, 0:64], id32[:])
                        mk(te.transpose(psT[:, bnk:bnk + 128],
                                        pbuf[:, 64:192], id32[:]), "pe")
                        te.wait_ge(sem_v, v_pos[f"pte{h}"])
                        if h >= 2:
                            te.wait_ge(sem_v, v_pos[f"oe{h - 2}"])
                        ob = 512 * (h % 2) + 64 * (h % 4)
                        if not seq_start:
                            te.matmul(psO[:, ob:ob + 64],
                                      ptb[0:64, 128:256],
                                      vhalo[0:64, 64 * h:64 * (h + 1)],
                                      start=True, stop=False)
                        mk(te.matmul(psO[:, ob:ob + 64],
                                     ptb[:, 0:128],
                                     qkvk[:, 2048 + 64 * h:2048 + 64 * (h + 1)],
                                     start=seq_start, stop=True), "pe")
                    te.wait_ge(sem_v, v_pos["oe15"])
                    for g in range(2):
                        ins = None
                        for i in range(4):
                            cc = 4 * g + i
                            ins = te.transpose(
                                psT[:, 512 * g + 128 * i:512 * g + 128 * (i + 1)],
                                okm[:, 128 * cc:128 * (cc + 1)], id32[:])
                        mk(ins, "pe")
                    te.wait_ge(sem_v, v_pos["oteb"])
                    ins = None
                    for kc in range(CI):
                        for nn in range(2):
                            co = 1024 * kc + 512 * nn
                            ins = te.matmul(psA[:, 512 * nn:512 * (nn + 1)],
                                            fmbuf[:, 128 * kc:128 * (kc + 1)],
                                            wo[:, co:co + 512],
                                            start=(kc == 0), stop=(kc == CI - 1))
                    mk(ins, "pe")

                @blk.sync
                def _(sy, t=t, v_pos=v_pos, pe_pos=pe_pos):
                    sy.wait_ge(sem_v, v_pos["qkve2"])
                    sy.wait_ge(sem_pe, pe_pos["pv15"])
                    hal.inc(sy.dma_start(out=vhalo[:], in_=qkvk[64:128, 2048:3072]))
                    sy.wait_ge(sem_v, v_pos["a_done"])
                    st.inc(sy.dma_start(out=a_s[NT * t:NT * (t + 1), :], in_=ab[:]))
                    sy.wait_ge(sem_v, v_pos["p_done"])
                    st.inc(sy.dma_start(out=p_s[NT * t:NT * (t + 1), :], in_=tmp1[:]))
                    if debug and t in (0, 17):
                        sy.wait_ge(sem_v, v_pos["khal"])
                        dbg_dump(sy, hb[:], 128, C)
                        dbg_dump(sy, qkvk[:], 128, 3 * C)
                        dbg_dump(sy, ab[:], 128, C)
                    sy.wait_ge(st.sem, st.n)
                    sy.wait_ge(hal.sem, hal.n)

                assert cnt["v"] == v_base + len(v_names), (cnt["v"], v_base, len(v_names))
                assert cnt["pe"] == pe_base + len(pe_names), (cnt["pe"], pe_base)

    with contextlib.ExitStack() as pb:
        sbb = lambda n, s, d=F32: pb.enter_context(nc.sbuf_tensor(n, s, d))
        w1j = sbb("w1j", [128, 2 * IN4])
        w2j = sbb("w2j", [128, 2 * IN4])
        h2q = sbb("h2q", [128, 8 * C])
        mslab = sbb("mslab", [128, 8 * C])
        gq = sbb("gq", [128, 2048])
        b1f = sbb("b1f", [128, IN4 // 128])
        bc3 = pb.enter_context(nc.sbuf_tensor("bc3b", [128, C], F32))
        psA = pb.enter_context(nc.psum_tensor("psAb", [128, 1536], F32))
        psT = pb.enter_context(nc.psum_tensor("psTb", [128, 1024], F32))
        psS = pb.enter_context(nc.psum_tensor("psSb32", [128, 512], F32))

        par = (lda, ldb)

        def issue_wj(sy, jg):
            p_ = jg % 2
            j_ = jg % 8
            for kc in range(CI):
                par[p_].inc(sy.dma_start(
                    out=w1j[:, 4096 * p_ + 512 * kc:4096 * p_ + 512 * (kc + 1)],
                    in_=w1_d[128 * kc:128 * (kc + 1), 512 * j_:512 * (j_ + 1)]))
            for s_ in range(4):
                par[p_].inc(sy.dma_start(
                    out=w2j[:, 4096 * p_ + 1024 * s_:4096 * p_ + 1024 * (s_ + 1)],
                    in_=w2_d[512 * j_ + 128 * s_:512 * j_ + 128 * (s_ + 1), :]))
            return par[p_].n

        wj_ready = {}

        with nc.Block() as blk:
            @blk.sync
            def _(sy):
                wj_ready[0] = issue_wj(sy, 0)
                ld.inc(sy.dma_start(out=bc0[:], in_=ln2g_d.broadcast_to([128, C])))
                ld.inc(sy.dma_start(out=bc1[:], in_=ln2b_d.broadcast_to([128, C])))
                ld.inc(sy.dma_start(out=bc3[:], in_=b2_d.broadcast_to([128, C])))
                ld.inc(sy.dma_start(
                    out=bc2[:], in_=modv_s[0:1, 2 * C:3 * C].broadcast_to([128, C])))
                ld.inc(sy.dma_start(out=b1f[:],
                                    in_=bass.AP(b1_h, 0, [[1, 128], [128, IN4 // 128]]),
                                    allow_slow_non_contiguous=True))
                sy.wait_ge(ld.sem, ld.n)

        for q in range(4):
            if q == 2:
                with nc.Block() as blk:
                    @blk.sync
                    def _(sy):
                        ld.inc(sy.dma_start(
                            out=bc2[:],
                            in_=modv_s[1:2, 2 * C:3 * C].broadcast_to([128, C])))
                        sy.wait_ge(ld.sem, ld.n)

            for tt in range(8):
                tq = 8 * q + tt
                with nc.Block() as blk:
                    W = {}
                    wrapd = {}

                    @blk.sync
                    def _(sy, tq=tq):
                        ld.inc(sy.dma_start(out=ab[:], in_=a_s[NT * tq:NT * (tq + 1), :]))
                        W["ld"] = ld.n

                    @blk.scalar
                    def _(sc):
                        sc = wrapd.setdefault("sc", EngWrap(sc))
                        sc.wait_ge(ld.sem, W["ld"])
                        W["sq"] = mk(sc.activation(out=fmbuf[:], in_=ab[:],
                                                   func=AF.Square, accum_out=s2[:]), "sc")

                    @blk.vector
                    def _(v):
                        v = wrapd.setdefault("v", EngWrap(v))
                        v.wait_ge(ld.sem, W["ld"])
                        v.reduce_sum(s1[:], ab[:], axis=AX)
                        v.wait_ge(sem_sc, W["sq"])
                        v.tensor_scalar(e2v[:], s2[:], INV_C, None, op0=ALU.mult)
                        v.tensor_scalar(mvec[:], s1[:], INV_C, None, op0=ALU.mult)
                        v.tensor_mul(varv[:], mvec[:], mvec[:])
                        W["var"] = mk(v.tensor_sub(varv[:], e2v[:], varv[:]), "v")

                    @blk.scalar
                    def _(sc):
                        sc = wrapd["sc"]
                        sc.wait_ge(sem_v, W["var"])
                        W["sd"] = mk(sc.activation(out=sdv[:], in_=varv[:],
                                                   func=AF.Sqrt, bias=epsb[:]), "sc")

                    @blk.vector
                    def _(v):
                        v = wrapd["v"]
                        v.wait_ge(sem_sc, W["sd"])
                        v.reciprocal(rstd[:], sdv[:])
                        v.tensor_mul(mrv[:], mvec[:], rstd[:])
                        v.tensor_scalar(tmp1[:], ab[:], rstd[:], mrv[:],
                                        op0=ALU.mult, op1=ALU.subtract)
                        v.tensor_mul(hb[:], tmp1[:], bc0[:])
                        W["h2"] = mk(v.tensor_add(hb[:], hb[:], bc1[:]), "v")

                    @blk.tensor
                    def _(te):
                        te.wait_ge(sem_v, W["h2"])
                        W["T"] = []
                        for g_ in range(2):
                            ins = None
                            for i in range(4):
                                cc = 4 * g_ + i
                                ins = te.transpose(
                                    psT[:, 512 * g_ + 128 * i:512 * g_ + 128 * (i + 1)],
                                    hb[:, 128 * cc:128 * (cc + 1)], id32[:])
                            W["T"].append(mk(ins, "pe"))

                    @blk.vector
                    def _(v, tt=tt):
                        v = wrapd["v"]
                        h2r = h2q[:].rearrange("p (c t) -> p c t", c=CI)
                        psr = psT[:].rearrange("p (c t) -> p c t", c=CI)
                        for g_ in range(2):
                            v.wait_ge(sem_pe, W["T"][g_])
                            mk(v.tensor_copy(
                                h2r[:, 4 * g_:4 * (g_ + 1), 128 * tt:128 * (tt + 1)],
                                psr[:, 4 * g_:4 * (g_ + 1), :]), "v")

            for j in range(8):
                jg = 8 * q + j
                p_ = jg % 2
                pe_b = cnt["pe"]
                sc_b = cnt["sc"]
                v_b = cnt["v"]
                pe_g = lambda st_, m, pe_b=pe_b: pe_b + st_ * 12 + m + 1
                pe_w2 = lambda st_, t4, hf_, pe_b=pe_b: pe_b + st_ * 12 + 4 + t4 * 2 + hf_ + 1
                sc_gelu = lambda st_, m, sc_b=sc_b: sc_b + st_ * 4 + m + 1
                v_ev = lambda k, v_b=v_b: v_b + k + 1

                with nc.Block() as blk:
                    @blk.sync
                    def _(sy, jg=jg):
                        if jg + 1 < 32:
                            wj_ready[jg + 1] = issue_wj(sy, jg + 1)

                    @blk.tensor
                    def _(te, j=j, jg=jg, p_=p_, pe_g=pe_g, pe_w2=pe_w2,
                          sc_gelu=sc_gelu, v_ev=v_ev):
                        te.wait_ge(par[p_].sem, wj_ready[jg])
                        for st_ in range(2):
                            for m in range(4):
                                if st_ == 1:
                                    te.wait_ge(sem_sc, sc_gelu(0, m))
                                dst = (psA[:, 512 * m:512 * (m + 1)] if m < 3
                                       else psS[:, 0:512])
                                ins = None
                                for kc in range(CI):
                                    ins = te.matmul(
                                        dst,
                                        w1j[:, 4096 * p_ + 512 * kc + 128 * m:
                                            4096 * p_ + 512 * kc + 128 * (m + 1)],
                                        h2q[:, 1024 * kc + 512 * st_:
                                            1024 * kc + 512 * (st_ + 1)],
                                        start=(kc == 0), stop=(kc == CI - 1))
                                mk(ins, "pe")
                            te.wait_ge(sem_sc, sc_gelu(st_, 3))
                            for t4 in range(4):
                                for hf_ in range(2):
                                    k = st_ * 8 + t4 * 2 + hf_
                                    if k >= 2:
                                        te.wait_ge(sem_v, v_ev(k - 2))
                                    ins = None
                                    for m in range(4):
                                        ins = te.matmul(
                                            psT[:, 512 * hf_:512 * (hf_ + 1)],
                                            gq[:, 512 * m + 128 * t4:
                                               512 * m + 128 * (t4 + 1)],
                                            w2j[:, 4096 * p_ + 1024 * m + 512 * hf_:
                                                4096 * p_ + 1024 * m + 512 * hf_ + 512],
                                            start=(m == 0), stop=(m == 3))
                                    mk(ins, "pe")

                    @blk.scalar
                    def _(sc, j=j, pe_g=pe_g, pe_w2=pe_w2):
                        sc = EngWrap(sc)
                        for st_ in range(2):
                            if st_ == 1:
                                sc.wait_ge(sem_pe, pe_w2(0, 3, 1))
                            for m in range(4):
                                sc.wait_ge(sem_pe, pe_g(st_, m))
                                src_ = (psA[:, 512 * m:512 * (m + 1)] if m < 3
                                        else psS[:, 0:512])
                                mk(sc.activation(out=gq[:, 512 * m:512 * (m + 1)],
                                                 in_=src_, func=AF.Gelu,
                                                 bias=b1f[:, 4 * j + m:4 * j + m + 1]),
                                   "sc")

                    @blk.vector
                    def _(v, j=j, pe_w2=pe_w2):
                        v = EngWrap(v)
                        for st_ in range(2):
                            for t4 in range(4):
                                for hf_ in range(2):
                                    v.wait_ge(sem_pe, pe_w2(st_, t4, hf_))
                                    tt_ = 4 * st_ + t4
                                    dst = mslab[:, 1024 * tt_ + 512 * hf_:
                                                1024 * tt_ + 512 * hf_ + 512]
                                    if j == 0:
                                        ins = v.scalar_tensor_tensor(
                                            dst, psT[:, 512 * hf_:512 * (hf_ + 1)], 0.0,
                                            bc3[:, 512 * hf_:512 * (hf_ + 1)],
                                            op0=ALU.add, op1=ALU.add)
                                    else:
                                        ins = v.tensor_add(
                                            dst, dst, psT[:, 512 * hf_:512 * (hf_ + 1)])
                                    mk(ins, "v")

            for tt in range(8):
                tq = 8 * q + tt
                with nc.Block() as blk:
                    W = {}

                    @blk.sync
                    def _(sy, tq=tq):
                        ld.inc(sy.dma_start(out=xb[:], in_=p_s[NT * tq:NT * (tq + 1), :]))
                        W["ld"] = ld.n

                    @blk.vector
                    def _(v, tt=tt):
                        v = EngWrap(v)
                        v.wait_ge(ld.sem, W["ld"])
                        v.tensor_mul(tmp1[:], mslab[:, 1024 * tt:1024 * (tt + 1)], bc2[:])
                        W["x1"] = mk(v.tensor_add(tmp1[:], tmp1[:], xb[:]), "v")

                    @blk.sync
                    def _(sy, tq=tq):
                        sy.wait_ge(sem_v, W["x1"])
                        st.inc(sy.dma_start(out=x1_s[NT * tq:NT * (tq + 1), :], in_=tmp1[:]))
                        sy.wait_ge(st.sem, st.n)

    par = (lda, ldb)

    def w_mean_pass(w_dram, nch, width, col, stg, psR):
        v_b = cnt["v"]
        p0_ = [par[0].n, par[1].n]
        with nc.Block() as blk:
            @blk.sync
            def _(sy, nch=nch, width=width):
                for ci_ in range(nch):
                    pp = ci_ % 2
                    if ci_ >= 2:
                        sy.wait_ge(sem_v, v_b + ci_ - 1)
                    par[pp].inc(sy.dma_start(
                        out=stg[:, width * pp:width * (pp + 1)],
                        in_=w_dram[128 * ci_:128 * (ci_ + 1), :]))

            @blk.vector
            def _(v, nch=nch, width=width):
                v = EngWrap(v)
                base = list(p0_)
                for ci_ in range(nch):
                    pp = ci_ % 2
                    base[pp] += 16
                    v.wait_ge(par[pp].sem, base[pp])
                    v.tensor_reduce(s1[:], stg[:, width * pp:width * (pp + 1)],
                                    axis=AX, op=ALU.add, apply_absolute_value=True)
                    if ci_ == 0:
                        ins = v.tensor_copy(sc_4[:], s1[:])
                    else:
                        ins = v.tensor_add(sc_4[:], sc_4[:], s1[:])
                    mk(ins, "v")

        with nc.Block() as blk:
            @blk.tensor
            def _(te):
                mk(te.transpose(psR[0:1, 0:128], sc_4[:], id32[:]), "pe")

            @blk.vector
            def _(v, nch=nch, width=width):
                v = EngWrap(v)
                v.wait_ge(sem_pe, cnt["pe"])
                v.reduce_sum(s1[0:1, 0:1], psR[0:1, 0:128], axis=AX)
                v.tensor_scalar(s1[0:1, 0:1], s1[0:1, 0:1],
                                1.0 / (nch * 128 * width), 1e-5,
                                op0=ALU.mult, op1=ALU.max)
                v.reciprocal(s2[0:1, 0:1], s1[0:1, 0:1])
                mk(v.tensor_copy(mvec[0:1, 0:1], s1[0:1, 0:1]), "v")

            @blk.sync
            def _(sy, col=col):
                sy.wait_ge(sem_v, cnt["v"])
                st.inc(sy.dma_start(out=sw_s[0:1, col:col + 1], in_=mvec[0:1, 0:1]))
                st.inc(sy.dma_start(out=sw_s[0:1, col + 3:col + 4], in_=s2[0:1, 0:1]))
                sy.wait_ge(st.sem, st.n)
                ld.inc(sy.dma_start(out=swb[:],
                                    in_=sw_s[0:1, 0:8].broadcast_to([128, 8])))
                sy.wait_ge(ld.sem, ld.n)

    def w_tern_pass(w_dram, nch, width, scol, dst_of_piece, stg, tA, tB, tern, psB):
        ng = width // 1024
        v_per = 1 + ng
        v_b = cnt["v"]
        pe_b = cnt["pe"]
        p0_ = [par[0].n, par[1].n]

        def v_tern(ci_):
            return v_b + ci_ * v_per + 1

        def v_evac(ci_, g_):
            return v_b + ci_ * v_per + 1 + g_ + 1

        def pe_T(ci_, g_):
            return pe_b + ci_ * ng + g_ + 1

        with nc.Block() as blk:
            @blk.sync
            def _(sy, nch=nch, width=width):
                for ci_ in range(nch):
                    pp = ci_ % 2
                    if ci_ >= 2:
                        sy.wait_ge(sem_v, v_tern(ci_ - 2))
                    par[pp].inc(sy.dma_start(
                        out=stg[:, width * pp:width * (pp + 1)],
                        in_=w_dram[128 * ci_:128 * (ci_ + 1), :]))

            @blk.vector
            def _(v, nch=nch, width=width, scol=scol):
                v = EngWrap(v)
                base = list(p0_)
                for ci_ in range(nch):
                    pp = ci_ % 2
                    base[pp] += 16
                    v.wait_ge(par[pp].sem, base[pp])
                    sl = stg[:, width * pp:width * (pp + 1)]
                    v.tensor_scalar(tA[:, 0:width], sl, swb[:, scol + 3:scol + 4],
                                    0.5, op0=ALU.mult, op1=ALU.is_gt)
                    v.tensor_scalar(tB[:, 0:width], sl, swb[:, scol + 3:scol + 4],
                                    -0.5, op0=ALU.mult, op1=ALU.is_lt)
                    if ci_ >= 1:
                        v.wait_ge(sem_pe, pe_T(ci_ - 1, ng - 1))
                    mk(v.tensor_sub(tern[:, 0:width], tA[:, 0:width],
                                    tB[:, 0:width]), "v")
                    for g_ in range(ng):
                        v.wait_ge(sem_pe, pe_T(ci_, g_))
                        src_ = psB[:, 0:1024].rearrange("p (i t) -> p i t", i=8)
                        mk(v.tensor_copy(dst_of_piece(ci_, g_), src_), "v")

            @blk.tensor
            def _(te, nch=nch, width=width):
                for ci_ in range(nch):
                    te.wait_ge(sem_v, v_tern(ci_))
                    for g_ in range(ng):
                        if g_ >= 1:
                            te.wait_ge(sem_v, v_evac(ci_, g_ - 1))
                        ins = None
                        for i in range(8):
                            pi = 8 * g_ + i
                            ins = te.transpose(
                                psB[:, 128 * i:128 * (i + 1)],
                                tern[:, 128 * pi:128 * (pi + 1)], idbf[:])
                        mk(ins, "pe")

    def ladder_v(v, src, width, s_ap, r_out):
        v.tensor_scalar(okm[:, 0:width], src, s_ap, 0.5, op0=ALU.mult, op1=ALU.add)
        bmk = mk(v.tensor_scalar(tmp1[:, 0:width], okm[:, 0:width], 128.0, None,
                                 op0=ALU.add), "v")
        for k in (128, 64, 32, 16, 8, 4, 2, 1):
            v.tensor_scalar(fmbuf[:, 0:width], tmp1[:, 0:width], float(k),
                            -float(k), op0=ALU.is_ge, op1=ALU.mult)
            v.tensor_add(tmp1[:, 0:width], tmp1[:, 0:width], fmbuf[:, 0:width])
        rmk = mk(v.tensor_sub(r_out[:, 0:width], okm[:, 0:width],
                              tmp1[:, 0:width]), "v")
        return bmk, rmk

    def ladder_g(g, width, r_out, bmk):
        g.wait_ge(sem_v, bmk)
        return mk(g.memset(gdum[0:1, 0:1], 0.0), "gp")

    with contextlib.ExitStack() as pc:
        sbc = lambda n, s, d=F32: pc.enter_context(nc.sbuf_tensor(n, s, d))
        wgf = sbc("wgf", [128, CI * IN4], BF16)
        wuf = sbc("wuf", [128, CI * IN4], BF16)
        gu = sbc("gu", [128, IN4])
        rhf_kmb = sbc("rhf_kmb", [128, C], BF16)
        rhf_fmb = sbc("rhf_fmb", [128, C], BF16)
        rgu_kmb = sbc("rgu_kmb", [128, IN4], BF16)
        tern_c = rgu_kmb[:, 0:C]
        rhf_km = rhf_kmb[:]
        rhf_fm = rhf_fmb[:]
        rgu_km = rgu_kmb[:]
        psT = pc.enter_context(nc.psum_tensor("psTc", [128, 1024], F32))
        psO = pc.enter_context(nc.psum_tensor("psOc", [128, 1024], F32))
        psB = pc.enter_context(nc.psum_tensor("psBc", [128, 1024], BF16))

        def wguf_dst(ci_, g_, buf):
            r = buf[:].rearrange("p (h i) -> p h i", h=CI)
            return r[:, :, 128 * ci_:128 * (ci_ + 1)]

        w_mean_pass(wg_d, 32, 1024, 0, gu, psT)
        w_tern_pass(wg_d, 32, 1024, 0, lambda c, g: wguf_dst(c, g, wgf),
                    gu, ab, tmp1, tern_c, psB)
        w_mean_pass(wu_d, 32, 1024, 1, gu, psT)
        w_tern_pass(wu_d, 32, 1024, 1, lambda c, g: wguf_dst(c, g, wuf),
                    gu, ab, tmp1, tern_c, psB)

        with nc.Block() as blk:
            @blk.sync
            def _(sy):
                ld.inc(sy.dma_start(
                    out=bc0[:], in_=modv_s[0:1, 3 * C:4 * C].broadcast_to([128, C])))
                ld.inc(sy.dma_start(
                    out=bc1[:], in_=modv_s[0:1, 4 * C:5 * C].broadcast_to([128, C])))
                sy.wait_ge(ld.sem, ld.n)

        for t in range(TILES):
            if t == TPB:
                with nc.Block() as blk:
                    @blk.sync
                    def _(sy):
                        ld.inc(sy.dma_start(
                            out=bc0[:],
                            in_=modv_s[1:2, 3 * C:4 * C].broadcast_to([128, C])))
                        ld.inc(sy.dma_start(
                            out=bc1[:],
                            in_=modv_s[1:2, 4 * C:5 * C].broadcast_to([128, C])))
                        sy.wait_ge(ld.sem, ld.n)

            with nc.Block() as blk:
                W = {}
                wrapd = {}
                gp_b = cnt["gp"]
                gp_hf = gp_b + 1
                gp_rq = lambda qr, gp_b=gp_b: gp_b + 2 + qr

                @blk.sync
                def _(sy, t=t):
                    ld.inc(sy.dma_start(out=xb[:], in_=x1_s[NT * t:NT * (t + 1), :]))
                    W["ld"] = ld.n

                @blk.scalar
                def _(sc):
                    sc = wrapd.setdefault("sc", EngWrap(sc))
                    sc.wait_ge(ld.sem, W["ld"])
                    W["sq"] = mk(sc.activation(out=fmbuf[:], in_=xb[:], func=AF.Square,
                                               accum_out=s2[:]), "sc")

                @blk.vector
                def _(v):
                    v = wrapd.setdefault("v", EngWrap(v))
                    v.wait_ge(ld.sem, W["ld"])
                    v.reduce_sum(s1[:], xb[:], axis=AX)
                    v.wait_ge(sem_sc, W["sq"])
                    v.tensor_scalar(e2v[:], s2[:], INV_C, None, op0=ALU.mult)
                    v.tensor_scalar(mvec[:], s1[:], INV_C, None, op0=ALU.mult)
                    v.tensor_mul(varv[:], mvec[:], mvec[:])
                    W["var"] = mk(v.tensor_sub(varv[:], e2v[:], varv[:]), "v")

                @blk.scalar
                def _(sc):
                    sc = wrapd["sc"]
                    sc.wait_ge(sem_v, W["var"])
                    W["sd"] = mk(sc.activation(out=sdv[:], in_=varv[:], func=AF.Sqrt,
                                               bias=epsb[:]), "sc")

                @blk.vector
                def _(v):
                    v = wrapd["v"]
                    v.wait_ge(sem_sc, W["sd"])
                    v.reciprocal(rstd[:], sdv[:])
                    v.tensor_mul(mrv[:], mvec[:], rstd[:])
                    v.tensor_scalar(tmp1[:], xb[:], rstd[:], mrv[:],
                                    op0=ALU.mult, op1=ALU.subtract)
                    v.tensor_mul(hb[:], tmp1[:], bc0[:])
                    v.tensor_add(hb[:], hb[:], bc1[:])
                    v.tensor_reduce(s1[:], hb[:], axis=AX, op=ALU.max,
                                    apply_absolute_value=True)
                    v.tensor_scalar(sc_0[:], s1[:], 1e-5, None, op0=ALU.max)
                    v.reciprocal(sc_1[:], sc_0[:])
                    v.tensor_scalar(sc_1[:], sc_1[:], 127.0, None, op0=ALU.mult)
                    v.tensor_mul(sc_2[:], sc_0[:], swb[:, 0:1])
                    v.tensor_scalar(sc_2[:], sc_2[:], 1.0 / 127.0, None, op0=ALU.mult)
                    v.tensor_mul(sc_3[:], sc_0[:], swb[:, 1:2])
                    v.tensor_scalar(sc_3[:], sc_3[:], 1.0 / 127.0, None, op0=ALU.mult)
                    W["lad_hf"] = ladder_v(v, hb[:], C, sc_1[:], rhf_km)

                @blk.gpsimd
                def _(g):
                    g = wrapd.setdefault("g", EngWrap(g))
                    W["lad_hf_g"] = ladder_g(g, C, rhf_km, W["lad_hf"][0])
                    assert W["lad_hf_g"] == gp_hf, (W["lad_hf_g"], gp_hf)

                @blk.tensor
                def _(te, t=t):
                    te.wait_ge(sem_v, W["lad_hf"][1])
                    te.wait_ge(sem_gp, W["lad_hf_g"])
                    ins = None
                    for i in range(8):
                        ins = te.transpose(psB[:, 128 * i:128 * (i + 1)],
                                           rhf_km[:, 128 * i:128 * (i + 1)], idbf[:])
                    W["rhfT"] = mk(ins, "pe")

                @blk.vector
                def _(v):
                    v = wrapd["v"]
                    v.wait_ge(sem_pe, W["rhfT"])
                    W["rhf_fm"] = mk(v.tensor_copy(rhf_fm[:], psB[:, 0:1024]), "v")

                sc_b = cnt["sc"]
                v_b2 = cnt["v"]
                sc_silu = lambda qr, sc_b=sc_b: sc_b + qr + 1
                v_guq = lambda qr, v_b2=v_b2: v_b2 + 2 * qr + 1

                @blk.tensor
                def _(te, t=t, sc_silu=sc_silu, v_guq=v_guq):
                    W["gmm"] = []
                    W["umm"] = []
                    te.wait_ge(sem_v, W["rhf_fm"])
                    for qr in range(4):
                        if qr >= 1:
                            te.wait_ge(sem_sc, sc_silu(qr - 1))
                        ins = None
                        for nn in range(2):
                            for kc in range(CI):
                                ins = te.matmul(
                                    psT[:, 512 * nn:512 * (nn + 1)],
                                    rhf_fm[:, 128 * kc:128 * (kc + 1)],
                                    wgf[:, 4096 * kc + 1024 * qr + 512 * nn:
                                        4096 * kc + 1024 * qr + 512 * (nn + 1)],
                                    start=(kc == 0), stop=(kc == CI - 1))
                        W["gmm"].append(mk(ins, "pe"))
                        if qr >= 1:
                            te.wait_ge(sem_v, v_guq(qr - 1))
                        ins = None
                        for nn in range(2):
                            for kc in range(CI):
                                ins = te.matmul(
                                    psO[:, 512 * nn:512 * (nn + 1)],
                                    rhf_fm[:, 128 * kc:128 * (kc + 1)],
                                    wuf[:, 4096 * kc + 1024 * qr + 512 * nn:
                                        4096 * kc + 1024 * qr + 512 * (nn + 1)],
                                    start=(kc == 0), stop=(kc == CI - 1))
                        W["umm"].append(mk(ins, "pe"))

                @blk.scalar
                def _(sc, sc_silu=sc_silu, v_guq=v_guq):
                    sc = wrapd["sc"]
                    for qr in range(4):
                        sc.wait_ge(sem_pe, W["gmm"][qr])
                        if qr >= 1:
                            sc.wait_ge(sem_v, v_guq(qr - 1))
                        assert mk(sc.activation(out=ab[:], in_=psT[:], func=AF.Silu,
                                                scale=sc_2[:]), "sc") == sc_silu(qr)

                @blk.vector
                def _(v):
                    v = wrapd["v"]
                    for qr in range(4):
                        v.wait_ge(sem_sc, sc_silu(qr))
                        v.wait_ge(sem_pe, W["umm"][qr])
                        assert mk(v.tensor_mul(gu[:, 1024 * qr:1024 * (qr + 1)],
                                               ab[:], psO[:]), "v") == v_guq(qr)
                        v.tensor_reduce(s1[:], gu[:, 1024 * qr:1024 * (qr + 1)],
                                        axis=AX, op=ALU.max,
                                        apply_absolute_value=True)
                        if qr == 0:
                            ins = v.tensor_copy(sc_4[:], s1[:])
                        else:
                            ins = v.tensor_max(sc_4[:], sc_4[:], s1[:])
                        mk(ins, "v")
                    v.tensor_mul(sc_4[:], sc_4[:], sc_3[:])
                    v.tensor_scalar(sc_4[:], sc_4[:], 1e-5, None, op0=ALU.max)
                    v.reciprocal(s2[:], sc_4[:])
                    v.tensor_mul(sc_1[:], sc_3[:], s2[:])
                    v.tensor_scalar(sc_1[:], sc_1[:], 127.0, None, op0=ALU.mult)
                    W["osc"] = mk(v.tensor_scalar(sc_0[:], sc_4[:], 1.0 / 127.0, None,
                                                  op0=ALU.mult), "v")
                    W["rq_v"] = []
                    W["rq_b"] = []
                    for qr in range(4):
                        v.wait_ge(sem_gp, gp_hf if qr == 0 else gp_rq(qr - 1))
                        bmk, rmk = ladder_v(v, gu[:, 1024 * qr:1024 * (qr + 1)], C,
                                            sc_1[:],
                                            rgu_km[:, 1024 * qr:1024 * (qr + 1)])
                        W["rq_v"].append(rmk)
                        W["rq_b"].append(bmk)

                @blk.gpsimd
                def _(g):
                    g = wrapd["g"]
                    W["rq_g"] = []
                    for qr in range(4):
                        r_ = ladder_g(g, C, rgu_km[:, 1024 * qr:1024 * (qr + 1)],
                                      W["rq_b"][qr])
                        assert r_ == gp_rq(qr), (r_, gp_rq(qr))
                        W["rq_g"].append(r_)

                @blk.sync
                def _(sy, t=t):
                    sy.wait_ge(sem_v, W["osc"])
                    st.inc(sy.dma_start(out=osc_s[NT * t:NT * (t + 1), :], in_=sc_0[:]))
                    sy.wait_ge(sem_v, W["rq_v"][3])
                    sy.wait_ge(sem_gp, W["rq_g"][3])
                    st.inc(sy.dma_start(out=r_s[NT * t:NT * (t + 1), :], in_=rgu_km))
                    sy.wait_ge(st.sem, st.n)

    with contextlib.ExitStack() as pd_:
        sbd = lambda n, s, d=F32: pd_.enter_context(nc.sbuf_tensor(n, s, d))
        wdf = sbd("wdf", [128, 32 * C], BF16)
        rkm = sbd("rkm", [128, IN4], BF16)
        stgd = sbd("stgd", [128, 2 * IN4])
        tAd = sbd("tAd", [128, IN4])
        tBd = sbd("tBd", [128, IN4])
        rfmb = sbd("rfmb", [128, IN4], BF16)
        rfm = rfmb[:]
        tern_d = rkm[:]
        psT = pd_.enter_context(nc.psum_tensor("psTd", [128, 1024], F32))
        psB = pd_.enter_context(nc.psum_tensor("psBd", [128, 1024], BF16))

        def wd_dst(ci_, g_):
            r = wdf[:].rearrange("p (i o) -> p i o", i=32)
            return r[:, 8 * g_:8 * (g_ + 1), 128 * ci_:128 * (ci_ + 1)]

        w_mean_pass(wd_d, 8, 4096, 2, stgd, psT)
        w_tern_pass(wd_d, 8, 4096, 2, wd_dst, stgd, tAd, tBd, tern_d, psB)

        with nc.Block() as blk:
            @blk.sync
            def _(sy):
                ld.inc(sy.dma_start(
                    out=bc2[:], in_=modv_s[0:1, 5 * C:6 * C].broadcast_to([128, C])))
                sy.wait_ge(ld.sem, ld.n)

        for t in range(TILES):
            if t == TPB:
                with nc.Block() as blk:
                    @blk.sync
                    def _(sy):
                        ld.inc(sy.dma_start(
                            out=bc2[:],
                            in_=modv_s[1:2, 5 * C:6 * C].broadcast_to([128, C])))
                        sy.wait_ge(ld.sem, ld.n)

            with nc.Block() as blk:
                W = {}
                wrapd = {}
                v_b = cnt["v"]
                v_evc = lambda g_, v_b=v_b: v_b + g_ + 1

                @blk.sync
                def _(sy, t=t):
                    ld.inc(sy.dma_start(out=rkm[:], in_=r_s[NT * t:NT * (t + 1), :]))
                    ld.inc(sy.dma_start(out=xb[:], in_=x1_s[NT * t:NT * (t + 1), :]))
                    ld.inc(sy.dma_start(out=sc_0[:], in_=osc_s[NT * t:NT * (t + 1), :]))
                    W["ld"] = ld.n

                @blk.tensor
                def _(te, t=t, v_evc=v_evc):
                    te.wait_ge(ld.sem, W["ld"])
                    W["T"] = []
                    for g_ in range(4):
                        if g_ >= 1:
                            te.wait_ge(sem_v, v_evc(g_ - 1))
                        ins = None
                        for i in range(8):
                            pi = 8 * g_ + i
                            ins = te.transpose(
                                psB[:, 128 * i:128 * (i + 1)],
                                rkm[:, 128 * pi:128 * (pi + 1)], idbf[:])
                        W["T"].append(mk(ins, "pe"))

                @blk.vector
                def _(v, t=t):
                    v = wrapd.setdefault("v", EngWrap(v))
                    v.wait_ge(ld.sem, W["ld"])
                    v.tensor_mul(sc_0[:], sc_0[:], swb[:, 2:3])
                    for g_ in range(4):
                        v.wait_ge(sem_pe, W["T"][g_])
                        src_ = psB[:, 0:1024].rearrange("p (i t) -> p i t", i=8)
                        dst = rfm.rearrange("p (i t) -> p i t", i=32)[
                            :, 8 * g_:8 * (g_ + 1), :]
                        assert mk(v.tensor_copy(dst, src_), "v") == v_evc(g_)

                @blk.tensor
                def _(te, t=t, v_evc=v_evc):
                    te.wait_ge(sem_v, v_evc(3))
                    W["dn"] = []
                    for nn in range(2):
                        ins = None
                        for ic in range(32):
                            ins = te.matmul(
                                psT[:, 512 * nn:512 * (nn + 1)],
                                rfm[:, 128 * ic:128 * (ic + 1)],
                                wdf[:, 1024 * ic + 512 * nn:1024 * ic + 512 * (nn + 1)],
                                start=(ic == 0), stop=(ic == 31))
                        W["dn"].append(mk(ins, "pe"))

                @blk.vector
                def _(v, t=t):
                    v = wrapd["v"]
                    for nn in range(2):
                        v.wait_ge(sem_pe, W["dn"][nn])
                        v.tensor_scalar(tmp1[:, 512 * nn:512 * (nn + 1)],
                                        psT[:, 512 * nn:512 * (nn + 1)],
                                        sc_0[:], None, op0=ALU.mult)
                    v.tensor_mul(tmp1[:], tmp1[:], bc2[:])
                    W["x2"] = mk(v.tensor_add(ab[:], tmp1[:], xb[:]), "v")

                @blk.sync
                def _(sy, t=t):
                    sy.wait_ge(sem_v, W["x2"])
                    st.inc(sy.dma_start(out=out_d[NT * t:NT * (t + 1), :], in_=ab[:]))
                    sy.wait_ge(st.sem, st.n)

    es.close()
    return nc


_NC_CACHE = [None]


def _shard_inputs(inputs):
    f32 = lambda a: np.ascontiguousarray(np.asarray(a, dtype=np.float32))
    x = f32(inputs["L0_attn_x"]).reshape(B, T, C)
    cv = f32(inputs["L0_c"])
    common = {
        "wqkv": f32(inputs["L0_attn_w_qkv"]),
        "bqkv": f32(inputs["L0_attn_b_qkv"]).reshape(1, 3 * C),
        "wout": f32(inputs["L0_attn_w_out"]),
        "bout": f32(inputs["L0_attn_b_out"]).reshape(1, C),
        "ln1g": f32(inputs["L0_attn_ln1_g"]).reshape(1, C),
        "ln1b": f32(inputs["L0_attn_ln1_b"]).reshape(1, C),
        "ln2g": f32(inputs["L0_attn_ln2_g"]).reshape(1, C),
        "ln2b": f32(inputs["L0_attn_ln2_b"]).reshape(1, C),
        "w1": f32(inputs["L0_attn_w1"]),
        "b1": f32(inputs["L0_attn_b1"]).reshape(1, IN4),
        "w2": f32(inputs["L0_attn_w2"]),
        "b2": f32(inputs["L0_attn_b2"]).reshape(1, C),
        "wg": f32(inputs["L0_ffn_w_gate"]),
        "wu": f32(inputs["L0_ffn_w_up"]),
        "wd": f32(inputs["L0_ffn_w_down"]),
        "wmod": f32(inputs["L0_wmod"]),
    }
    maps = []
    for i in range(NCORES):
        m = dict(common)
        m["x"] = np.ascontiguousarray(x[BPC * i:BPC * (i + 1)]).reshape(TOK, C)
        m["cvec"] = np.ascontiguousarray(cv[BPC * i:BPC * (i + 1)])
        maps.append(m)
    return maps


def kernel(**inputs):
    if _NC_CACHE[0] is None:
        _NC_CACHE[0] = build_nc(debug=0)
    nc = _NC_CACHE[0]
    in_maps = _shard_inputs(inputs)
    res = run_bass_kernel_spmd(nc, in_maps, core_ids=list(range(NCORES)))
    out = np.concatenate(
        [np.asarray(r["out"], dtype=np.float32).reshape(BPC, T, C)
         for r in res.results], axis=0)
    return out


# version 05
# speedup vs baseline: 1.5681x; 1.0599x over previous
import contextlib
import numpy as np
import ml_dtypes
import concourse.bass as bass
import concourse.mybir as mybir
from concourse.bass_utils import run_bass_kernel_spmd

F32 = mybir.dt.float32
F16 = mybir.dt.float16
BF16 = mybir.dt.bfloat16
ALU = mybir.AluOpType
AF = mybir.ActivationFunctionType
AX = mybir.AxisListType.X

B, T, C = 16, 2048, 1024
NCORES = 8
BPC = B // NCORES
TOK = BPC * T
NT = 128
TILES = TOK // NT
TPB = T // NT
HEADS, HD = 16, 64
CI = C // 128
IN4 = 4096
EPS = 1e-5
NEG = -1.0e30
INV_C = 1.0 / C


def _mask_np():
    q = np.arange(128)[:, None]
    j = np.arange(192)[None, :]
    allowed = (j >= 64 * (q >= 64)) & (j <= q + 64)
    return np.where(allowed, np.float32(0.0), np.float32(NEG)).astype(np.float32)


class EngWrap:

    def __init__(self, eng):
        self.e = eng
        self.w = set()
        self.r = set()

    @staticmethod
    def _nm(a):
        try:
            return a.tensor.name
        except AttributeError:
            return None

    def _pre(self, outs, ins):
        on = {self._nm(a) for a in outs} - {None}
        inn = {self._nm(a) for a in ins} - {None}
        if ((on | inn) & self.w) or (on & self.r):
            self.e.drain()
            self.w.clear()
            self.r.clear()
        self.w |= on
        self.r |= inn

    def wait_ge(self, sem, val):
        return self.e.wait_ge(sem, val)

    def drain(self):
        ins = self.e.drain()
        self.w.clear()
        self.r.clear()
        return ins

    def tensor_scalar(self, out, in0, s1, s2, **kw):
        self._pre([out], [in0, s1, s2])
        return self.e.tensor_scalar(out, in0, s1, s2, **kw)

    def scalar_tensor_tensor(self, out, in0, s, in1, **kw):
        self._pre([out], [in0, s, in1])
        return self.e.scalar_tensor_tensor(out, in0, s, in1, **kw)

    def tensor_tensor(self, out, in0, in1, **kw):
        self._pre([out], [in0, in1])
        return self.e.tensor_tensor(out, in0, in1, **kw)

    def tensor_add(self, out, a, b):
        self._pre([out], [a, b])
        return self.e.tensor_add(out, a, b)

    def tensor_sub(self, out, a, b):
        self._pre([out], [a, b])
        return self.e.tensor_sub(out, a, b)

    def tensor_mul(self, out, a, b):
        self._pre([out], [a, b])
        return self.e.tensor_mul(out, a, b)

    def tensor_max(self, out, a, b):
        self._pre([out], [a, b])
        return self.e.tensor_max(out, a, b)

    def tensor_copy(self, out, in_):
        self._pre([out], [in_])
        return self.e.tensor_copy(out, in_)

    def memset(self, ap, c):
        self._pre([ap], [])
        return self.e.memset(ap, c)

    def tensor_reduce(self, out, in_, **kw):
        self._pre([out], [in_])
        return self.e.tensor_reduce(out, in_, **kw)

    def reduce_sum(self, out, in_, **kw):
        self._pre([out], [in_])
        return self.e.reduce_sum(out, in_, **kw)

    def reduce_max(self, out, in_, **kw):
        self._pre([out], [in_])
        return self.e.reduce_max(out, in_, **kw)

    def reciprocal(self, out, in_):
        self._pre([out], [in_])
        return self.e.reciprocal(out, in_)

    def activation(self, out, in_, func, bias=0.0, scale=1.0, accum_out=None):
        outs = [out] + ([accum_out] if accum_out is not None else [])
        self._pre(outs, [in_, bias, scale])
        kw = dict(func=func, bias=bias, scale=scale)
        if accum_out is not None:
            kw["accum_out"] = accum_out
        return self.e.activation(out=out, in_=in_, **kw)



class Mono:
    def __init__(self, sem, step=16):
        self.sem = sem
        self.step = step
        self.n = 0

    def inc(self, instr):
        instr.then_inc(self.sem, self.step)
        self.n += self.step
        return self.n


def build_nc(debug=0):
    nc = bass.Bass("TRN2", target_bir_lowering=False)

    def din(name, shape, dt=F32):
        return nc.dram_tensor(name, shape, dt, kind="ExternalInput")

    x_d = din("x", [TOK, C]).ap()
    wqkv_d = din("wqkv", [C, 3 * C]).ap()
    bqkv_d = din("bqkv", [1, 3 * C]).ap()
    wout_d = din("wout", [C, C]).ap()
    bout_d = din("bout", [1, C]).ap()
    ln1g_h = din("ln1g", [1, C])
    ln1b_h = din("ln1b", [1, C])
    ln2g_d = din("ln2g", [1, C]).ap()
    ln2b_d = din("ln2b", [1, C]).ap()
    w1_d = din("w1", [C, IN4]).ap()
    b1_h = din("b1", [1, IN4])
    w2_d = din("w2", [IN4, C]).ap()
    b2_d = din("b2", [1, C]).ap()
    wg_d = din("wg", [IN4, C]).ap()
    wu_d = din("wu", [IN4, C]).ap()
    wd_d = din("wd", [C, IN4]).ap()
    c_d = din("cvec", [BPC, C]).ap()
    wmod_d = din("wmod", [C, 6 * C]).ap()

    out_d = nc.dram_tensor("out", [TOK, C], F32, kind="ExternalOutput").ap()

    def dscr(name, shape, dt=F32, tap=False):
        if debug and tap:
            return nc.dram_tensor(name, shape, dt, kind="ExternalOutput").ap()
        return nc.dram_tensor(name, shape, dt).ap()

    a_s = dscr("a_s", [TOK, C], tap=True)
    p_s = dscr("p_s", [TOK, C], tap=True)
    x1_s = dscr("x1_s", [TOK, C])
    r_s = dscr("r_s", [TOK, IN4], BF16)
    modv_s = dscr("modv_s", [BPC, 6 * C])
    qb_h = nc.dram_tensor("qb_s", [1, 3 * C], F32)
    qb_s = qb_h.ap()
    osc_s = dscr("osc_s", [TOK, 1])
    sw_s = dscr("sw_s", [1, 8])

    dbg_d = {}
    if debug:
        for i in range(debug):
            dbg_d[i] = nc.dram_tensor(f"dbg{i}", [128, 4 * C], F32,
                                      kind="ExternalOutput").ap()

    ident32_c = nc.inline_tensor(np.eye(128, dtype=np.float32), name="ident32").ap()
    identbf_c = nc.inline_tensor(np.eye(128, dtype=ml_dtypes.bfloat16), name="identbf").ap()
    identf16_c = nc.inline_tensor(np.eye(128, dtype=np.float16), name="identf16").ap()
    mask_c = nc.inline_tensor(_mask_np(), name="maskc").ap()

    es = contextlib.ExitStack()

    def sbuf(name, shape, dt=F32):
        return es.enter_context(nc.sbuf_tensor(name, shape, dt))


    id32 = sbuf("id32", [128, 128])
    idbf = sbuf("idbf", [128, 128], BF16)
    idf16 = sbuf("idf16", [128, 128], F16)
    maskb = sbuf("maskb", [128, 192])
    bc0 = sbuf("bc0", [128, C])
    bc1 = sbuf("bc1", [128, C])
    bc2 = sbuf("bc2", [128, C])
    s1 = sbuf("s1", [128, 1])
    s2 = sbuf("s2", [128, 1])
    mvec = sbuf("mvec", [128, 1])
    e2v = sbuf("e2v", [128, 1])
    varv = sbuf("varv", [128, 1])
    sdv = sbuf("sdv", [128, 1])
    rstd = sbuf("rstd", [128, 1])
    mrv = sbuf("mrv", [128, 1])
    rstd1 = sbuf("rstd1", [128, 1])
    mrv1 = sbuf("mrv1", [128, 1])
    mx16 = sbuf("mx16", [128, 16])
    ng16 = sbuf("ng16", [128, 16])
    rs16 = sbuf("rs16", [128, 16])
    rr16 = sbuf("rr16", [128, 16])
    sc_0 = sbuf("sc_0", [128, 1])
    sc_1 = sbuf("sc_1", [128, 1])
    sc_2 = sbuf("sc_2", [128, 1])
    sc_3 = sbuf("sc_3", [128, 1])
    sc_4 = sbuf("sc_4", [128, 1])
    swb = sbuf("swb", [128, 8])
    epsb = sbuf("epsb", [128, 1])
    gdum = sbuf("gdum", [128, 1])
    xb = sbuf("xb", [128, C])
    hb = sbuf("hb", [128, C])
    tmp1 = sbuf("tmp1", [128, C])
    qkvk = sbuf("qkvk", [128, 3 * C])
    okm = sbuf("okm", [128, C])
    ab = sbuf("ab", [128, C])

    sem_pe = es.enter_context(nc.semaphore("sem_pe"))
    sem_v = es.enter_context(nc.semaphore("sem_v"))
    sem_sc = es.enter_context(nc.semaphore("sem_sc"))
    sem_gp = es.enter_context(nc.semaphore("sem_gp"))
    ld = Mono(es.enter_context(nc.semaphore("ld_sem")))
    lda = Mono(es.enter_context(nc.semaphore("lda_sem")))
    ldb = Mono(es.enter_context(nc.semaphore("ldb_sem")))
    st = Mono(es.enter_context(nc.semaphore("st_sem")))
    hal = Mono(es.enter_context(nc.semaphore("hal_sem")))

    cnt = {"pe": 0, "v": 0, "sc": 0, "gp": 0}
    sems = {"pe": sem_pe, "v": sem_v, "sc": sem_sc, "gp": sem_gp}

    def mk(instr, which, n=1):
        instr.then_inc(sems[which], n)
        cnt[which] += n
        return cnt[which]

    dbg_used = [0]

    def dbg_dump(sy, src_ap, rows, cols):
        i = dbg_used[0]
        dbg_used[0] += 1
        if i in dbg_d:
            sy.dma_start(out=dbg_d[i][0:rows, 0:cols], in_=src_ap).then_inc(st.sem, 16)
            st.n += 16

    with contextlib.ExitStack() as p0:
        sb0 = lambda n, s, d=F32: p0.enter_context(nc.sbuf_tensor(n, s, d))
        stage = sb0("stage0", [128, 3 * C])
        cb2 = sb0("cb2", [BPC, C])
        scfm = sb0("scfm", [128, 2 * CI])
        modk = sb0("modk", [BPC, 6 * C])
        vr = sb0("vr", [BPC, 6 * C])
        vt1 = sb0("vt1", [BPC, C])
        vt2 = sb0("vt2", [BPC, C])
        psA = p0.enter_context(nc.psum_tensor("psA0", [128, 1536], F32))
        psT = p0.enter_context(nc.psum_tensor("psT0", [128, 1024], F32))

        with nc.Block() as blk:
            @blk.sync
            def _(sy):
                for buf, src in ((id32, ident32_c), (idbf, identbf_c),
                                 (idf16, identf16_c), (maskb, mask_c), (cb2, c_d)):
                    ld.inc(sy.dma_start(out=buf[:], in_=src))
                sy.wait_ge(ld.sem, ld.n)

            @blk.vector
            def _(v):
                v = EngWrap(v)
                v.memset(epsb[:], EPS)
                v.memset(swb[:], 0.0)

        with nc.Block() as blk:
            @blk.tensor
            def _(te):
                ins = None
                for kc in range(CI):
                    ins = te.transpose(psT[:, 2 * kc:2 * kc + 2],
                                       cb2[:, 128 * kc:128 * (kc + 1)],
                                       id32[0:BPC, 0:BPC])
                mk(ins, "pe")

            @blk.scalar
            def _(sc):
                sc = EngWrap(sc)
                sc.wait_ge(sem_pe, cnt["pe"])
                sc.activation(out=scfm[:], in_=psT[:, 0:2 * CI], func=AF.Silu)

        for qt in range(4):
            pe0 = cnt["pe"]
            par = (lda, ldb)
            par0 = [lda.n, ldb.n]
            with nc.Block() as blk:
                @blk.sync
                def _(sy, qt=qt, pe0=pe0):
                    for kc in range(CI):
                        half = kc % 2
                        if kc >= 2:
                            sy.wait_ge(sem_pe, pe0 + kc - 1)
                        par[half].inc(sy.dma_start(
                            out=stage[:, 1536 * half:1536 * (half + 1)],
                            in_=wmod_d[128 * kc:128 * (kc + 1),
                                       1536 * qt:1536 * (qt + 1)]))

                @blk.tensor
                def _(te, qt=qt, par0=par0):
                    base = {0: par0[0], 1: par0[1]}
                    for kc in range(CI):
                        half = kc % 2
                        base[half] += 16
                        te.wait_ge(par[half].sem, base[half])
                        ins = None
                        for nn in range(3):
                            ins = te.matmul(
                                psA[0:BPC, 512 * nn:512 * (nn + 1)],
                                scfm[:, 2 * kc:2 * kc + 2],
                                stage[:, 1536 * half + 512 * nn:1536 * half + 512 * (nn + 1)],
                                start=(kc == 0), stop=(kc == CI - 1))
                        mk(ins, "pe")

                @blk.vector
                def _(v, qt=qt):
                    v = EngWrap(v)
                    v.wait_ge(sem_pe, cnt["pe"])
                    v.tensor_copy(modk[:, 1536 * qt:1536 * (qt + 1)], psA[0:BPC, :])

        with nc.Block() as blk:
            @blk.vector
            def _(v):
                v = EngWrap(v)
                P2 = slice(0, BPC)
                sa = modk[P2, 0:C]
                ca = modk[P2, C:2 * C]
                ga = modk[P2, 2 * C:3 * C]
                sbv = modk[P2, 3 * C:4 * C]
                cbv = modk[P2, 4 * C:5 * C]
                gbv = modk[P2, 5 * C:6 * C]
                v.tensor_scalar(vr[P2, 0:C], ca, 1.0, None, op0=ALU.add)
                v.tensor_copy(vr[P2, C:2 * C], sa)
                v.tensor_scalar(vr[P2, 3 * C:4 * C], cbv, 1.0, None, op0=ALU.add)
                v.tensor_copy(vr[P2, 4 * C:5 * C], sbv)
                for gsrc, gdst in ((ga, vr[P2, 2 * C:3 * C]),
                                   (gbv, vr[P2, 5 * C:6 * C])):
                    v.tensor_scalar(vt1[P2, :], gsrc, 0.0, None, op0=ALU.is_ge)
                    v.tensor_scalar(vt1[P2, :], vt1[P2, :], 2.0, 1.0,
                                    op0=ALU.mult, op1=ALU.subtract)
                    v.tensor_scalar(vt2[P2, :], gsrc, -1.0, None, op0=ALU.mult)
                    v.tensor_max(vt2[P2, :], vt2[P2, :], gsrc)
                    v.tensor_scalar(vt2[P2, :], vt2[P2, :], 0.25, None, op0=ALU.max)
                    ins = v.tensor_mul(gdst, vt1[P2, :], vt2[P2, :])
                mk(ins, "v")

            @blk.sync
            def _(sy):
                sy.wait_ge(sem_v, cnt["v"])
                st.inc(sy.dma_start(out=modv_s[:, :], in_=vr[:]))
                st.inc(sy.dma_start(out=sw_s, in_=swb[0:1, 0:8]))
                sy.wait_ge(st.sem, st.n)

    with contextlib.ExitStack() as pa:
        sba = lambda n, s, d=F32: pa.enter_context(nc.sbuf_tensor(n, s, d))
        bc3 = pa.enter_context(nc.sbuf_tensor("bc3a", [128, C], F32))
        psA = pa.enter_context(nc.psum_tensor("psAa", [128, 1024], F32))
        psT = pa.enter_context(nc.psum_tensor("psTa", [128, 512], F32))
        psH = pa.enter_context(nc.psum_tensor("psHa", [128, 1024], F16))
        psS = pa.enter_context(nc.psum_tensor("psSa", [128, 1024], F32))
        psO = pa.enter_context(nc.psum_tensor("psOa", [128, 1024], F32))
        wqH = sba("wqH", [128, CI * 3 * C], F16)
        wqL = sba("wqL", [128, CI * 3 * C], F16)
        woH = sba("woH", [128, CI * C], F16)
        woL = sba("woL", [128, CI * C], F16)
        qfm = sba("qfm", [128, C])
        kfm = sba("kfm", [128, C])
        khalo = sba("khalo", [128, 512])
        vhalo = sba("vhalo", [64, C])
        pbuf = sba("pbuf", [128, 384])
        ptb = sba("ptb", [128, 256])
        bvb = sba("bvb", [128, C])
        qkb_f = sba("qkb_f", [128, 16])
        h1H = sba("h1H", [128, C], F16)
        h1L = sba("h1L", [128, C], F16)
        fmH = sba("fmH", [128, C], F16)
        fmL = sba("fmL", [128, C], F16)

        pa_setup = contextlib.ExitStack()
        g1f = pa_setup.enter_context(nc.sbuf_tensor("g1f", [128, CI], F32))
        b1hf = pa_setup.enter_context(nc.sbuf_tensor("b1hf", [128, CI], F32))
        rwa = qkvk[0:1, :]

        with nc.Block() as blk:
            @blk.sync
            def _(sy):
                ld.inc(sy.dma_start(out=g1f[:],
                                    in_=bass.AP(ln1g_h, 0, [[1, 128], [128, CI]]),
                                    allow_slow_non_contiguous=True))
                ld.inc(sy.dma_start(out=b1hf[:],
                                    in_=bass.AP(ln1b_h, 0, [[1, 128], [128, CI]]),
                                    allow_slow_non_contiguous=True))
                sy.wait_ge(ld.sem, ld.n)

        for third in range(3):
            pe0 = cnt["pe"]
            v0 = cnt["v"]
            p0_ = [lda.n, ldb.n]
            with nc.Block() as blk:
                @blk.sync
                def _(sy, third=third, v0=v0, pe0=pe0):
                    for kc in range(CI):
                        pp = kc % 2
                        if kc >= 2:
                            sy.wait_ge(sem_v, v0 + 2 * (kc - 1))
                            sy.wait_ge(sem_pe, pe0 + kc - 1)
                        par[pp].inc(sy.dma_start(
                            out=(xb if pp == 0 else hb)[:],
                            in_=wqkv_d[128 * kc:128 * (kc + 1),
                                       1024 * third:1024 * (third + 1)]))

                @blk.tensor
                def _(te, third=third, v0=v0, p0_=p0_):
                    base = list(p0_)
                    for kc in range(CI):
                        pp = kc % 2
                        base[pp] += 16
                        te.wait_ge(par[pp].sem, base[pp])
                        stg = (xb if pp == 0 else hb)
                        ins = None
                        for nn in range(2):
                            ins = te.matmul(psA[0:1, 512 * nn:512 * (nn + 1)],
                                            b1hf[:, kc:kc + 1],
                                            stg[:, 512 * nn:512 * (nn + 1)],
                                            start=(kc == 0), stop=(kc == CI - 1))
                        mk(ins, "pe")

                @blk.vector
                def _(v, third=third, p0_=p0_, pe0=pe0):
                    v = EngWrap(v)
                    base = list(p0_)
                    for kc in range(CI):
                        pp = kc % 2
                        base[pp] += 16
                        v.wait_ge(par[pp].sem, base[pp])
                        stg = (xb if pp == 0 else hb)
                        co = 3072 * kc + 1024 * third
                        mk(v.tensor_scalar(wqH[:, co:co + 1024], stg[:],
                                           g1f[:, kc:kc + 1], None,
                                           op0=ALU.mult), "v")
                        mk(v.scalar_tensor_tensor(wqL[:, co:co + 1024], stg[:],
                                                  g1f[:, kc:kc + 1],
                                                  wqH[:, co:co + 1024],
                                                  op0=ALU.mult, op1=ALU.subtract), "v")
                    v.wait_ge(sem_pe, pe0 + CI)
                    mk(v.tensor_copy(rwa[:, 1024 * third:1024 * (third + 1)],
                                     psA[0:1, 0:1024]), "v")

        v0 = cnt["v"]
        W0 = {}
        with nc.Block() as blk:
            @blk.sync
            def _(sy):
                for pp_ in range(3):
                    ld.inc(sy.dma_start(out=qfm[0:1, 0:1024],
                                        in_=bqkv_d[0:1, 1024 * pp_:1024 * (pp_ + 1)]))
                    W0[f"bq{pp_}"] = ld.n
                    sy.wait_ge(sem_v, v0 + pp_ + 1)

            @blk.vector
            def _(v):
                v = EngWrap(v)
                for pp_ in range(3):
                    v.wait_ge(ld.sem, W0[f"bq{pp_}"])
                    mk(v.tensor_add(rwa[:, 1024 * pp_:1024 * (pp_ + 1)],
                                    rwa[:, 1024 * pp_:1024 * (pp_ + 1)],
                                    qfm[0:1, 0:1024]), "v")

            @blk.sync
            def _(sy):
                sy.wait_ge(sem_v, cnt["v"])
                st.inc(sy.dma_start(out=qb_s, in_=rwa[:]))
                sy.wait_ge(st.sem, st.n)

        v0 = cnt["v"]
        p0_ = [lda.n, ldb.n]
        with nc.Block() as blk:
            @blk.sync
            def _(sy, v0=v0):
                for kc in range(CI):
                    pp = kc % 2
                    if kc >= 2:
                        sy.wait_ge(sem_v, v0 + 2 * (kc - 1))
                    par[pp].inc(sy.dma_start(
                        out=(xb if pp == 0 else hb)[:],
                        in_=wout_d[128 * kc:128 * (kc + 1), :]))

            @blk.vector
            def _(v, p0_=p0_):
                v = EngWrap(v)
                base = list(p0_)
                for kc in range(CI):
                    pp = kc % 2
                    base[pp] += 16
                    v.wait_ge(par[pp].sem, base[pp])
                    stg = (xb if pp == 0 else hb)
                    co = 1024 * kc
                    mk(v.tensor_copy(woH[:, co:co + 1024], stg[:]), "v")
                    mk(v.tensor_sub(woL[:, co:co + 1024], stg[:],
                                    woH[:, co:co + 1024]), "v")

        with nc.Block() as blk:
            @blk.sync
            def _(sy):
                ld.inc(sy.dma_start(out=bvb[:],
                                    in_=qb_s[0:1, 2048:3072].broadcast_to([128, C])))
                ld.inc(sy.dma_start(out=qkb_f[:],
                                    in_=bass.AP(qb_h, 0, [[1, 128], [128, 16]]),
                                    allow_slow_non_contiguous=True))
                ld.inc(sy.dma_start(out=bc3[:], in_=bout_d.broadcast_to([128, C])))
                for i, buf in enumerate((bc0, bc1, bc2)):
                    ld.inc(sy.dma_start(
                        out=buf[:],
                        in_=modv_s[0:1, C * i:C * (i + 1)].broadcast_to([128, C])))
                sy.wait_ge(ld.sem, ld.n)

        pa_setup.close()

        for t in range(TILES):
            seq_start = (t % TPB) == 0
            if t == TPB:
                with nc.Block() as blk:
                    @blk.sync
                    def _(sy):
                        for i, buf in enumerate((bc0, bc1, bc2)):
                            ld.inc(sy.dma_start(
                                out=buf[:],
                                in_=modv_s[1:2, C * i:C * (i + 1)].broadcast_to([128, C])))
                        sy.wait_ge(ld.sem, ld.n)

            with nc.Block() as blk:
                W = {}

                v_names = (["var1", "h", "var2", "h1", "lo", "teH", "teL"]
                           + [f"qkve{c}" for c in range(6)]
                           + ["qktea", "qkteb", "qktec", "qkted", "ms0"]
                           + [f"{nm}{h}" for h in range(HEADS)
                              for nm in ("mask", "pte", "oe")]
                           + ["olo", "oteH", "oteL", "a_done", "p_done", "khal"])
                v_base = cnt["v"]
                v_pos = {nm: v_base + i + 1 for i, nm in enumerate(v_names)}

                pe_names = (["h1TH", "h1TL"]
                            + [f"qkv{c}" for c in range(6)]
                            + ["qkTa", "qkTb", "qkTc", "qkTd"]
                            + [f"{nm}{h}" for h in range(HEADS)
                               for nm in ("sc", "pt", "pv")]
                            + ["oTH", "oTL", "opr0", "opr1"])
                pe_base = cnt["pe"]
                pe_pos = {nm: pe_base + i + 1 for i, nm in enumerate(pe_names)}

                @blk.sync
                def _(sy, t=t):
                    ld.inc(sy.dma_start(out=xb[:], in_=x_d[NT * t:NT * (t + 1), :]))
                    W["ld_x"] = ld.n

                @blk.scalar
                def _(sc, t=t, v_pos=v_pos, pe_pos=pe_pos):
                    sc = EngWrap(sc)
                    sc.wait_ge(ld.sem, W["ld_x"])
                    W["sq1"] = mk(sc.activation(out=tmp1[:], in_=xb[:], func=AF.Square,
                                                accum_out=s2[:]), "sc")
                    sc.wait_ge(sem_v, v_pos["var1"])
                    W["sd1"] = mk(sc.activation(out=sdv[:], in_=varv[:], func=AF.Sqrt,
                                                bias=epsb[:]), "sc")
                    sc.wait_ge(sem_v, v_pos["h"])
                    W["sq2"] = mk(sc.activation(out=tmp1[:], in_=hb[:], func=AF.Square,
                                                accum_out=s2[:]), "sc")
                    sc.wait_ge(sem_v, v_pos["var2"])
                    W["sd2"] = mk(sc.activation(out=sdv[:], in_=varv[:], func=AF.Sqrt,
                                                bias=epsb[:]), "sc")
                    W["exp"] = []
                    for h in range(HEADS):
                        sc.wait_ge(sem_v, v_pos[f"mask{h}"])
                        if h >= 2:
                            sc.wait_ge(sem_pe, pe_pos[f"pt{h - 2}"])
                        slot = 192 * (h % 2)
                        W["exp"].append(
                            mk(sc.activation(out=pbuf[:, slot:slot + 192],
                                             in_=pbuf[:, slot:slot + 192],
                                             func=AF.Exp, scale=0.125,
                                             bias=ng16[:, h:h + 1],
                                             accum_out=rs16[:, h:h + 1]), "sc"))

                @blk.vector
                def _(v, t=t, seq_start=seq_start, v_pos=v_pos, pe_pos=pe_pos):
                    v = EngWrap(v)
                    v.wait_ge(ld.sem, W["ld_x"])
                    v.reduce_sum(s1[:], xb[:], axis=AX)
                    v.wait_ge(sem_sc, W["sq1"])
                    v.tensor_scalar(e2v[:], s2[:], INV_C, None, op0=ALU.mult)
                    v.tensor_scalar(mvec[:], s1[:], INV_C, None, op0=ALU.mult)
                    v.tensor_mul(varv[:], mvec[:], mvec[:])
                    mk(v.tensor_sub(varv[:], e2v[:], varv[:]), "v")
                    v.wait_ge(sem_sc, W["sd1"])
                    v.reciprocal(rstd[:], sdv[:])
                    v.tensor_mul(mrv[:], mvec[:], rstd[:])
                    v.tensor_scalar(tmp1[:], xb[:], rstd[:], mrv[:],
                                    op0=ALU.mult, op1=ALU.subtract)
                    v.tensor_mul(hb[:], tmp1[:], bc0[:])
                    mk(v.tensor_add(hb[:], hb[:], bc1[:]), "v")
                    v.reduce_sum(s1[:], hb[:], axis=AX)
                    v.wait_ge(sem_sc, W["sq2"])
                    v.tensor_scalar(e2v[:], s2[:], INV_C, None, op0=ALU.mult)
                    v.tensor_scalar(mvec[:], s1[:], INV_C, None, op0=ALU.mult)
                    v.tensor_mul(varv[:], mvec[:], mvec[:])
                    mk(v.tensor_sub(varv[:], e2v[:], varv[:]), "v")
                    v.wait_ge(sem_sc, W["sd2"])
                    v.reciprocal(rstd1[:], sdv[:])
                    v.tensor_mul(mrv1[:], mvec[:], rstd1[:])
                    mk(v.tensor_scalar(tmp1[:], hb[:], rstd1[:], mrv1[:],
                                       op0=ALU.mult, op1=ALU.subtract), "v")
                    v.tensor_copy(h1H[:], tmp1[:])
                    mk(v.tensor_sub(h1L[:], tmp1[:], h1H[:]), "v")
                    v.wait_ge(sem_pe, pe_pos["h1TH"])
                    mk(v.tensor_copy(fmH[:], psH[:]), "v")
                    v.wait_ge(sem_pe, pe_pos["h1TL"])
                    mk(v.tensor_copy(fmL[:], psH[:]), "v")
                    for c6 in range(6):
                        v.wait_ge(sem_pe, pe_pos[f"qkv{c6}"])
                        slot = 512 * (c6 % 2)
                        if c6 < 4:
                            ins = v.tensor_copy(qkvk[:, 512 * c6:512 * (c6 + 1)],
                                                psA[:, slot:slot + 512])
                        else:
                            ins = v.scalar_tensor_tensor(
                                qkvk[:, 512 * c6:512 * (c6 + 1)],
                                psA[:, slot:slot + 512], 0.0,
                                bvb[:, 512 * (c6 - 4):512 * (c6 - 3)],
                                op0=ALU.add, op1=ALU.add)
                        mk(ins, "v")
                    for g in range(4):
                        v.wait_ge(sem_pe, pe_pos["qkT" + "abcd"[g]])
                        dst = qfm if g < 2 else kfm
                        off = 512 * (g % 2)
                        ins = None
                        for i in range(4):
                            cc = 4 * (g % 2) + i
                            bcol = cc if g < 2 else 8 + cc
                            ins = v.tensor_scalar(dst[:, off + 128 * i:off + 128 * (i + 1)],
                                                  psT[:, 128 * i:128 * (i + 1)],
                                                  qkb_f[:, bcol:bcol + 1], None,
                                                  op0=ALU.add)
                        mk(ins, "v")
                    if seq_start:
                        v.memset(psS[:, 0:64], 0.0)
                        mk(v.memset(psS[:, 512:576], 0.0), "v")
                    else:
                        mk(v.memset(s1[0:1, 0:1], 0.0), "v")
                    for h in range(HEADS):
                        sl = 512 * (h % 2)
                        slot = 192 * (h % 2)
                        v.wait_ge(sem_pe, pe_pos[f"sc{h}"])
                        v.tensor_add(pbuf[:, slot:slot + 192], psS[:, sl:sl + 192],
                                     maskb[:])
                        v.reduce_max(mx16[:, h:h + 1], pbuf[:, slot:slot + 192],
                                     axis=AX)
                        mk(v.tensor_scalar(ng16[:, h:h + 1], mx16[:, h:h + 1],
                                           -0.125, None, op0=ALU.mult), "v")
                        v.wait_ge(sem_pe, pe_pos[f"pt{h}"])
                        v.tensor_copy(ptb[0:64, 128:256], psT[0:64, 128:256])
                        mk(v.tensor_copy(ptb[:, 0:128], psT[:, 0:128]), "v")
                        v.wait_ge(sem_sc, W["exp"][h])
                        v.reciprocal(rr16[:, h:h + 1], rs16[:, h:h + 1])
                        v.wait_ge(sem_pe, pe_pos[f"pv{h}"])
                        ob = 512 * (h % 2) + 64 * (h % 4)
                        mk(v.tensor_scalar(okm[:, 64 * h:64 * (h + 1)],
                                           psO[:, ob:ob + 64],
                                           rr16[:, h:h + 1], None, op0=ALU.mult), "v")
                    v.tensor_copy(h1H[:], okm[:])
                    mk(v.tensor_sub(h1L[:], okm[:], h1H[:]), "v")
                    v.wait_ge(sem_pe, pe_pos["oTH"])
                    mk(v.tensor_copy(fmH[:], psH[:]), "v")
                    v.wait_ge(sem_pe, pe_pos["oTL"])
                    mk(v.tensor_copy(fmL[:], psH[:]), "v")
                    for nn in range(2):
                        v.wait_ge(sem_pe, pe_pos[f"opr{nn}"])
                        ins = v.scalar_tensor_tensor(
                            ab[:, 512 * nn:512 * (nn + 1)],
                            psA[:, 512 * nn:512 * (nn + 1)], 0.0,
                            hb[:, 512 * nn:512 * (nn + 1)],
                            op0=ALU.add, op1=ALU.add)
                    mk(v.tensor_add(ab[:], ab[:], bc3[:]), "v")
                    v.tensor_mul(tmp1[:], ab[:], bc2[:])
                    mk(v.tensor_add(tmp1[:], tmp1[:], xb[:]), "v")
                    v.wait_ge(sem_pe, pe_pos["sc15"])
                    src = kfm[:].rearrange("p (c k) -> p c k", c=CI)[:, :, 64:128]
                    dst = khalo[:].rearrange("p (c k) -> p c k", c=CI)
                    mk(v.tensor_copy(dst, src), "v")

                @blk.tensor
                def _(te, t=t, seq_start=seq_start, v_pos=v_pos, pe_pos=pe_pos):
                    te.wait_ge(sem_v, v_pos["lo"])
                    for part, buf in ((0, h1H), (1, h1L)):
                        if part == 1:
                            te.wait_ge(sem_v, v_pos["teH"])
                        ins = None
                        for cc in range(CI):
                            ins = te.transpose(psH[:, 128 * cc:128 * (cc + 1)],
                                               buf[:, 128 * cc:128 * (cc + 1)], idf16[:])
                        mk(ins, "pe")
                    te.wait_ge(sem_v, v_pos["teL"])
                    for c6 in range(6):
                        if c6 >= 2:
                            te.wait_ge(sem_v, v_pos[f"qkve{c6 - 2}"])
                        slot = 512 * (c6 % 2)
                        ins = None
                        first = True
                        for kc in range(CI):
                            co = 3072 * kc + 512 * c6
                            for lh, rw in ((fmH, wqH), (fmH, wqL), (fmL, wqH)):
                                ins = te.matmul(psA[:, slot:slot + 512],
                                                lh[:, 128 * kc:128 * (kc + 1)],
                                                rw[:, co:co + 512],
                                                start=first, stop=(kc == CI - 1 and rw is wqH and lh is fmL))
                                first = False
                        mk(ins, "pe")
                    for g in range(4):
                        te.wait_ge(sem_v, v_pos[f"qkve{(0, 1, 2, 3)[g]}"])
                        if g >= 1:
                            te.wait_ge(sem_v, v_pos["qkte" + "abcd"[g - 1]])
                        qk = 0 if g < 2 else 1
                        ins = None
                        for i in range(4):
                            cc = 4 * (g % 2) + i
                            src_col = 1024 * qk + 128 * cc
                            ins = te.transpose(psT[:, 128 * i:128 * (i + 1)],
                                               qkvk[:, src_col:src_col + 128], id32[:])
                        mk(ins, "pe")
                    for h in range(HEADS):
                        r0 = 64 * (h % 2)
                        qc = h // 2
                        sl = 512 * (h % 2)
                        te.wait_ge(sem_v, v_pos["qkted"])
                        te.wait_ge(sem_v, v_pos["ms0"] if h < 2 else v_pos[f"mask{h - 2}"])
                        if not seq_start:
                            te.matmul(psS[:, sl:sl + 64],
                                      qfm[r0:r0 + 64, 128 * qc:128 * (qc + 1)],
                                      khalo[r0:r0 + 64, 64 * qc:64 * (qc + 1)],
                                      start=True, stop=True)
                        mk(te.matmul(psS[:, sl + 64:sl + 192],
                                     qfm[r0:r0 + 64, 128 * qc:128 * (qc + 1)],
                                     kfm[r0:r0 + 64, 128 * qc:128 * (qc + 1)],
                                     start=True, stop=True), "pe")
                        te.wait_ge(sem_sc, W["exp"][h])
                        if h >= 1:
                            te.wait_ge(sem_v, v_pos[f"pte{h - 1}"])
                        slot = 192 * (h % 2)
                        te.transpose(psT[0:64, 128:256],
                                     pbuf[:, slot:slot + 64], id32[:])
                        mk(te.transpose(psT[:, 0:128],
                                        pbuf[:, slot + 64:slot + 192], id32[:]), "pe")
                        te.wait_ge(sem_v, v_pos[f"pte{h}"])
                        if h >= 2:
                            te.wait_ge(sem_v, v_pos[f"oe{h - 2}"])
                        ob = 512 * (h % 2) + 64 * (h % 4)
                        if not seq_start:
                            te.matmul(psO[:, ob:ob + 64],
                                      ptb[0:64, 128:256],
                                      vhalo[0:64, 64 * h:64 * (h + 1)],
                                      start=True, stop=False)
                        mk(te.matmul(psO[:, ob:ob + 64],
                                     ptb[:, 0:128],
                                     qkvk[:, 2048 + 64 * h:2048 + 64 * (h + 1)],
                                     start=seq_start, stop=True), "pe")
                    te.wait_ge(sem_v, v_pos["olo"])
                    for part, buf in ((0, h1H), (1, h1L)):
                        if part == 1:
                            te.wait_ge(sem_v, v_pos["oteH"])
                        ins = None
                        for cc in range(CI):
                            ins = te.transpose(psH[:, 128 * cc:128 * (cc + 1)],
                                               buf[:, 128 * cc:128 * (cc + 1)], idf16[:])
                        mk(ins, "pe")
                    te.wait_ge(sem_v, v_pos["oteL"])
                    for nn in range(2):
                        slot = 512 * nn
                        ins = None
                        first = True
                        for kc in range(CI):
                            co = 1024 * kc + 512 * nn
                            for lh, rw in ((fmH, woH), (fmH, woL), (fmL, woH)):
                                ins = te.matmul(psA[:, slot:slot + 512],
                                                lh[:, 128 * kc:128 * (kc + 1)],
                                                rw[:, co:co + 512],
                                                start=first, stop=(kc == CI - 1 and rw is woH and lh is fmL))
                                first = False
                        mk(ins, "pe")

                @blk.sync
                def _(sy, t=t, v_pos=v_pos, pe_pos=pe_pos):
                    sy.wait_ge(sem_v, v_pos["qkve5"])
                    sy.wait_ge(sem_pe, pe_pos["pv15"])
                    hal.inc(sy.dma_start(out=vhalo[:], in_=qkvk[64:128, 2048:3072]))
                    sy.wait_ge(sem_v, v_pos["a_done"])
                    st.inc(sy.dma_start(out=a_s[NT * t:NT * (t + 1), :], in_=ab[:]))
                    sy.wait_ge(sem_v, v_pos["p_done"])
                    st.inc(sy.dma_start(out=p_s[NT * t:NT * (t + 1), :], in_=tmp1[:]))
                    sy.wait_ge(st.sem, st.n)
                    sy.wait_ge(hal.sem, hal.n)

                assert cnt["v"] == v_base + len(v_names), (cnt["v"], v_base, len(v_names))
                assert cnt["pe"] == pe_base + len(pe_names), (cnt["pe"], pe_base)

    with contextlib.ExitStack() as pb:
        sbb = lambda n, s, d=F32: pb.enter_context(nc.sbuf_tensor(n, s, d))
        w1j = sbb("w1j", [128, 2 * IN4])
        w2j = sbb("w2j", [128, 2 * IN4])
        h2q = sbb("h2q", [128, 8 * C])
        mslab = sbb("mslab", [128, 8 * C])
        gq = sbb("gq", [128, 2048])
        b1f = sbb("b1f", [128, IN4 // 128])
        bc3 = pb.enter_context(nc.sbuf_tensor("bc3b", [128, C], F32))
        psA = pb.enter_context(nc.psum_tensor("psAb", [128, 1536], F32))
        psT = pb.enter_context(nc.psum_tensor("psTb", [128, 1024], F32))
        psS = pb.enter_context(nc.psum_tensor("psSb32", [128, 512], F32))

        par = (lda, ldb)

        def issue_wj(sy, jg):
            p_ = jg % 2
            j_ = jg % 8
            for kc in range(CI):
                par[p_].inc(sy.dma_start(
                    out=w1j[:, 4096 * p_ + 512 * kc:4096 * p_ + 512 * (kc + 1)],
                    in_=w1_d[128 * kc:128 * (kc + 1), 512 * j_:512 * (j_ + 1)]))
            for s_ in range(4):
                par[p_].inc(sy.dma_start(
                    out=w2j[:, 4096 * p_ + 1024 * s_:4096 * p_ + 1024 * (s_ + 1)],
                    in_=w2_d[512 * j_ + 128 * s_:512 * j_ + 128 * (s_ + 1), :]))
            return par[p_].n

        wj_ready = {}

        with nc.Block() as blk:
            @blk.sync
            def _(sy):
                wj_ready[0] = issue_wj(sy, 0)
                ld.inc(sy.dma_start(out=bc0[:], in_=ln2g_d.broadcast_to([128, C])))
                ld.inc(sy.dma_start(out=bc1[:], in_=ln2b_d.broadcast_to([128, C])))
                ld.inc(sy.dma_start(out=bc3[:], in_=b2_d.broadcast_to([128, C])))
                ld.inc(sy.dma_start(
                    out=bc2[:], in_=modv_s[0:1, 2 * C:3 * C].broadcast_to([128, C])))
                ld.inc(sy.dma_start(out=b1f[:],
                                    in_=bass.AP(b1_h, 0, [[1, 128], [128, IN4 // 128]]),
                                    allow_slow_non_contiguous=True))
                sy.wait_ge(ld.sem, ld.n)

        for q in range(4):
            if q == 2:
                with nc.Block() as blk:
                    @blk.sync
                    def _(sy):
                        ld.inc(sy.dma_start(
                            out=bc2[:],
                            in_=modv_s[1:2, 2 * C:3 * C].broadcast_to([128, C])))
                        sy.wait_ge(ld.sem, ld.n)

            for tt in range(8):
                tq = 8 * q + tt
                with nc.Block() as blk:
                    W = {}
                    wrapd = {}

                    @blk.sync
                    def _(sy, tq=tq):
                        ld.inc(sy.dma_start(out=ab[:], in_=a_s[NT * tq:NT * (tq + 1), :]))
                        W["ld"] = ld.n

                    @blk.scalar
                    def _(sc):
                        sc = wrapd.setdefault("sc", EngWrap(sc))
                        sc.wait_ge(ld.sem, W["ld"])
                        W["sq"] = mk(sc.activation(out=tmp1[:], in_=ab[:],
                                                   func=AF.Square, accum_out=s2[:]), "sc")

                    @blk.vector
                    def _(v):
                        v = wrapd.setdefault("v", EngWrap(v))
                        v.wait_ge(ld.sem, W["ld"])
                        v.reduce_sum(s1[:], ab[:], axis=AX)
                        v.wait_ge(sem_sc, W["sq"])
                        v.tensor_scalar(e2v[:], s2[:], INV_C, None, op0=ALU.mult)
                        v.tensor_scalar(mvec[:], s1[:], INV_C, None, op0=ALU.mult)
                        v.tensor_mul(varv[:], mvec[:], mvec[:])
                        W["var"] = mk(v.tensor_sub(varv[:], e2v[:], varv[:]), "v")

                    @blk.scalar
                    def _(sc):
                        sc = wrapd["sc"]
                        sc.wait_ge(sem_v, W["var"])
                        W["sd"] = mk(sc.activation(out=sdv[:], in_=varv[:],
                                                   func=AF.Sqrt, bias=epsb[:]), "sc")

                    @blk.vector
                    def _(v):
                        v = wrapd["v"]
                        v.wait_ge(sem_sc, W["sd"])
                        v.reciprocal(rstd[:], sdv[:])
                        v.tensor_mul(mrv[:], mvec[:], rstd[:])
                        v.tensor_scalar(tmp1[:], ab[:], rstd[:], mrv[:],
                                        op0=ALU.mult, op1=ALU.subtract)
                        v.tensor_mul(hb[:], tmp1[:], bc0[:])
                        W["h2"] = mk(v.tensor_add(hb[:], hb[:], bc1[:]), "v")

                    @blk.tensor
                    def _(te):
                        te.wait_ge(sem_v, W["h2"])
                        W["T"] = []
                        for g_ in range(2):
                            ins = None
                            for i in range(4):
                                cc = 4 * g_ + i
                                ins = te.transpose(
                                    psT[:, 512 * g_ + 128 * i:512 * g_ + 128 * (i + 1)],
                                    hb[:, 128 * cc:128 * (cc + 1)], id32[:])
                            W["T"].append(mk(ins, "pe"))

                    @blk.vector
                    def _(v, tt=tt):
                        v = wrapd["v"]
                        h2r = h2q[:].rearrange("p (c t) -> p c t", c=CI)
                        psr = psT[:].rearrange("p (c t) -> p c t", c=CI)
                        for g_ in range(2):
                            v.wait_ge(sem_pe, W["T"][g_])
                            mk(v.tensor_copy(
                                h2r[:, 4 * g_:4 * (g_ + 1), 128 * tt:128 * (tt + 1)],
                                psr[:, 4 * g_:4 * (g_ + 1), :]), "v")

            for j in range(8):
                jg = 8 * q + j
                p_ = jg % 2
                pe_b = cnt["pe"]
                sc_b = cnt["sc"]
                v_b = cnt["v"]
                pe_g = lambda st_, m, pe_b=pe_b: pe_b + st_ * 12 + m + 1
                pe_w2 = lambda st_, t4, hf_, pe_b=pe_b: pe_b + st_ * 12 + 4 + t4 * 2 + hf_ + 1
                sc_gelu = lambda st_, m, sc_b=sc_b: sc_b + st_ * 4 + m + 1
                v_ev = lambda k, v_b=v_b: v_b + k + 1

                with nc.Block() as blk:
                    @blk.sync
                    def _(sy, jg=jg):
                        if jg + 1 < 32:
                            wj_ready[jg + 1] = issue_wj(sy, jg + 1)

                    @blk.tensor
                    def _(te, j=j, jg=jg, p_=p_, pe_g=pe_g, pe_w2=pe_w2,
                          sc_gelu=sc_gelu, v_ev=v_ev):
                        te.wait_ge(par[p_].sem, wj_ready[jg])
                        for st_ in range(2):
                            for m in range(4):
                                if st_ == 1:
                                    te.wait_ge(sem_sc, sc_gelu(0, m))
                                dst = (psA[:, 512 * m:512 * (m + 1)] if m < 3
                                       else psS[:, 0:512])
                                ins = None
                                for kc in range(CI):
                                    ins = te.matmul(
                                        dst,
                                        w1j[:, 4096 * p_ + 512 * kc + 128 * m:
                                            4096 * p_ + 512 * kc + 128 * (m + 1)],
                                        h2q[:, 1024 * kc + 512 * st_:
                                            1024 * kc + 512 * (st_ + 1)],
                                        start=(kc == 0), stop=(kc == CI - 1))
                                mk(ins, "pe")
                            te.wait_ge(sem_sc, sc_gelu(st_, 3))
                            for t4 in range(4):
                                for hf_ in range(2):
                                    k = st_ * 8 + t4 * 2 + hf_
                                    if k >= 2:
                                        te.wait_ge(sem_v, v_ev(k - 2))
                                    ins = None
                                    for m in range(4):
                                        ins = te.matmul(
                                            psT[:, 512 * hf_:512 * (hf_ + 1)],
                                            gq[:, 512 * m + 128 * t4:
                                               512 * m + 128 * (t4 + 1)],
                                            w2j[:, 4096 * p_ + 1024 * m + 512 * hf_:
                                                4096 * p_ + 1024 * m + 512 * hf_ + 512],
                                            start=(m == 0), stop=(m == 3))
                                    mk(ins, "pe")

                    @blk.scalar
                    def _(sc, j=j, pe_g=pe_g, pe_w2=pe_w2):
                        sc = EngWrap(sc)
                        for st_ in range(2):
                            if st_ == 1:
                                sc.wait_ge(sem_pe, pe_w2(0, 3, 1))
                            for m in range(4):
                                sc.wait_ge(sem_pe, pe_g(st_, m))
                                src_ = (psA[:, 512 * m:512 * (m + 1)] if m < 3
                                        else psS[:, 0:512])
                                mk(sc.activation(out=gq[:, 512 * m:512 * (m + 1)],
                                                 in_=src_, func=AF.Gelu,
                                                 bias=b1f[:, 4 * j + m:4 * j + m + 1]),
                                   "sc")

                    @blk.vector
                    def _(v, j=j, pe_w2=pe_w2):
                        v = EngWrap(v)
                        for st_ in range(2):
                            for t4 in range(4):
                                for hf_ in range(2):
                                    v.wait_ge(sem_pe, pe_w2(st_, t4, hf_))
                                    tt_ = 4 * st_ + t4
                                    dst = mslab[:, 1024 * tt_ + 512 * hf_:
                                                1024 * tt_ + 512 * hf_ + 512]
                                    if j == 0:
                                        ins = v.scalar_tensor_tensor(
                                            dst, psT[:, 512 * hf_:512 * (hf_ + 1)], 0.0,
                                            bc3[:, 512 * hf_:512 * (hf_ + 1)],
                                            op0=ALU.add, op1=ALU.add)
                                    else:
                                        ins = v.tensor_add(
                                            dst, dst, psT[:, 512 * hf_:512 * (hf_ + 1)])
                                    mk(ins, "v")

            for tt in range(8):
                tq = 8 * q + tt
                with nc.Block() as blk:
                    W = {}

                    @blk.sync
                    def _(sy, tq=tq):
                        ld.inc(sy.dma_start(out=xb[:], in_=p_s[NT * tq:NT * (tq + 1), :]))
                        W["ld"] = ld.n

                    @blk.vector
                    def _(v, tt=tt):
                        v = EngWrap(v)
                        v.wait_ge(ld.sem, W["ld"])
                        v.tensor_mul(tmp1[:], mslab[:, 1024 * tt:1024 * (tt + 1)], bc2[:])
                        W["x1"] = mk(v.tensor_add(tmp1[:], tmp1[:], xb[:]), "v")

                    @blk.sync
                    def _(sy, tq=tq):
                        sy.wait_ge(sem_v, W["x1"])
                        st.inc(sy.dma_start(out=x1_s[NT * tq:NT * (tq + 1), :], in_=tmp1[:]))
                        sy.wait_ge(st.sem, st.n)

    par = (lda, ldb)

    def w_mean_pass(w_dram, nch, width, col, stg, psR):
        v_b = cnt["v"]
        p0_ = [par[0].n, par[1].n]
        with nc.Block() as blk:
            @blk.sync
            def _(sy, nch=nch, width=width):
                for ci_ in range(nch):
                    pp = ci_ % 2
                    if ci_ >= 2:
                        sy.wait_ge(sem_v, v_b + ci_ - 1)
                    par[pp].inc(sy.dma_start(
                        out=stg[:, width * pp:width * (pp + 1)],
                        in_=w_dram[128 * ci_:128 * (ci_ + 1), :]))

            @blk.vector
            def _(v, nch=nch, width=width):
                v = EngWrap(v)
                base = list(p0_)
                for ci_ in range(nch):
                    pp = ci_ % 2
                    base[pp] += 16
                    v.wait_ge(par[pp].sem, base[pp])
                    v.tensor_reduce(s1[:], stg[:, width * pp:width * (pp + 1)],
                                    axis=AX, op=ALU.add, apply_absolute_value=True)
                    if ci_ == 0:
                        ins = v.tensor_copy(sc_4[:], s1[:])
                    else:
                        ins = v.tensor_add(sc_4[:], sc_4[:], s1[:])
                    mk(ins, "v")

        with nc.Block() as blk:
            @blk.tensor
            def _(te):
                mk(te.transpose(psR[0:1, 0:128], sc_4[:], id32[:]), "pe")

            @blk.vector
            def _(v, nch=nch, width=width):
                v = EngWrap(v)
                v.wait_ge(sem_pe, cnt["pe"])
                v.reduce_sum(s1[0:1, 0:1], psR[0:1, 0:128], axis=AX)
                v.tensor_scalar(s1[0:1, 0:1], s1[0:1, 0:1],
                                1.0 / (nch * 128 * width), 1e-5,
                                op0=ALU.mult, op1=ALU.max)
                v.reciprocal(s2[0:1, 0:1], s1[0:1, 0:1])
                mk(v.tensor_copy(mvec[0:1, 0:1], s1[0:1, 0:1]), "v")

            @blk.sync
            def _(sy, col=col):
                sy.wait_ge(sem_v, cnt["v"])
                st.inc(sy.dma_start(out=sw_s[0:1, col:col + 1], in_=mvec[0:1, 0:1]))
                st.inc(sy.dma_start(out=sw_s[0:1, col + 3:col + 4], in_=s2[0:1, 0:1]))
                sy.wait_ge(st.sem, st.n)
                ld.inc(sy.dma_start(out=swb[:],
                                    in_=sw_s[0:1, 0:8].broadcast_to([128, 8])))
                sy.wait_ge(ld.sem, ld.n)

    def w_tern_pass(w_dram, nch, width, scol, dst_of_piece, stg, tA, tB, tern, psB):
        ng = width // 1024
        v_per = 1 + ng
        v_b = cnt["v"]
        pe_b = cnt["pe"]
        p0_ = [par[0].n, par[1].n]

        def v_tern(ci_):
            return v_b + ci_ * v_per + 1

        def v_evac(ci_, g_):
            return v_b + ci_ * v_per + 1 + g_ + 1

        def pe_T(ci_, g_):
            return pe_b + ci_ * ng + g_ + 1

        with nc.Block() as blk:
            @blk.sync
            def _(sy, nch=nch, width=width):
                for ci_ in range(nch):
                    pp = ci_ % 2
                    if ci_ >= 2:
                        sy.wait_ge(sem_v, v_tern(ci_ - 2))
                    par[pp].inc(sy.dma_start(
                        out=stg[:, width * pp:width * (pp + 1)],
                        in_=w_dram[128 * ci_:128 * (ci_ + 1), :]))

            @blk.vector
            def _(v, nch=nch, width=width, scol=scol):
                v = EngWrap(v)
                base = list(p0_)
                for ci_ in range(nch):
                    pp = ci_ % 2
                    base[pp] += 16
                    v.wait_ge(par[pp].sem, base[pp])
                    sl = stg[:, width * pp:width * (pp + 1)]
                    v.tensor_scalar(tA[:, 0:width], sl, swb[:, scol + 3:scol + 4],
                                    0.5, op0=ALU.mult, op1=ALU.is_gt)
                    v.tensor_scalar(tB[:, 0:width], sl, swb[:, scol + 3:scol + 4],
                                    -0.5, op0=ALU.mult, op1=ALU.is_lt)
                    if ci_ >= 1:
                        v.wait_ge(sem_pe, pe_T(ci_ - 1, ng - 1))
                    mk(v.tensor_sub(tern[:, 0:width], tA[:, 0:width],
                                    tB[:, 0:width]), "v")
                    for g_ in range(ng):
                        v.wait_ge(sem_pe, pe_T(ci_, g_))
                        src_ = psB[:, 0:1024].rearrange("p (i t) -> p i t", i=8)
                        mk(v.tensor_copy(dst_of_piece(ci_, g_), src_), "v")

            @blk.tensor
            def _(te, nch=nch, width=width):
                for ci_ in range(nch):
                    te.wait_ge(sem_v, v_tern(ci_))
                    for g_ in range(ng):
                        if g_ >= 1:
                            te.wait_ge(sem_v, v_evac(ci_, g_ - 1))
                        ins = None
                        for i in range(8):
                            pi = 8 * g_ + i
                            ins = te.transpose(
                                psB[:, 128 * i:128 * (i + 1)],
                                tern[:, 128 * pi:128 * (pi + 1)], idbf[:])
                        mk(ins, "pe")

    def ladder_v(v, src, width, s_ap, r_out):
        v.tensor_scalar(okm[:, 0:width], src, s_ap, 0.5, op0=ALU.mult, op1=ALU.add)
        bmk = mk(v.tensor_scalar(tmp1[:, 0:width], okm[:, 0:width], 128.0, None,
                                 op0=ALU.add), "v")
        for k in (128, 64, 32, 16, 8, 4, 2, 1):
            v.tensor_scalar(fmbuf[:, 0:width], tmp1[:, 0:width], float(k),
                            -float(k), op0=ALU.is_ge, op1=ALU.mult)
            v.tensor_add(tmp1[:, 0:width], tmp1[:, 0:width], fmbuf[:, 0:width])
        rmk = mk(v.tensor_sub(r_out[:, 0:width], okm[:, 0:width],
                              tmp1[:, 0:width]), "v")
        return bmk, rmk

    def ladder_g(g, width, r_out, bmk):
        g.wait_ge(sem_v, bmk)
        return mk(g.memset(gdum[0:1, 0:1], 0.0), "gp")

    with contextlib.ExitStack() as pc:
        sbc = lambda n, s, d=F32: pc.enter_context(nc.sbuf_tensor(n, s, d))
        wgf = sbc("wgf", [128, CI * IN4], BF16)
        wuf = sbc("wuf", [128, CI * IN4], BF16)
        gu = sbc("gu", [128, IN4])
        fmbuf = sbc("fmbuf_c", [128, C])
        rhf_kmb = sbc("rhf_kmb", [128, C], BF16)
        rhf_fmb = sbc("rhf_fmb", [128, C], BF16)
        rgu_kmb = sbc("rgu_kmb", [128, IN4], BF16)
        tern_c = rgu_kmb[:, 0:C]
        rhf_km = rhf_kmb[:]
        rhf_fm = rhf_fmb[:]
        rgu_km = rgu_kmb[:]
        psT = pc.enter_context(nc.psum_tensor("psTc", [128, 1024], F32))
        psO = pc.enter_context(nc.psum_tensor("psOc", [128, 1024], F32))
        psB = pc.enter_context(nc.psum_tensor("psBc", [128, 1024], BF16))

        def wguf_dst(ci_, g_, buf):
            r = buf[:].rearrange("p (h i) -> p h i", h=CI)
            return r[:, :, 128 * ci_:128 * (ci_ + 1)]

        w_mean_pass(wg_d, 32, 1024, 0, gu, psT)
        w_tern_pass(wg_d, 32, 1024, 0, lambda c, g: wguf_dst(c, g, wgf),
                    gu, ab, tmp1, tern_c, psB)
        w_mean_pass(wu_d, 32, 1024, 1, gu, psT)
        w_tern_pass(wu_d, 32, 1024, 1, lambda c, g: wguf_dst(c, g, wuf),
                    gu, ab, tmp1, tern_c, psB)

        with nc.Block() as blk:
            @blk.sync
            def _(sy):
                ld.inc(sy.dma_start(
                    out=bc0[:], in_=modv_s[0:1, 3 * C:4 * C].broadcast_to([128, C])))
                ld.inc(sy.dma_start(
                    out=bc1[:], in_=modv_s[0:1, 4 * C:5 * C].broadcast_to([128, C])))
                sy.wait_ge(ld.sem, ld.n)

        for t in range(TILES):
            if t == TPB:
                with nc.Block() as blk:
                    @blk.sync
                    def _(sy):
                        ld.inc(sy.dma_start(
                            out=bc0[:],
                            in_=modv_s[1:2, 3 * C:4 * C].broadcast_to([128, C])))
                        ld.inc(sy.dma_start(
                            out=bc1[:],
                            in_=modv_s[1:2, 4 * C:5 * C].broadcast_to([128, C])))
                        sy.wait_ge(ld.sem, ld.n)

            with nc.Block() as blk:
                W = {}
                wrapd = {}
                gp_b = cnt["gp"]
                gp_hf = gp_b + 1
                gp_rq = lambda qr, gp_b=gp_b: gp_b + 2 + qr

                @blk.sync
                def _(sy, t=t):
                    ld.inc(sy.dma_start(out=xb[:], in_=x1_s[NT * t:NT * (t + 1), :]))
                    W["ld"] = ld.n

                @blk.scalar
                def _(sc):
                    sc = wrapd.setdefault("sc", EngWrap(sc))
                    sc.wait_ge(ld.sem, W["ld"])
                    W["sq"] = mk(sc.activation(out=fmbuf[:], in_=xb[:], func=AF.Square,
                                               accum_out=s2[:]), "sc")

                @blk.vector
                def _(v):
                    v = wrapd.setdefault("v", EngWrap(v))
                    v.wait_ge(ld.sem, W["ld"])
                    v.reduce_sum(s1[:], xb[:], axis=AX)
                    v.wait_ge(sem_sc, W["sq"])
                    v.tensor_scalar(e2v[:], s2[:], INV_C, None, op0=ALU.mult)
                    v.tensor_scalar(mvec[:], s1[:], INV_C, None, op0=ALU.mult)
                    v.tensor_mul(varv[:], mvec[:], mvec[:])
                    W["var"] = mk(v.tensor_sub(varv[:], e2v[:], varv[:]), "v")

                @blk.scalar
                def _(sc):
                    sc = wrapd["sc"]
                    sc.wait_ge(sem_v, W["var"])
                    W["sd"] = mk(sc.activation(out=sdv[:], in_=varv[:], func=AF.Sqrt,
                                               bias=epsb[:]), "sc")

                @blk.vector
                def _(v):
                    v = wrapd["v"]
                    v.wait_ge(sem_sc, W["sd"])
                    v.reciprocal(rstd[:], sdv[:])
                    v.tensor_mul(mrv[:], mvec[:], rstd[:])
                    v.tensor_scalar(tmp1[:], xb[:], rstd[:], mrv[:],
                                    op0=ALU.mult, op1=ALU.subtract)
                    v.tensor_mul(hb[:], tmp1[:], bc0[:])
                    v.tensor_add(hb[:], hb[:], bc1[:])
                    v.tensor_reduce(s1[:], hb[:], axis=AX, op=ALU.max,
                                    apply_absolute_value=True)
                    v.tensor_scalar(sc_0[:], s1[:], 1e-5, None, op0=ALU.max)
                    v.reciprocal(sc_1[:], sc_0[:])
                    v.tensor_scalar(sc_1[:], sc_1[:], 127.0, None, op0=ALU.mult)
                    v.tensor_mul(sc_2[:], sc_0[:], swb[:, 0:1])
                    v.tensor_scalar(sc_2[:], sc_2[:], 1.0 / 127.0, None, op0=ALU.mult)
                    v.tensor_mul(sc_3[:], sc_0[:], swb[:, 1:2])
                    v.tensor_scalar(sc_3[:], sc_3[:], 1.0 / 127.0, None, op0=ALU.mult)
                    W["lad_hf"] = ladder_v(v, hb[:], C, sc_1[:], rhf_km)

                @blk.gpsimd
                def _(g):
                    g = wrapd.setdefault("g", EngWrap(g))
                    W["lad_hf_g"] = ladder_g(g, C, rhf_km, W["lad_hf"][0])
                    assert W["lad_hf_g"] == gp_hf, (W["lad_hf_g"], gp_hf)

                @blk.tensor
                def _(te, t=t):
                    te.wait_ge(sem_v, W["lad_hf"][1])
                    te.wait_ge(sem_gp, W["lad_hf_g"])
                    ins = None
                    for i in range(8):
                        ins = te.transpose(psB[:, 128 * i:128 * (i + 1)],
                                           rhf_km[:, 128 * i:128 * (i + 1)], idbf[:])
                    W["rhfT"] = mk(ins, "pe")

                @blk.vector
                def _(v):
                    v = wrapd["v"]
                    v.wait_ge(sem_pe, W["rhfT"])
                    W["rhf_fm"] = mk(v.tensor_copy(rhf_fm[:], psB[:, 0:1024]), "v")

                sc_b = cnt["sc"]
                v_b2 = cnt["v"]
                sc_silu = lambda qr, sc_b=sc_b: sc_b + qr + 1
                v_guq = lambda qr, v_b2=v_b2: v_b2 + 2 * qr + 1

                @blk.tensor
                def _(te, t=t, sc_silu=sc_silu, v_guq=v_guq):
                    W["gmm"] = []
                    W["umm"] = []
                    te.wait_ge(sem_v, W["rhf_fm"])
                    for qr in range(4):
                        if qr >= 1:
                            te.wait_ge(sem_sc, sc_silu(qr - 1))
                        ins = None
                        for nn in range(2):
                            for kc in range(CI):
                                ins = te.matmul(
                                    psT[:, 512 * nn:512 * (nn + 1)],
                                    rhf_fm[:, 128 * kc:128 * (kc + 1)],
                                    wgf[:, 4096 * kc + 1024 * qr + 512 * nn:
                                        4096 * kc + 1024 * qr + 512 * (nn + 1)],
                                    start=(kc == 0), stop=(kc == CI - 1))
                        W["gmm"].append(mk(ins, "pe"))
                        if qr >= 1:
                            te.wait_ge(sem_v, v_guq(qr - 1))
                        ins = None
                        for nn in range(2):
                            for kc in range(CI):
                                ins = te.matmul(
                                    psO[:, 512 * nn:512 * (nn + 1)],
                                    rhf_fm[:, 128 * kc:128 * (kc + 1)],
                                    wuf[:, 4096 * kc + 1024 * qr + 512 * nn:
                                        4096 * kc + 1024 * qr + 512 * (nn + 1)],
                                    start=(kc == 0), stop=(kc == CI - 1))
                        W["umm"].append(mk(ins, "pe"))

                @blk.scalar
                def _(sc, sc_silu=sc_silu, v_guq=v_guq):
                    sc = wrapd["sc"]
                    for qr in range(4):
                        sc.wait_ge(sem_pe, W["gmm"][qr])
                        if qr >= 1:
                            sc.wait_ge(sem_v, v_guq(qr - 1))
                        assert mk(sc.activation(out=ab[:], in_=psT[:], func=AF.Silu,
                                                scale=sc_2[:]), "sc") == sc_silu(qr)

                @blk.vector
                def _(v):
                    v = wrapd["v"]
                    for qr in range(4):
                        v.wait_ge(sem_sc, sc_silu(qr))
                        v.wait_ge(sem_pe, W["umm"][qr])
                        assert mk(v.tensor_mul(gu[:, 1024 * qr:1024 * (qr + 1)],
                                               ab[:], psO[:]), "v") == v_guq(qr)
                        v.tensor_reduce(s1[:], gu[:, 1024 * qr:1024 * (qr + 1)],
                                        axis=AX, op=ALU.max,
                                        apply_absolute_value=True)
                        if qr == 0:
                            ins = v.tensor_copy(sc_4[:], s1[:])
                        else:
                            ins = v.tensor_max(sc_4[:], sc_4[:], s1[:])
                        mk(ins, "v")
                    v.tensor_mul(sc_4[:], sc_4[:], sc_3[:])
                    v.tensor_scalar(sc_4[:], sc_4[:], 1e-5, None, op0=ALU.max)
                    v.reciprocal(s2[:], sc_4[:])
                    v.tensor_mul(sc_1[:], sc_3[:], s2[:])
                    v.tensor_scalar(sc_1[:], sc_1[:], 127.0, None, op0=ALU.mult)
                    W["osc"] = mk(v.tensor_scalar(sc_0[:], sc_4[:], 1.0 / 127.0, None,
                                                  op0=ALU.mult), "v")
                    W["rq_v"] = []
                    W["rq_b"] = []
                    for qr in range(4):
                        v.wait_ge(sem_gp, gp_hf if qr == 0 else gp_rq(qr - 1))
                        bmk, rmk = ladder_v(v, gu[:, 1024 * qr:1024 * (qr + 1)], C,
                                            sc_1[:],
                                            rgu_km[:, 1024 * qr:1024 * (qr + 1)])
                        W["rq_v"].append(rmk)
                        W["rq_b"].append(bmk)

                @blk.gpsimd
                def _(g):
                    g = wrapd["g"]
                    W["rq_g"] = []
                    for qr in range(4):
                        r_ = ladder_g(g, C, rgu_km[:, 1024 * qr:1024 * (qr + 1)],
                                      W["rq_b"][qr])
                        assert r_ == gp_rq(qr), (r_, gp_rq(qr))
                        W["rq_g"].append(r_)

                @blk.sync
                def _(sy, t=t):
                    sy.wait_ge(sem_v, W["osc"])
                    st.inc(sy.dma_start(out=osc_s[NT * t:NT * (t + 1), :], in_=sc_0[:]))
                    sy.wait_ge(sem_v, W["rq_v"][3])
                    sy.wait_ge(sem_gp, W["rq_g"][3])
                    st.inc(sy.dma_start(out=r_s[NT * t:NT * (t + 1), :], in_=rgu_km))
                    sy.wait_ge(st.sem, st.n)

    with contextlib.ExitStack() as pd_:
        sbd = lambda n, s, d=F32: pd_.enter_context(nc.sbuf_tensor(n, s, d))
        wdf = sbd("wdf", [128, 32 * C], BF16)
        rkm = sbd("rkm", [128, IN4], BF16)
        stgd = sbd("stgd", [128, 2 * IN4])
        tAd = sbd("tAd", [128, IN4])
        tBd = sbd("tBd", [128, IN4])
        rfmb = sbd("rfmb", [128, IN4], BF16)
        rfm = rfmb[:]
        tern_d = rkm[:]
        psT = pd_.enter_context(nc.psum_tensor("psTd", [128, 1024], F32))
        psB = pd_.enter_context(nc.psum_tensor("psBd", [128, 1024], BF16))

        def wd_dst(ci_, g_):
            r = wdf[:].rearrange("p (i o) -> p i o", i=32)
            return r[:, 8 * g_:8 * (g_ + 1), 128 * ci_:128 * (ci_ + 1)]

        w_mean_pass(wd_d, 8, 4096, 2, stgd, psT)
        w_tern_pass(wd_d, 8, 4096, 2, wd_dst, stgd, tAd, tBd, tern_d, psB)

        with nc.Block() as blk:
            @blk.sync
            def _(sy):
                ld.inc(sy.dma_start(
                    out=bc2[:], in_=modv_s[0:1, 5 * C:6 * C].broadcast_to([128, C])))
                sy.wait_ge(ld.sem, ld.n)

        for t in range(TILES):
            if t == TPB:
                with nc.Block() as blk:
                    @blk.sync
                    def _(sy):
                        ld.inc(sy.dma_start(
                            out=bc2[:],
                            in_=modv_s[1:2, 5 * C:6 * C].broadcast_to([128, C])))
                        sy.wait_ge(ld.sem, ld.n)

            with nc.Block() as blk:
                W = {}
                wrapd = {}
                v_b = cnt["v"]
                v_evc = lambda g_, v_b=v_b: v_b + g_ + 1

                @blk.sync
                def _(sy, t=t):
                    ld.inc(sy.dma_start(out=rkm[:], in_=r_s[NT * t:NT * (t + 1), :]))
                    ld.inc(sy.dma_start(out=xb[:], in_=x1_s[NT * t:NT * (t + 1), :]))
                    ld.inc(sy.dma_start(out=sc_0[:], in_=osc_s[NT * t:NT * (t + 1), :]))
                    W["ld"] = ld.n

                @blk.tensor
                def _(te, t=t, v_evc=v_evc):
                    te.wait_ge(ld.sem, W["ld"])
                    W["T"] = []
                    for g_ in range(4):
                        if g_ >= 1:
                            te.wait_ge(sem_v, v_evc(g_ - 1))
                        ins = None
                        for i in range(8):
                            pi = 8 * g_ + i
                            ins = te.transpose(
                                psB[:, 128 * i:128 * (i + 1)],
                                rkm[:, 128 * pi:128 * (pi + 1)], idbf[:])
                        W["T"].append(mk(ins, "pe"))

                @blk.vector
                def _(v, t=t):
                    v = wrapd.setdefault("v", EngWrap(v))
                    v.wait_ge(ld.sem, W["ld"])
                    v.tensor_mul(sc_0[:], sc_0[:], swb[:, 2:3])
                    for g_ in range(4):
                        v.wait_ge(sem_pe, W["T"][g_])
                        src_ = psB[:, 0:1024].rearrange("p (i t) -> p i t", i=8)
                        dst = rfm.rearrange("p (i t) -> p i t", i=32)[
                            :, 8 * g_:8 * (g_ + 1), :]
                        assert mk(v.tensor_copy(dst, src_), "v") == v_evc(g_)

                @blk.tensor
                def _(te, t=t, v_evc=v_evc):
                    te.wait_ge(sem_v, v_evc(3))
                    W["dn"] = []
                    for nn in range(2):
                        ins = None
                        for ic in range(32):
                            ins = te.matmul(
                                psT[:, 512 * nn:512 * (nn + 1)],
                                rfm[:, 128 * ic:128 * (ic + 1)],
                                wdf[:, 1024 * ic + 512 * nn:1024 * ic + 512 * (nn + 1)],
                                start=(ic == 0), stop=(ic == 31))
                        W["dn"].append(mk(ins, "pe"))

                @blk.vector
                def _(v, t=t):
                    v = wrapd["v"]
                    for nn in range(2):
                        v.wait_ge(sem_pe, W["dn"][nn])
                        v.tensor_scalar(tmp1[:, 512 * nn:512 * (nn + 1)],
                                        psT[:, 512 * nn:512 * (nn + 1)],
                                        sc_0[:], None, op0=ALU.mult)
                    v.tensor_mul(tmp1[:], tmp1[:], bc2[:])
                    W["x2"] = mk(v.tensor_add(ab[:], tmp1[:], xb[:]), "v")

                @blk.sync
                def _(sy, t=t):
                    sy.wait_ge(sem_v, W["x2"])
                    st.inc(sy.dma_start(out=out_d[NT * t:NT * (t + 1), :], in_=ab[:]))
                    sy.wait_ge(st.sem, st.n)

    es.close()
    return nc


_NC_CACHE = [None]


def _shard_inputs(inputs):
    f32 = lambda a: np.ascontiguousarray(np.asarray(a, dtype=np.float32))
    x = f32(inputs["L0_attn_x"]).reshape(B, T, C)
    cv = f32(inputs["L0_c"])
    common = {
        "wqkv": f32(inputs["L0_attn_w_qkv"]),
        "bqkv": f32(inputs["L0_attn_b_qkv"]).reshape(1, 3 * C),
        "wout": f32(inputs["L0_attn_w_out"]),
        "bout": f32(inputs["L0_attn_b_out"]).reshape(1, C),
        "ln1g": f32(inputs["L0_attn_ln1_g"]).reshape(1, C),
        "ln1b": f32(inputs["L0_attn_ln1_b"]).reshape(1, C),
        "ln2g": f32(inputs["L0_attn_ln2_g"]).reshape(1, C),
        "ln2b": f32(inputs["L0_attn_ln2_b"]).reshape(1, C),
        "w1": f32(inputs["L0_attn_w1"]),
        "b1": f32(inputs["L0_attn_b1"]).reshape(1, IN4),
        "w2": f32(inputs["L0_attn_w2"]),
        "b2": f32(inputs["L0_attn_b2"]).reshape(1, C),
        "wg": f32(inputs["L0_ffn_w_gate"]),
        "wu": f32(inputs["L0_ffn_w_up"]),
        "wd": f32(inputs["L0_ffn_w_down"]),
        "wmod": f32(inputs["L0_wmod"]),
    }
    maps = []
    for i in range(NCORES):
        m = dict(common)
        m["x"] = np.ascontiguousarray(x[BPC * i:BPC * (i + 1)]).reshape(TOK, C)
        m["cvec"] = np.ascontiguousarray(cv[BPC * i:BPC * (i + 1)])
        maps.append(m)
    return maps


def kernel(**inputs):
    if _NC_CACHE[0] is None:
        _NC_CACHE[0] = build_nc(debug=0)
    nc = _NC_CACHE[0]
    in_maps = _shard_inputs(inputs)
    res = run_bass_kernel_spmd(nc, in_maps, core_ids=list(range(NCORES)))
    out = np.concatenate(
        [np.asarray(r["out"], dtype=np.float32).reshape(BPC, T, C)
         for r in res.results], axis=0)
    return out
